# version 15 (final)
# speedup vs baseline: 1.4369x; 1.0095x over previous
import numpy as np
import concourse.bass as bass
import concourse.mybir as mybir
from concourse.bass_utils import run_bass_kernel_spmd

F32 = mybir.dt.float32
BF16 = mybir.dt.bfloat16
I32 = mybir.dt.int32
AF = mybir.ActivationFunctionType
ALU = mybir.AluOpType
AX = mybir.AxisListType

S = 4096
D = 1024
DC = D // 128
TT = 512
NTT = S // TT
EPS = 1e-5

ENGINES = ("sync", "scalar", "vector", "tensor", "gpsimd")
QUEUE_ENG = {"qs": "sync", "qg": "gpsimd", "qa": "scalar"}
N_DMA_SEM_POOL = 12


def _free_span(ap):
    dims = ap.ap
    part_stride = dims[0][0]
    f0 = ap.offset % part_stride if part_stride > 0 else ap.offset
    span = 1
    for step, count in dims[1:]:
        span += abs(step) * (count - 1)
    return (f0, f0 + span)


def _part_span(ap):
    dims = ap.ap
    part_stride = dims[0][0]
    p0 = ap.offset // part_stride if part_stride > 0 else 0
    return (p0, p0 + dims[0][1])


def _flat_span(ap):
    span = 1
    for step, count in ap.ap:
        span += abs(step) * (count - 1)
    return (ap.offset, ap.offset + span)


def _region(ap):
    t = ap.tensor
    name = t.name
    kind = type(t).__name__
    if kind == "SBTensorHandle":
        p = _part_span(ap); f = _free_span(ap)
        return (name, p[0], p[1], f[0], f[1])
    if kind == "PSumTensorHandle":
        f = _free_span(ap)
        esz = mybir.dt.size(ap.dtype)
        bank = 2048 // esz
        return (name, 0, 128, (f[0] // bank) * bank,
                ((f[1] + bank - 1) // bank) * bank)
    flat = _flat_span(ap)
    return (name, 0, 1, flat[0], flat[1])


def _overlap(r1, r2):
    return (r1[0] == r2[0] and r1[1] < r2[2] and r2[1] < r1[2]
            and r1[3] < r2[4] and r2[3] < r1[4])


class Op:
    __slots__ = ("idx", "eng", "emit", "deps", "inc_needed", "sem", "val", "is_dma", "dbg")

    def __init__(self, idx, eng, emit, is_dma, dbg):
        self.idx = idx
        self.eng = eng
        self.emit = emit
        self.deps = []
        self.inc_needed = is_dma
        self.sem = None
        self.val = None
        self.is_dma = is_dma
        self.dbg = dbg


class P:

    def __init__(self):
        self.nc = bass.Bass("TRN2", target_bir_lowering=False)
        self.ops = []
        self.writers = {}
        self.readers = {}
        self.semh = {}
        self.dma_cnt = {}
        self.phase_idx = {q: 0 for q in QUEUE_ENG}
        self.dbg_tag = ""
        self.last_op = {}

    def barrier(self):
        self._dma_sem_map = {}
        sem_deps = [("sem", s, c * 16) for s, c in self.dma_cnt.items()]
        lasts = dict(self.last_op)
        for eng in ENGINES:
            deps = [("op", o) for e2, o in lasts.items() if e2 != eng and not o.is_dma]
            o = Op(len(self.ops), eng, lambda e: e.nop(), False, "barrier")
            o.deps = deps + list(sem_deps)
            for kind, *rest in deps:
                rest[0].inc_needed = True
            self.ops.append(o)
            self.last_op[eng] = o

    def sem(self, name):
        if name not in self.semh:
            self.semh[name] = self.nc.alloc_semaphore(name=name)
        return self.semh[name]

    def _record(self, op, r, w):
        regs_r = [_region(a) for a in r]
        regs_w = [_region(a) for a in w]
        deps = {}
        for reg in regs_r:
            for oreg, oop in self.writers.get(reg[0], ()):
                if _overlap(reg, oreg):
                    deps[oop.idx] = oop
        for reg in regs_w:
            for oreg, oop in self.writers.get(reg[0], ()):
                if _overlap(reg, oreg):
                    deps[oop.idx] = oop
            for oreg, oop in self.readers.get(reg[0], ()):
                if _overlap(reg, oreg):
                    deps[oop.idx] = oop
        for dep in deps.values():
            if dep is op:
                continue
            if dep.is_dma:
                op.deps.append(("sem", dep.sem, self.dma_cnt[dep.sem] * 16))
            else:
                if dep.eng == op.eng == "tensor" and not op.is_dma:
                    continue
                dep.inc_needed = True
                op.deps.append(("op", dep))
        for reg in regs_r:
            self.readers.setdefault(reg[0], []).append((reg, op))
        for reg in regs_w:
            wl = self.writers.setdefault(reg[0], [])
            wl[:] = [(orr, oo) for (orr, oo) in wl
                     if not (orr[1] >= reg[1] and orr[2] <= reg[2]
                             and orr[3] >= reg[3] and orr[4] <= reg[4])]
            wl.append((reg, op))
            rl = self.readers.get(reg[0])
            if rl:
                rl[:] = [(orr, oo) for (orr, oo) in rl if not _overlap(reg, orr)]

    def op(self, eng, emit, r=(), w=(), dbg=""):
        o = Op(len(self.ops), eng, emit, False, dbg or self.dbg_tag)
        self.ops.append(o)
        self._record(o, r, w)
        self.last_op[eng] = o
        return o

    def _dma_sem_for(self, out, in_):
        tname = None
        for ap in (out, in_):
            if type(ap.tensor).__name__ == "SBTensorHandle":
                tname = ap.tensor.name
                break
        if tname is None:
            tname = "@dram"
        return tname

    def dma(self, queue, out, in_, group=None, **kw):
        swdge = queue == "qg"
        key = (group or self._dma_sem_for(out, in_), swdge)
        if not hasattr(self, "_dma_sem_map"):
            self._dma_sem_map = {}
            self._sem_rr = {False: 0, True: 0}
        if key not in self._dma_sem_map:
            if swdge:
                self._dma_sem_map[key] = "dg%d" % (self._sem_rr[True] % 32)
            else:
                self._dma_sem_map[key] = "dh%d" % (self._sem_rr[False] % 64)
            self._sem_rr[swdge] += 1
        semname = self._dma_sem_map[key]
        self.sem(semname)

        def emit(e, out=out, in_=in_, kw=kw):
            return e.dma_start(out=out, in_=in_, **kw)
        o = Op(len(self.ops), QUEUE_ENG[queue], emit, True, self.dbg_tag)
        o.sem = semname
        self.ops.append(o)
        self._record(o, [in_], [out])
        self.dma_cnt[semname] = self.dma_cnt.get(semname, 0) + 1
        o.val = self.dma_cnt[semname] * 16
        self.last_op[o.eng] = o
        return o

    def phase(self, queue=None):
        for q in ([queue] if queue else list(QUEUE_ENG)):
            self.phase_idx[q] += 1

    def build(self):
        nc = self.nc
        eng_cnt = {e: 0 for e in ENGINES}
        th = {e: [] for e in ENGINES}
        watermark = {e: {} for e in ENGINES}
        n_waits = 0
        for o in self.ops:
            if not o.is_dma and o.inc_needed:
                eng_cnt[o.eng] += 1
                o.sem = "e_" + o.eng
                o.val = eng_cnt[o.eng]
                self.sem(o.sem)
                if o.eng == "gpsimd" and o.val > 1:
                    o.deps.append(("sem", o.sem, o.val - 1))
            for d in o.deps:
                if d[0] == "sem":
                    semname, val = d[1], d[2]
                else:
                    dep = d[1]
                    semname, val = dep.sem, dep.val
                    assert val is not None, (o.dbg, dep.dbg)
                wm = watermark[o.eng]
                if wm.get(semname, 0) >= val:
                    continue
                wm[semname] = val
                h = self.semh[semname]
                th[o.eng].append(lambda e, h=h, v=val: e.wait_ge(h, v))
                n_waits += 1
            if o.sem is not None and (o.is_dma or o.inc_needed):
                h = self.semh[o.sem]
                th[o.eng].append(
                    lambda e, o=o, h=h: o.emit(e).then_inc(h, 16 if o.is_dma else 1))
            else:
                th[o.eng].append(o.emit)
        for semname, cnt in self.dma_cnt.items():
            h = self.semh[semname]
            th["sync"].append(lambda e, h=h, v=cnt * 16: e.wait_ge(h, v))
        self._n_waits = n_waits
        with nc.Block() as block:
            for eng in ENGINES:
                lst = th[eng]
                if lst:
                    getattr(block, eng)(lambda e, lst=lst: [f(e) for f in lst])
        return nc


def is_ap(x):
    return hasattr(x, "tensor")


class K:

    def __init__(self, p):
        self.p = p
        self._consts = {}

    def constcol(self, val, dtype=F32):
        key = (float(val), dtype)
        if key not in self._consts:
            t = self.p.nc.alloc_sbuf_tensor(f"c_k{len(self._consts)}", [128, 1], dtype, side="right")
            self.memset(t[:], float(val))
            self._consts[key] = t
        return self._consts[key]

    def mm(self, out, lhsT, rhs, start=True, stop=True):
        return self.p.op("tensor",
                         lambda e, out=out, lhsT=lhsT, rhs=rhs, start=start, stop=stop:
                         e.matmul(out, lhsT, rhs, start=start, stop=stop, skip_group_check=True),
                         r=[lhsT, rhs], w=[out])

    def trans(self, out, in_, ident):
        return self.p.op("tensor",
                         lambda e, out=out, in_=in_, ident=ident: e.transpose(out, in_, ident),
                         r=[in_, ident], w=[out])

    def act(self, out, in_, func, bias=0.0, scale=1.0, accum=None):
        np0 = _part_span(out)
        if not is_ap(bias) and float(bias) != 0.0:
            bias = self.constcol(bias)[np0[0]:np0[1], :]
        if not is_ap(scale) and float(scale) != 1.0:
            scale = self.constcol(scale)[np0[0]:np0[1], :]
        r = [in_] + [x for x in (bias, scale) if is_ap(x)]
        w = [out] + ([accum] if accum is not None else [])
        kw = {}
        if accum is not None:
            kw["accum_out"] = accum
        return self.p.op("scalar",
                         lambda e, out=out, in_=in_, func=func, bias=bias, scale=scale, kw=kw:
                         e.activation(out, in_, func, bias=bias, scale=scale, **kw),
                         r=r, w=w)

    def ts(self, out, in0, s1, s2=None, op0=ALU.mult, op1=ALU.bypass, eng="vector"):
        r = [in0] + [x for x in (s1, s2) if is_ap(x)]
        if s2 is None:
            return self.p.op(eng,
                             lambda e, out=out, in0=in0, s1=s1, op0=op0:
                             e.tensor_scalar(out, in0, s1, None, op0=op0),
                             r=r, w=[out])
        return self.p.op(eng,
                         lambda e, out=out, in0=in0, s1=s1, s2=s2, op0=op0, op1=op1:
                         e.tensor_scalar(out, in0, s1, s2, op0=op0, op1=op1),
                         r=r, w=[out])

    def tt(self, out, in0, in1, op, eng="vector"):
        return self.p.op(eng,
                         lambda e, out=out, in0=in0, in1=in1, op=op:
                         e.tensor_tensor(out, in0, in1, op=op),
                         r=[in0, in1], w=[out])

    def stt(self, out, in0, s, in1, op0, op1, eng="vector"):
        r = [in0, in1] + ([s] if is_ap(s) else [])
        return self.p.op(eng,
                         lambda e, out=out, in0=in0, s=s, in1=in1, op0=op0, op1=op1:
                         e.scalar_tensor_tensor(out, in0, s, in1, op0=op0, op1=op1),
                         r=r, w=[out])

    def red(self, out, in_, op, eng="vector"):
        return self.p.op(eng,
                         lambda e, out=out, in_=in_, op=op:
                         e.tensor_reduce(out, in_, axis=AX.X, op=op),
                         r=[in_], w=[out])

    def cp(self, out, in_, eng="vector"):
        return self.p.op(eng,
                         lambda e, out=out, in_=in_: e.tensor_copy(out, in_),
                         r=[in_], w=[out])

    def memset(self, ap, c, eng="vector"):
        return self.p.op(eng, lambda e, ap=ap, c=c: e.memset(ap, c), r=[], w=[ap])

    def iota(self, ap, pattern, base=0, chan=0):
        return self.p.op("gpsimd",
                         lambda e, ap=ap, pattern=pattern, base=base, chan=chan:
                         e.iota(ap, pattern=pattern, base=base, channel_multiplier=chan,
                                allow_small_or_imprecise_dtypes=True),
                         r=[], w=[ap])


class Consts:
    def __init__(self, p, k):
        nc = p.nc
        self.ident = nc.alloc_sbuf_tensor("c_ident", [128, 128], F32, side="right")
        self.ones_col_b = nc.alloc_sbuf_tensor("c_ones_col_b", [128, 1], BF16, side="right")
        self.ones_col_f = nc.alloc_sbuf_tensor("c_ones_col_f", [128, 1], F32, side="right")
        self.ones_row_f = nc.alloc_sbuf_tensor("c_ones_row_f", [1, 128], F32, side="right")
        self.ones_row_b = nc.alloc_sbuf_tensor("c_ones_row_b", [1, 128], BF16, side="right")
        self.jrow = nc.alloc_sbuf_tensor("c_jrow", [128, TT], F32, side="right")
        self.pcol = nc.alloc_sbuf_tensor("c_pcol", [128, 1], F32, side="right")
        self.masks = nc.alloc_sbuf_tensor("c_masks", [128, 4 * TT], BF16, side="right")

        k.iota(self.jrow[:], [[1, TT]], 0, 0)
        k.iota(self.pcol[:], [[0, 1]], 0, 1)
        k.ts(self.ident[:], self.jrow[:, 0:128], self.pcol[:], op0=ALU.is_equal)
        k.memset(self.ones_col_b[:], 1.0)
        k.memset(self.ones_col_f[:], 1.0)
        k.memset(self.ones_row_f[:], 1.0)
        k.memset(self.ones_row_b[:], 1.0)
        for d in range(4):
            m = self.masks[:, d * TT:(d + 1) * TT]
            k.ts(m, self.jrow[:], self.pcol[:], float(128 * d),
                 op0=ALU.subtract, op1=ALU.subtract)
            k.ts(m, m, 0.0, op0=ALU.is_ge)


class PrologueBufs:

    def __init__(self, p, st):
        nc = p.nc
        self.slab_f = [st.enter_context(nc.sbuf_tensor(f"pl_sf{i}", [128, 3072], F32))
                       for i in range(3)]
        self.slab_i = [st.enter_context(nc.sbuf_tensor(f"pl_si{i}", [128, 3072], I32))
                       for i in range(3)]
        self.slab_c = [st.enter_context(nc.sbuf_tensor(f"pl_sc{i}", [128, 3072], F32))
                       for i in range(1)]
        self.slab_b = [st.enter_context(nc.sbuf_tensor(f"pl_sb{i}", [128, 3072], BF16))
                       for i in range(3)]
        self.pst = [st.enter_context(nc.psum_tensor(f"pl_ps{i}", [128, 128], F32))
                    for i in range(2)]
        self.band = [st.enter_context(nc.sbuf_tensor(f"pl_band{i}", [128, 24 * 512], BF16))
                     for i in range(1)]


def emit_prep_transpose(p, k, C, src, dst, rows, cols, src_dtype, uid, B):
    nb = rows // 128
    ncol = cols // 128
    slab = B.slab_f if src_dtype == F32 else B.slab_i
    for b in range(nb):
        sl = slab[b % 3]
        p.dma("qs", sl[:, 0:cols], src[b * 128:(b + 1) * 128, :])
        if src_dtype != F32:
            k.cp(B.slab_c[0][:, 0:cols], sl[:, 0:cols])
            sf = B.slab_c[0]
        else:
            sf = sl
        g = 0
        for c in range(ncol):
            k.trans(B.pst[c % 2][:], sf[:, c * 128:(c + 1) * 128], C.ident[:])
            k.cp(B.band[g][:, c * 512 + (b % 4) * 128:c * 512 + (b % 4) * 128 + 128],
                 B.pst[c % 2][:])
        if b % 4 == 3 or b == nb - 1:
            b0 = (b // 4) * 4
            w = (b - b0 + 1) * 128
            for c in range(ncol):
                p.dma("qa", dst[c * 128:(c + 1) * 128, b0 * 128:b0 * 128 + w],
                      B.band[g][:, c * 512:c * 512 + w])


def emit_ln_fm(p, k, C, x_src, xn_dst, uid, mr_dst=None, sumx_fp32=False):
    from contextlib import ExitStack
    nc = p.nc
    with ExitStack() as stk:
        xt = [[stk.enter_context(nc.sbuf_tensor(f"{uid}_x{j}_{i}", [128, TT], F32))
               for i in range(DC)] for j in range(2)]
        xn = [[stk.enter_context(nc.sbuf_tensor(f"{uid}_xn{j}_{i}", [128, TT], BF16))
               for i in range(DC)] for j in range(2)]
        xb = [stk.enter_context(nc.sbuf_tensor(f"{uid}_xb{i}", [128, TT], BF16)) for i in range(DC)]
        sq = [stk.enter_context(nc.sbuf_tensor(f"{uid}_sq{i}", [128, TT], BF16)) for i in range(DC)]
        sqf = [stk.enter_context(nc.sbuf_tensor(f"{uid}_sqf{i}", [128, TT], F32)) for i in range(DC)]             if sumx_fp32 else None
        rows2 = [stk.enter_context(nc.sbuf_tensor(f"{uid}_rows{j}", [1, 6 * TT], F32)) for j in range(2)]
        tmp = [stk.enter_context(nc.sbuf_tensor(f"{uid}_tmp{j}", [128, TT], F32)) for j in range(2)]
        ps_st = [stk.enter_context(nc.psum_tensor(f"{uid}_ps_st{j}", [1, 2 * TT], F32)) for j in range(2)]
        ps_b = [stk.enter_context(nc.psum_tensor(f"{uid}_ps_b{j}", [128, 2 * TT], F32)) for j in range(2)]

        for tt in range(NTT):
            j = tt % 2
            rows = rows2[j]
            r_m = rows[:, 0:TT]; r_ms = rows[:, TT:2 * TT]; r_var = rows[:, 2 * TT:3 * TT]
            r_ln = rows[:, 3 * TT:4 * TT]; r_r = rows[:, 4 * TT:5 * TT]; r_mr = rows[:, 5 * TT:6 * TT]
            for dc in range(DC):
                p.dma("qs", xt[j][dc][:], x_src[dc * 128:(dc + 1) * 128, tt * TT:(tt + 1) * TT])
            for dc in range(DC):
                if sumx_fp32:
                    k.act(sqf[dc][:], xt[j][dc][:], AF.Square)
                else:
                    k.act(sq[dc][:], xt[j][dc][:], AF.Square)
                k.cp(xb[dc][:], xt[j][dc][:], eng="gpsimd")
            for dc in range(DC):
                if sumx_fp32:
                    k.mm(ps_st[j][:, 0:TT], C.ones_col_f[:], xt[j][dc][:], start=(dc == 0), stop=(dc == DC - 1))
                else:
                    k.mm(ps_st[j][:, 0:TT], C.ones_col_b[:], xb[dc][:], start=(dc == 0), stop=(dc == DC - 1))
            for dc in range(DC):
                if sumx_fp32:
                    k.mm(ps_st[j][:, TT:2 * TT], C.ones_col_f[:], sqf[dc][:], start=(dc == 0), stop=(dc == DC - 1))
                else:
                    k.mm(ps_st[j][:, TT:2 * TT], C.ones_col_b[:], sq[dc][:], start=(dc == 0), stop=(dc == DC - 1))
            k.ts(r_m, ps_st[j][:, 0:TT], 1.0 / D)
            k.ts(r_ms, ps_st[j][:, TT:2 * TT], 1.0 / D)
            k.tt(r_var, r_m, r_m, ALU.mult)
            k.tt(r_var, r_ms, r_var, ALU.subtract)
            k.act(r_ln, r_var, AF.Ln, bias=float(EPS))
            k.act(r_r, r_ln, AF.Exp, scale=-0.5)
            k.tt(r_mr, r_m, r_r, ALU.mult)
            k.mm(ps_b[j][:, 0:TT], C.ones_row_f[:], r_r)
            k.mm(ps_b[j][:, TT:2 * TT], C.ones_row_f[:], r_mr)
            if mr_dst is not None:
                p.dma("qa", mr_dst[0:1, tt * TT:(tt + 1) * TT], r_m)
                p.dma("qa", mr_dst[1:2, tt * TT:(tt + 1) * TT], r_r)
            for dc in range(DC):
                k.tt(tmp[j][:], xt[j][dc][:], ps_b[j][:, 0:TT], ALU.mult)
                k.tt(xn[j][dc][:], tmp[j][:], ps_b[j][:, TT:2 * TT], ALU.subtract)
                p.dma("qa", xn_dst[dc * 128:(dc + 1) * 128, tt * TT:(tt + 1) * TT], xn[j][dc][:])
    p.barrier()


def emit_block1(p, k, C, xin, xout, W, uid="b1"):
    from contextlib import ExitStack
    nc = p.nc
    wvT = get_prep(p, k, C, W, "wvT")
    woT = get_prep(p, k, C, W, "woT")

    with ExitStack() as stk:
        ND = 4
        xt = [stk.enter_context(nc.sbuf_tensor(f"{uid}_x{i}", [128, D], F32)) for i in range(ND)]
        xnb4 = [stk.enter_context(nc.sbuf_tensor(f"{uid}_xnb{i}", [128, D], BF16)) for i in range(ND)]
        sqd4 = [stk.enter_context(nc.sbuf_tensor(f"{uid}_sqd{i}", [128, D], BF16)) for i in range(ND)]
        cols4 = [stk.enter_context(nc.sbuf_tensor(f"{uid}_cols{i}", [128, 8], F32)) for i in range(ND)]
        xfb = [stk.enter_context(nc.sbuf_tensor(f"{uid}_xfb{i}", [128, 128], F32)) for i in range(4)]
        ps_cs = stk.enter_context(nc.psum_tensor(f"{uid}_ps_cs", [1, D], F32))
        ps_tr = stk.enter_context(nc.psum_tensor(f"{uid}_ps_tr", [128, 128], F32))
        ps_mv = stk.enter_context(nc.psum_tensor(f"{uid}_ps_mv", [1, D], F32))
        ps_c1 = stk.enter_context(nc.psum_tensor(f"{uid}_ps_c1", [128, 1], F32))

        for i in range(S // 128):
            x0 = xt[i % ND]
            xnb = xnb4[i % ND]
            sqd = sqd4[i % ND]
            cols = cols4[i % ND]
            s_col = cols[:, 0:1]; m_col = cols[:, 1:2]; acc = cols[:, 2:3]; v1 = cols[:, 3:4]
            m2 = cols[:, 4:5]; var = cols[:, 5:6]; rstd = cols[:, 6:7]; nmr = cols[:, 7:8]
            p.dma("qs", x0[:], xin[i * 128:(i + 1) * 128, :])
            k.red(s_col, x0[:], ALU.add)
            k.ts(m_col, s_col, 1.0 / D)
            k.act(sqd[:], x0[:], AF.Square, accum=acc)
            k.ts(v1, acc, 1.0 / D)
            k.tt(m2, m_col, m_col, ALU.mult)
            k.tt(var, v1, m2, ALU.subtract)
            k.act(var, var, AF.Ln, bias=float(EPS))
            k.act(rstd, var, AF.Exp, scale=-0.5)
            k.tt(nmr, m_col, rstd, ALU.mult)
            k.ts(nmr, nmr, -1.0)
            k.act(xnb[:], x0[:], AF.Identity, bias=nmr, scale=rstd)
            for c in range(8):
                k.trans(ps_tr[:], x0[:, c * 128:(c + 1) * 128], C.ident[:])
                f = xfb[c % 4]
                k.cp(f[:], ps_tr[:])
                p.dma("qs", xout[c * 128:(c + 1) * 128, i * 128:(i + 1) * 128], f[:])
            if i > 0:
                xp = xnb4[(i - 1) % ND]
                k.mm(ps_cs[:, 0:512], C.ones_col_b[:], xp[:, 0:512],
                     start=(i == 1), stop=False)
                k.mm(ps_cs[:, 512:1024], C.ones_col_b[:], xp[:, 512:1024],
                     start=(i == 1), stop=False)
            if i == S // 128 - 1:
                k.mm(ps_cs[:, 0:512], C.ones_col_b[:], xnb[:, 0:512],
                     start=False, stop=True)
                k.mm(ps_cs[:, 512:1024], C.ones_col_b[:], xnb[:, 512:1024],
                     start=False, stop=True)

        rows = stk.enter_context(nc.sbuf_tensor(f"{uid}_rows", [1, 3 * D], F32))
        cs_row = rows[:, 0:D]; vs_row = rows[:, D:2 * D]; or_row = rows[:, 2 * D:3 * D]
        ccols = stk.enter_context(nc.sbuf_tensor(f"{uid}_ccols", [128, 16], BF16))
        cs_colb = ccols[:, 0:8]; vs_colb = ccols[:, 8:16]
        or_col = stk.enter_context(nc.sbuf_tensor(f"{uid}_orcol", [128, 8], F32))
        wrow = [stk.enter_context(nc.sbuf_tensor(f"{uid}_w{i}", [128, D], BF16)) for i in range(2)]

        k.cp(cs_row, ps_cs[:])
        for c in range(8):
            k.trans(ps_c1[:], cs_row[0:1, c * 128:(c + 1) * 128], C.ident[0:1, 0:1])
            k.cp(cs_colb[:, c:c + 1], ps_c1[:])
        for dc in range(8):
            wb = wrow[dc % 2]
            p.dma("qs", wb[:], wvT[dc * 128:(dc + 1) * 128, :])
            k.mm(ps_mv[:, 0:512], cs_colb[:, dc:dc + 1], wb[:, 0:512],
                 start=(dc == 0), stop=(dc == 7))
            k.mm(ps_mv[:, 512:1024], cs_colb[:, dc:dc + 1], wb[:, 512:1024],
                 start=(dc == 0), stop=(dc == 7))
        k.cp(vs_row, ps_mv[:])
        for c in range(8):
            k.trans(ps_c1[:], vs_row[0:1, c * 128:(c + 1) * 128], C.ident[0:1, 0:1])
            k.cp(vs_colb[:, c:c + 1], ps_c1[:])
        for dc in range(8):
            wb = wrow[dc % 2]
            p.dma("qs", wb[:], woT[dc * 128:(dc + 1) * 128, :])
            k.mm(ps_mv[:, 0:512], vs_colb[:, dc:dc + 1], wb[:, 0:512],
                 start=(dc == 0), stop=(dc == 7))
            k.mm(ps_mv[:, 512:1024], vs_colb[:, dc:dc + 1], wb[:, 512:1024],
                 start=(dc == 0), stop=(dc == 7))
        k.cp(or_row, ps_mv[:])
        for c in range(8):
            k.trans(ps_c1[:], or_row[0:1, c * 128:(c + 1) * 128], C.ident[0:1, 0:1])
            k.cp(or_col[:, c:c + 1], ps_c1[:])

        rt = [stk.enter_context(nc.sbuf_tensor(f"{uid}_rt{i}", [128, TT], F32)) for i in range(2)]
        for dc in range(DC):
            for tt in range(NTT):
                bbuf = rt[tt % 2]
                p.dma("qs", bbuf[:], xout[dc * 128:(dc + 1) * 128, tt * TT:(tt + 1) * TT])
                k.ts(bbuf[:], bbuf[:], or_col[:, dc:dc + 1], op0=ALU.add)
                p.dma("qs", xout[dc * 128:(dc + 1) * 128, tt * TT:(tt + 1) * TT], bbuf[:])
    p.barrier()


def emit_l1ffn(p, k, C, xin, xout, W, uid, preps, last=False):
    from contextlib import ExitStack
    nc = p.nc
    wgT, wuT, wdT = preps["wgT"], preps["wuT"], preps["wdT"]
    xn_buf = nc.dram_tensor(f"{uid}_xn", [D, S], BF16, kind="Internal").ap()
    emit_ln_fm(p, k, C, xin, xn_buf, uid + "ln")
    HB = 3072
    HC = HB // 128
    INV = 1.0 / 127.0
    with ExitStack() as stk:
        wg_sb = [stk.enter_context(nc.sbuf_tensor(f"{uid}_wg{i}", [128, HB], BF16)) for i in range(DC)]
        wu_sb = [stk.enter_context(nc.sbuf_tensor(f"{uid}_wu{i}", [128, HB], BF16)) for i in range(DC)]
        wd_sb = [stk.enter_context(nc.sbuf_tensor(f"{uid}_wd{i}", [128, D], BF16)) for i in range(HC)]
        for dc in range(DC):
            p.dma("qs", wg_sb[dc][:], wgT[dc * 128:(dc + 1) * 128, :], group="wgu")
            p.dma("qs", wu_sb[dc][:], wuT[dc * 128:(dc + 1) * 128, :], group="wug")
        for hc in range(HC):
            p.dma("qs", wd_sb[hc][:], wdT[hc * 128:(hc + 1) * 128, :], group="wd")
        xn_sb = [[stk.enter_context(nc.sbuf_tensor(f"{uid}_xn{j}_{i}", [128, TT], BF16))
                  for i in range(DC)] for j in range(2)]
        sg = [stk.enter_context(nc.sbuf_tensor(f"{uid}_sg{j}", [128, TT], BF16)) for j in range(2)]
        h_sb = [stk.enter_context(nc.sbuf_tensor(f"{uid}_h{i}", [128, TT], BF16)) for i in range(HC)]
        xo = [stk.enter_context(nc.sbuf_tensor(f"{uid}_xo{i}", [128, TT], F32)) for i in range(2)]
        xr = [stk.enter_context(nc.sbuf_tensor(f"{uid}_xr{i}", [128, TT], F32)) for i in range(2)]
        yt = [stk.enter_context(nc.sbuf_tensor(f"{uid}_yt{i}", [128, 128], F32)) for i in range(2)]
        ps_gu = [stk.enter_context(nc.psum_tensor(f"{uid}_psgu{j}", [128, TT], F32)) for j in range(2)]
        ps_y = [stk.enter_context(nc.psum_tensor(f"{uid}_psy{i}", [128, 1024], F32)) for i in range(2)]
        ps_tr = [stk.enter_context(nc.psum_tensor(f"{uid}_pst{i}", [128, 128], F32)) for i in range(2)]             if last else None

        def py(oc):
            return ps_y[(oc % 4) // 2][:, (oc % 2) * TT:(oc % 2) * TT + TT]

        def evac(oc, tt):
            xrb = xr[oc % 2]
            p.dma("qs", xrb[:], xin[oc * 128:(oc + 1) * 128, tt * TT:(tt + 1) * TT])
            ob = xo[oc % 2]
            k.stt(ob[:], py(oc), INV, xrb[:], op0=ALU.mult, op1=ALU.add)
            if not last:
                p.dma("qa", xout[oc * 128:(oc + 1) * 128, tt * TT:(tt + 1) * TT], ob[:])
            else:
                for q in range(TT // 128):
                    k.trans(ps_tr[q % 2][:], ob[:, q * 128:(q + 1) * 128], C.ident[:])
                    k.cp(yt[q % 2][:], ps_tr[q % 2][:])
                    t0 = tt * TT + q * 128
                    p.dma("qa", xout[t0:t0 + 128, oc * 128:(oc + 1) * 128], yt[q % 2][:])

        for tt in range(NTT):
            j = tt % 2
            for dc in range(DC):
                p.dma("qs", xn_sb[j][dc][:],
                      xn_buf[dc * 128:(dc + 1) * 128, tt * TT:(tt + 1) * TT])
            for hc in range(HC):
                hj = hc % 2
                gu = ps_gu[hj]
                for dc in range(DC):
                    k.mm(gu[:], wg_sb[dc][:, hc * 128:(hc + 1) * 128], xn_sb[j][dc][:],
                         start=(dc == 0), stop=(dc == DC - 1))
                k.act(sg[hj][:], gu[:], AF.Silu, scale=INV)
                for dc in range(DC):
                    k.mm(gu[:], wu_sb[dc][:, hc * 128:(hc + 1) * 128], xn_sb[j][dc][:],
                         start=(dc == 0), stop=(dc == DC - 1))
                k.stt(h_sb[hc][:], gu[:], INV, sg[hj][:], op0=ALU.mult, op1=ALU.mult)
                for oc in range(4):
                    k.mm(py(oc), wd_sb[hc][:, oc * 128:(oc + 1) * 128], h_sb[hc][:],
                         start=(hc == 0), stop=(hc == HC - 1))
            for oc in range(4):
                evac(oc, tt)
            for oc in range(4, DC):
                for hc in range(HC):
                    k.mm(py(oc), wd_sb[hc][:, oc * 128:(oc + 1) * 128], h_sb[hc][:],
                         start=(hc == 0), stop=(hc == HC - 1))
            for oc in range(4, DC):
                evac(oc, tt)
    p.barrier()


INPUT_SPECS = {
    "L0_attn_a_x": ((8, 4096, 1024), F32),
    "L0_attn_a_w_qkv": ((3072, 1024), F32),
    "L0_attn_a_w_o": ((1024, 1024), F32),
    "L0_attn_b_W_qkv": ((1024, 3072), F32),
    "L0_attn_b_u_qkv": ((3072,), F32),
    "L0_attn_b_sigma_qkv": ((1,), F32),
    "L0_attn_b_W_proj": ((1024, 1024), F32),
    "L0_attn_b_u_proj": ((1024,), F32),
    "L0_attn_b_sigma_proj": ((1,), F32),
    "L0_ffn_gate_w": ((16, 1024), F32),
    "L0_ffn_gate_b": ((16,), F32),
    "L0_ffn_We1": ((16, 1024, 512), F32),
    "L0_ffn_be1": ((16, 512), F32),
    "L0_ffn_We2": ((16, 512, 1024), F32),
    "L0_ffn_be2": ((16, 1024), F32),
    "L0_ffn_We3": ((16, 1024, 512), F32),
    "L0_ffn_be3": ((16, 512), F32),
    "L0_ffn_ws1": ((1024, 1024), F32),
    "L0_ffn_bs1": ((1024,), F32),
    "L0_ffn_ws2": ((1024, 1024), F32),
    "L0_ffn_bs2": ((1024,), F32),
    "L0_ffn_ws3": ((1024, 1024), F32),
    "L0_ffn_bs3": ((1024,), F32),
    "L1_attn_a_Wq": ((16, 64, 1024), F32),
    "L1_attn_a_bq": ((16, 64), F32),
    "L1_attn_a_Wk": ((16, 64, 1024), F32),
    "L1_attn_a_bk": ((16, 64), F32),
    "L1_attn_a_Wv": ((16, 64, 1024), F32),
    "L1_attn_a_bv": ((16, 64), F32),
    "L1_attn_b_W_qkv": ((3072, 1024), F32),
    "L1_attn_b_W_proj": ((1024, 1024), F32),
    "L1_attn_b_b_proj": ((1024,), F32),
    "L1_ffn_w_gate": ((3072, 1024), I32),
    "L1_ffn_w_up": ((3072, 1024), I32),
    "L1_ffn_w_down": ((1024, 3072), I32),
}


def build_program(block_ids):
    p = P()
    k = K(p)
    C = Consts(p, k)
    nc = p.nc
    needed = set()
    defs = BLOCK_DEFS
    if block_ids[0] == 0:
        xin = nc.dram_tensor("L0_attn_a_x", [S, D], F32, kind="ExternalInput").ap()
        needed.add("L0_attn_a_x")
    else:
        xin = nc.dram_tensor("x_fm_in", [D, S], F32, kind="ExternalInput").ap()
    n_interior = len(block_ids) - 1
    interiors = [nc.dram_tensor(f"xi{i}", [D, S], F32, kind="Internal").ap()
                 for i in range(n_interior)]
    if block_ids[-1] == 7:
        xout_last = nc.dram_tensor("y", [S, D], F32, kind="ExternalOutput").ap()
    else:
        xout_last = nc.dram_tensor("x_fm_out", [D, S], F32, kind="ExternalOutput").ap()
    W = {}
    for bi in block_ids:
        for name in defs[bi][2]:
            if name not in W:
                shape, dt = INPUT_SPECS[name]
                W[name] = nc.dram_tensor(name, list(shape), dt, kind="ExternalInput").ap()
                needed.add(name)
    from contextlib import ExitStack
    with ExitStack() as plst:
        B = PrologueBufs(p, plst)
        if 0 in block_ids:
            for key in PREPS_FOR_BLOCK.get(0, []):
                get_prep(p, k, C, W, key, B)
        if block_ids and block_ids[0] == 0:
            xo0 = interiors[0] if n_interior > 0 else xout_last
            defs[0][1](p, k, C, xin, xo0, W, uid="B0")
        for bi in block_ids:
            for key in PREPS_FOR_BLOCK.get(bi, []):
                get_prep(p, k, C, W, key, B)
        for bi in block_ids:
            for key in CASTS_FOR_BLOCK.get(bi, []):
                get_cast(p, k, C, W, key, B)
    p.barrier()
    for j, bi in enumerate(block_ids):
        if bi == 0:
            continue
        xo = interiors[j] if j < n_interior else xout_last
        xi = xin if j == 0 else interiors[j - 1]
        defs[bi][1](p, k, C, xi, xo, W, uid=f"B{bi}")
        p.barrier()
    return p.build(), sorted(needed)


def load_vec_cols(p, k, C, src_vec, n, uid, stk, ps_c1):
    nc = p.nc
    row = stk.enter_context(nc.sbuf_tensor(f"{uid}_vr", [1, n], F32))
    col = stk.enter_context(nc.sbuf_tensor(f"{uid}_vc", [128, n // 128], F32))
    p.dma("qs", row[:], src_vec.rearrange("(a n) -> a n", a=1))
    for c in range(n // 128):
        k.trans(ps_c1[:], row[0:1, c * 128:(c + 1) * 128], C.ident[0:1, 0:1])
        k.cp(col[:, c:c + 1], ps_c1[:])
    return col


def emit_sn_scale(p, k, C, W_ap, u_ap, sigma_ap, n_out, uid, s_col):
    from contextlib import ExitStack
    nc = p.nc
    NP = n_out // 512
    with ExitStack() as stk:
        wch = [stk.enter_context(nc.sbuf_tensor(f"{uid}_w{i}", [128, n_out], F32)) for i in range(2)]
        urow = stk.enter_context(nc.sbuf_tensor(f"{uid}_u", [1, n_out], F32))
        ub = stk.enter_context(nc.sbuf_tensor(f"{uid}_ub", [128, n_out], F32))
        tmp = stk.enter_context(nc.sbuf_tensor(f"{uid}_t", [128, n_out], F32))
        vcol = stk.enter_context(nc.sbuf_tensor(f"{uid}_vcol", [128, 8], F32))
        sq8 = stk.enter_context(nc.sbuf_tensor(f"{uid}_sq8", [128, 8], F32))
        vs2 = stk.enter_context(nc.sbuf_tensor(f"{uid}_vs2", [128, 1], F32))
        u2row = stk.enter_context(nc.sbuf_tensor(f"{uid}_u2", [1, n_out], F32))
        sqrow = stk.enter_context(nc.sbuf_tensor(f"{uid}_sqr", [1, n_out], F32))
        sc = stk.enter_context(nc.sbuf_tensor(f"{uid}_sc", [1, 8], F32))
        nv2 = sc[:, 0:1]; nu2 = sc[:, 1:2]; lnv = sc[:, 2:3]; lnu = sc[:, 3:4]
        dif = sc[:, 4:5]; s0 = sc[:, 5:6]; sig = sc[:, 6:7]; s1 = sc[:, 7:8]
        ps_b = stk.enter_context(nc.psum_tensor(f"{uid}_psb", [128, 512], F32))
        ps_u2 = stk.enter_context(nc.psum_tensor(f"{uid}_psu", [1, n_out], F32))
        ps_s = ps_b[0:1, 0:1]
        ps_c = ps_b[:, 0:1]

        p.dma("qs", urow[:], u_ap.rearrange("(a n) -> a n", a=1))
        p.dma("qs", sig[0:1, :], sigma_ap.rearrange("(a n) -> a n", a=1))
        for pc in range(NP):
            k.mm(ps_b[:], C.ones_row_f[:], urow[:, pc * 512:(pc + 1) * 512])
            k.cp(ub[:, pc * 512:(pc + 1) * 512], ps_b[:])
        for rc in range(8):
            w = wch[rc % 2]
            p.dma("qs", w[:], W_ap[rc * 128:(rc + 1) * 128, :])
            k.tt(tmp[:], w[:], ub[:], ALU.mult)
            k.red(vcol[:, rc:rc + 1], tmp[:], ALU.add)
            for pc in range(NP):
                k.mm(ps_u2[:, pc * 512:(pc + 1) * 512], vcol[:, rc:rc + 1],
                     w[:, pc * 512:(pc + 1) * 512], start=(rc == 0), stop=(rc == 7))
        k.act(sq8[:], vcol[:], AF.Square, accum=vs2[:])
        k.mm(ps_s[:], vs2[:], C.ones_col_f[:])
        k.cp(nv2[0:1, :], ps_s[:])
        k.cp(u2row[:], ps_u2[:])
        k.act(sqrow[:], u2row[:], AF.Square, accum=nu2[0:1, :])
        k.act(lnv[0:1, :], nv2[0:1, :], AF.Ln)
        k.act(lnu[0:1, :], nu2[0:1, :], AF.Ln)
        k.tt(dif[0:1, :], lnv[0:1, :], lnu[0:1, :], ALU.subtract)
        k.act(s0[0:1, :], dif[0:1, :], AF.Exp, scale=self_half(k))
        k.tt(s1[0:1, :], s0[0:1, :], sig[0:1, :], ALU.mult)
        k.mm(ps_c[:], C.ones_row_f[:], s1[0:1, :])
        k.cp(s_col, ps_c[:])
    p.barrier()


def self_half(k):
    return 0.5


def emit_attention(p, k, C, xi, xo, W, uid, flavor):
    from contextlib import ExitStack
    nc = p.nc
    causal = flavor in ("l0b", "l1a")
    has_proj = flavor in ("l0b", "l1b")
    H, HD = 16, 64
    xn_buf = nc.dram_tensor(f"{uid}_xn", [D, S], BF16, kind="Internal").ap()
    q_buf = nc.dram_tensor(f"{uid}_q", [D, S], BF16, kind="Internal").ap()
    k_buf = nc.dram_tensor(f"{uid}_k", [D, S], BF16, kind="Internal").ap()
    ao_buf = nc.dram_tensor(f"{uid}_ao", [D, S], BF16, kind="Internal").ap() if has_proj else None
    emit_ln_fm(p, k, C, xi, xn_buf, uid + "ln")

    with ExitStack() as outer:
        v_aug = outer.enter_context(nc.sbuf_tensor(f"{uid}_vaug", [128, 32 * H * 65], BF16))
        vv4 = v_aug.ap().rearrange("p (t h j) -> p t h j", h=H, j=65)
        k.memset(vv4[:, :, :, 64:65], 1.0)
        scol = outer.enter_context(nc.sbuf_tensor(f"{uid}_scol", [128, 4], F32))
        s_q = scol[:, 0:1]; s_k = scol[:, 1:2]; s_p = scol[:, 2:3]

        if flavor == "l0b":
            emit_sn_scale(p, k, C, W["L0_attn_b_W_qkv"], W["L0_attn_b_u_qkv"],
                          W["L0_attn_b_sigma_qkv"], 3072, uid + "snq", s_k)
            emit_sn_scale(p, k, C, W["L0_attn_b_W_proj"], W["L0_attn_b_u_proj"],
                          W["L0_attn_b_sigma_proj"], 1024, uid + "snp", s_p)
            k.ts(s_q, s_k, 0.125)

        with ExitStack() as stk:
            wqkv = [stk.enter_context(nc.sbuf_tensor(f"{uid}_wqkv{i}", [128, 3072], BF16))
                    for i in range(DC)]
            if flavor == "l0b":
                wqkv0c = get_cast(p, k, C, W, "wqkv0c")
                for dc in range(DC):
                    p.dma("qs", wqkv[dc][:], wqkv0c[dc * 128:(dc + 1) * 128, :])
            elif flavor == "l1b":
                qkvT = get_prep(p, k, C, W, "qkvT1b")
                for dc in range(DC):
                    p.dma("qs", wqkv[dc][:], qkvT[dc * 128:(dc + 1) * 128, :])
            else:
                WqT = get_prep(p, k, C, W, "WqT")
                WkT = get_prep(p, k, C, W, "WkT")
                WvT = get_prep(p, k, C, W, "WvT")
                for dc in range(DC):
                    p.dma("qs", wqkv[dc][:, 0:1024], WqT[dc * 128:(dc + 1) * 128, :])
                    p.dma("qs", wqkv[dc][:, 1024:2048], WkT[dc * 128:(dc + 1) * 128, :])
                    p.dma("qs", wqkv[dc][:, 2048:3072], WvT[dc * 128:(dc + 1) * 128, :])
            bqc = bkc = bvb = None
            if flavor == "l1a":
                ps_c1 = stk.enter_context(nc.psum_tensor(f"{uid}_psc1", [128, 1], F32))
                bqc = load_vec_cols(p, k, C, W["L1_attn_a_bq"].rearrange("h d -> (h d)"),
                                    D, uid + "bq", stk, ps_c1)
                bkc = load_vec_cols(p, k, C, W["L1_attn_a_bk"].rearrange("h d -> (h d)"),
                                    D, uid + "bk", stk, ps_c1)
                bvr = stk.enter_context(nc.sbuf_tensor(f"{uid}_bvr", [1, D], F32))
                bvb = stk.enter_context(nc.sbuf_tensor(f"{uid}_bvb", [128, D], F32))
                ps_bb = stk.enter_context(nc.psum_tensor(f"{uid}_psbb", [128, D], F32))
                p.dma("qs", bvr[:], W["L1_attn_a_bv"].rearrange("h d -> (h d)").rearrange("(a n) -> a n", a=1))
                k.mm(ps_bb[:, 0:512], C.ones_row_f[:], bvr[:, 0:512])
                k.mm(ps_bb[:, 512:1024], C.ones_row_f[:], bvr[:, 512:1024])
                k.cp(bvb[:], ps_bb[:])

            xn_sb2 = [[stk.enter_context(nc.sbuf_tensor(f"{uid}_pxn{j}_{i}", [128, TT], BF16))
                       for i in range(DC)] for j in range(2)]
            qe2 = [stk.enter_context(nc.sbuf_tensor(f"{uid}_qe{j}", [128, TT], BF16)) for j in range(2)]
            ke2 = [stk.enter_context(nc.sbuf_tensor(f"{uid}_ke{j}", [128, TT], BF16)) for j in range(2)]
            ps_p = stk.enter_context(nc.psum_tensor(f"{uid}_psp", [128, 2 * TT], F32))
            ps_v = stk.enter_context(nc.psum_tensor(f"{uid}_psv", [128, 2 * TT], F32))
            for tt in range(NTT):
                xn_sb = xn_sb2[tt % 2]
                for dc in range(DC):
                    p.dma("qs", xn_sb[dc][:], xn_buf[dc * 128:(dc + 1) * 128, tt * TT:(tt + 1) * TT])
                for fc in range(16):
                    pp = ps_p[:, (fc % 2) * TT:(fc % 2) * TT + TT]
                    for dc in range(DC):
                        k.mm(pp, wqkv[dc][:, fc * 128:(fc + 1) * 128], xn_sb[dc][:],
                             start=(dc == 0), stop=(dc == DC - 1))
                    isq = fc < 8
                    dstb = (qe2 if isq else ke2)[(fc // 2) % 2]
                    if flavor == "l0b":
                        k.ts(dstb[:], pp, s_q if isq else s_k, op0=ALU.mult)
                    elif flavor == "l1a":
                        col = (bqc if isq else bkc)[:, fc % 8:fc % 8 + 1]
                        if isq:
                            k.ts(dstb[:], pp, col, 0.125, op0=ALU.add, op1=ALU.mult)
                        else:
                            k.ts(dstb[:], pp, col, op0=ALU.add)
                    else:
                        if isq:
                            k.ts(dstb[:], pp, 0.125, op0=ALU.mult)
                        else:
                            k.cp(dstb[:], pp)
                    buf = q_buf if isq else k_buf
                    p.dma("qa", buf[(fc % 8) * 128:(fc % 8) * 128 + 128, tt * TT:(tt + 1) * TT], dstb[:])
                for st in range(4):
                    ktile = tt * 4 + st
                    for vv in range(2):
                        pv = ps_v[:, vv * TT:(vv + 1) * TT]
                        for dc in range(DC):
                            k.mm(pv, xn_sb[dc][:, st * 128:(st + 1) * 128],
                                 wqkv[dc][:, 2048 + vv * 512:2048 + (vv + 1) * 512],
                                 start=(dc == 0), stop=(dc == DC - 1))
                        dst = vv4[:, ktile, vv * 8:(vv + 1) * 8, 0:64]
                        pvr = pv.rearrange("p (h j) -> p h j", j=64)
                        if flavor == "l0b":
                            k.ts(dst, pvr, s_k, op0=ALU.mult)
                        elif flavor == "l1a":
                            k.tt(dst, pvr, bvb[:, vv * 512:(vv + 1) * 512].rearrange(
                                "p (h j) -> p h j", j=64), ALU.add)
                        else:
                            k.cp(dst, pvr)
        p.barrier()

        with ExitStack() as stk:
            q2 = [stk.enter_context(nc.sbuf_tensor(f"{uid}_qh{j}", [128, S], BF16)) for j in range(2)]
            k2 = [stk.enter_context(nc.sbuf_tensor(f"{uid}_kh{j}", [128, S], BF16)) for j in range(2)]
            aA = [stk.enter_context(nc.sbuf_tensor(f"{uid}_aA{i}", [128, 2 * TT], BF16)) for i in range(2)]
            aB = [stk.enter_context(nc.sbuf_tensor(f"{uid}_aB{i}", [128, 2 * TT], BF16)) for i in range(2)]
            t1 = [stk.enter_context(nc.sbuf_tensor(f"{uid}_t1{j}", [64, TT], F32)) for j in range(2)]
            t2 = [stk.enter_context(nc.sbuf_tensor(f"{uid}_t2{j}", [64, TT], F32)) for j in range(2)]
            aoe = [stk.enter_context(nc.sbuf_tensor(f"{uid}_aoe{j}", [64, TT], BF16)) for j in range(2)]
            xrow = [stk.enter_context(nc.sbuf_tensor(f"{uid}_xrow{j}", [64, TT], F32)) for j in range(2)]
            lrow = [stk.enter_context(nc.sbuf_tensor(f"{uid}_lrow{j}", [1, 2 * TT], F32)) for j in range(2)]
            ps_sA = stk.enter_context(nc.psum_tensor(f"{uid}_psA", [128, 2 * TT], F32))
            ps_sB = stk.enter_context(nc.psum_tensor(f"{uid}_psB", [128, 2 * TT], F32))
            ps_o = stk.enter_context(nc.psum_tensor(f"{uid}_pso", [128, 2 * TT], F32))
            ps_r = stk.enter_context(nc.psum_tensor(f"{uid}_psr", [64, 2 * TT], F32))
            for hp in range(H // 2):
                hj = hp % 2
                p.dma("qs", q2[hj][:], q_buf[hp * 128:(hp + 1) * 128, :])
                p.dma("qs", k2[hj][:], k_buf[hp * 128:(hp + 1) * 128, :])
                qs0, qs1 = q2[hj][0:64, :], q2[hj][64:128, :]
                ks0, ks1 = k2[hj][0:64, :], k2[hj][64:128, :]
                for qt in range(NTT):
                    nkt = (qt + 1) * 4 if causal else 32
                    qsl = slice(qt * TT, (qt + 1) * TT)

                    def st_stage(ktp):
                        kt0, kt1 = 2 * ktp, 2 * ktp + 1
                        j = ktp % 2
                        k.mm(ps_sA[:, 0:TT], ks0[:, kt0 * 128:(kt0 + 1) * 128], qs0[:, qsl])
                        k.mm(ps_sA[:, TT:2 * TT], ks0[:, kt1 * 128:(kt1 + 1) * 128], qs0[:, qsl])
                        k.mm(ps_sB[:, 0:TT], ks1[:, kt0 * 128:(kt0 + 1) * 128], qs1[:, qsl])
                        k.mm(ps_sB[:, TT:2 * TT], ks1[:, kt1 * 128:(kt1 + 1) * 128], qs1[:, qsl])
                        k.act(aA[j][:], ps_sA[:], AF.Exp)
                        k.act(aB[j][:], ps_sB[:], AF.Exp)
                        if causal and kt1 >= 4 * qt:
                            dlt = kt0 - 4 * qt
                            msl = C.masks[:, dlt * TT:(dlt + 2) * TT]
                            k.tt(aA[j][:], aA[j][:], msl, ALU.mult)
                            k.tt(aB[j][:], aB[j][:], msl, ALU.mult, eng="gpsimd")

                    def av_stage(ktp, last):
                        kt0, kt1 = 2 * ktp, 2 * ktp + 1
                        j = ktp % 2
                        k.mm(ps_o[0:65, 0:TT], vv4[:, kt0, 2 * hp, :], aA[j][:, 0:TT],
                             start=(ktp == 0), stop=False)
                        k.mm(ps_o[0:65, 0:TT], vv4[:, kt1, 2 * hp, :], aA[j][:, TT:2 * TT],
                             start=False, stop=last)
                        k.mm(ps_o[0:65, TT:2 * TT], vv4[:, kt0, 2 * hp + 1, :], aB[j][:, 0:TT],
                             start=(ktp == 0), stop=False)
                        k.mm(ps_o[0:65, TT:2 * TT], vv4[:, kt1, 2 * hp + 1, :], aB[j][:, TT:2 * TT],
                             start=False, stop=last)

                    nktp = nkt // 2
                    st_stage(0)
                    for ktp in range(1, nktp):
                        st_stage(ktp)
                        av_stage(ktp - 1, False)
                    av_stage(nktp - 1, True)
                    for sub in range(2):
                        h = 2 * hp + sub
                        osl = ps_o[:, sub * TT:(sub + 1) * TT]
                        ps_rv = ps_r[:, sub * TT:(sub + 1) * TT]
                        l_ln = lrow[sub][:, 0:TT]; r_row = lrow[sub][:, TT:2 * TT]
                        k.act(l_ln, osl[64:65, :], AF.Ln)
                        k.act(r_row, l_ln, AF.Exp, scale=-1.0)
                        k.mm(ps_rv, C.ones_row_f[0:1, 0:64], r_row)
                        k.cp(t1[sub][:], osl[0:64, :])
                        if has_proj:
                            k.tt(aoe[sub][:], t1[sub][:], ps_rv, ALU.mult)
                            p.dma("qs", ao_buf[h * 64:(h + 1) * 64, qt * TT:(qt + 1) * TT],
                                  aoe[sub][:])
                        else:
                            k.tt(t2[sub][:], t1[sub][:], ps_rv, ALU.mult)
                            p.dma("qs", xrow[sub][:], xi[h * 64:(h + 1) * 64, qt * TT:(qt + 1) * TT])
                            k.tt(t2[sub][:], t2[sub][:], xrow[sub][:], ALU.add)
                            p.dma("qa", xo[h * 64:(h + 1) * 64, qt * TT:(qt + 1) * TT], t2[sub][:])
        p.barrier()

    if has_proj:
        with ExitStack() as stk:
            wp = [stk.enter_context(nc.sbuf_tensor(f"{uid}_wp{i}", [128, D], BF16))
                  for i in range(DC)]
            ps_c2 = stk.enter_context(nc.psum_tensor(f"{uid}_psc2", [128, 1], F32))
            bpc = None
            if flavor == "l0b":
                wproj0c = get_cast(p, k, C, W, "wproj0c")
                for fc in range(DC):
                    p.dma("qs", wp[fc][:], wproj0c[fc * 128:(fc + 1) * 128, :])
                spc = nc.alloc_sbuf_tensor(f"{uid}_spc", [128, 1], F32, side="right")
                k.cp(spc[:], s_p)
            else:
                projT = get_prep(p, k, C, W, "projT1b")
                for fc in range(DC):
                    p.dma("qs", wp[fc][:], projT[fc * 128:(fc + 1) * 128, :])
                bpc = load_vec_cols(p, k, C, W["L1_attn_b_b_proj"], D, uid + "bp", stk, ps_c2)
            ao_sb2 = [[stk.enter_context(nc.sbuf_tensor(f"{uid}_aos{j}_{i}", [128, TT], BF16))
                       for i in range(DC)] for j in range(2)]
            xr = [stk.enter_context(nc.sbuf_tensor(f"{uid}_pxr{i}", [128, TT], F32)) for i in range(2)]
            ob = [stk.enter_context(nc.sbuf_tensor(f"{uid}_pob{i}", [128, TT], F32)) for i in range(2)]
            ps_y = stk.enter_context(nc.psum_tensor(f"{uid}_psy", [128, 2 * TT], F32))
            for tt in range(NTT):
                ao_sb = ao_sb2[tt % 2]
                for fc in range(DC):
                    p.dma("qs", ao_sb[fc][:], ao_buf[fc * 128:(fc + 1) * 128, tt * TT:(tt + 1) * TT])
                for oc in range(DC):
                    py = ps_y[:, (oc % 2) * TT:(oc % 2) * TT + TT]
                    for fc in range(DC):
                        k.mm(py, wp[fc][:, oc * 128:(oc + 1) * 128], ao_sb[fc][:],
                             start=(fc == 0), stop=(fc == DC - 1))
                    xrb = xr[oc % 2]
                    p.dma("qs", xrb[:], xi[oc * 128:(oc + 1) * 128, tt * TT:(tt + 1) * TT])
                    obb = ob[oc % 2]
                    if flavor == "l0b":
                        k.stt(obb[:], py, spc[:], xrb[:], op0=ALU.mult, op1=ALU.add)
                    else:
                        k.stt(obb[:], py, bpc[:, oc:oc + 1], xrb[:], op0=ALU.add, op1=ALU.add)
                    p.dma("qa", xo[oc * 128:(oc + 1) * 128, tt * TT:(tt + 1) * TT], obb[:])
    p.barrier()


def emit_l0ffn(p, k, C, xi, xo, W, uid):
    from contextlib import ExitStack
    nc = p.nc
    E, G, INTER = 16, 4, 512
    IC = INTER // 128
    BIGN = -1.0e30
    xn_buf = nc.dram_tensor(f"{uid}_xn", [D, S], BF16, kind="Internal").ap()
    h_buf = nc.dram_tensor(f"{uid}_h", [E * INTER, S], BF16, kind="Internal").ap()
    zh_buf = nc.dram_tensor(f"{uid}_zh", [D, S], BF16, kind="Internal").ap()
    mr_buf = nc.dram_tensor(f"{uid}_mr", [2, S], F32, kind="Internal").ap()
    emit_ln_fm(p, k, C, xi, xn_buf, uid + "ln", mr_dst=mr_buf, sumx_fp32=True)

    comb_dram = nc.dram_tensor(f"{uid}_comb", [16, S], BF16, kind="Internal").ap()
    with ExitStack() as outer:

        with ExitStack() as stk:
            gw = stk.enter_context(nc.sbuf_tensor(f"{uid}_gw", [16, D], F32))
            gwT = [stk.enter_context(nc.sbuf_tensor(f"{uid}_gwT{i}", [128, 16], F32))
                   for i in range(DC)]
            gbb = stk.enter_context(nc.sbuf_tensor(f"{uid}_gbb", [128, 16], F32))
            csb = stk.enter_context(nc.sbuf_tensor(f"{uid}_csb0", [128, 16], F32))
            gbr = stk.enter_context(nc.sbuf_tensor(f"{uid}_gbr", [1, 16], F32))
            gcs = stk.enter_context(nc.sbuf_tensor(f"{uid}_gcs", [16, 1], F32))
            gcsr = stk.enter_context(nc.sbuf_tensor(f"{uid}_gcsr", [1, 16], F32))
            mrow = stk.enter_context(nc.sbuf_tensor(f"{uid}_mrow", [1, TT], F32))
            rrow = stk.enter_context(nc.sbuf_tensor(f"{uid}_rrow", [1, TT], F32))
            mrc = stk.enter_context(nc.sbuf_tensor(f"{uid}_mrc", [128, 8], F32))
            sc_t = stk.enter_context(nc.sbuf_tensor(f"{uid}_sct", [128, 16], F32))
            xn_sb2 = [[stk.enter_context(nc.sbuf_tensor(f"{uid}_g_xn{j}_{i}", [128, TT], F32))
                       for i in range(DC)] for j in range(2)]
            wt = stk.enter_context(nc.sbuf_tensor(f"{uid}_wt", [128, 24], F32))
            esc = stk.enter_context(nc.sbuf_tensor(f"{uid}_esc", [128, 16], F32))
            orig = stk.enter_context(nc.sbuf_tensor(f"{uid}_orig", [128, 16], F32))
            biased = stk.enter_context(nc.sbuf_tensor(f"{uid}_biased", [128, 16], F32))
            sgp = stk.enter_context(nc.sbuf_tensor(f"{uid}_sgp", [128, 16], F32))
            m3 = stk.enter_context(nc.sbuf_tensor(f"{uid}_m3", [128, 16], F32))
            me = stk.enter_context(nc.sbuf_tensor(f"{uid}_me", [128, 16], F32))
            cmb = stk.enter_context(nc.sbuf_tensor(f"{uid}_cmb", [128, 16], F32))
            gwork = stk.enter_context(nc.sbuf_tensor(f"{uid}_gwork", [128, 16], F32))
            ctmp = stk.enter_context(nc.sbuf_tensor(f"{uid}_ctmp", [16, 128], BF16))
            ps_g = stk.enter_context(nc.psum_tensor(f"{uid}_psg0", [128, 16], F32))
            ps_tr = stk.enter_context(nc.psum_tensor(f"{uid}_pstr0", [128, 128], F32))
            ps_b0 = stk.enter_context(nc.psum_tensor(f"{uid}_psb0", [128, 16], F32))
            c_m1 = wt[:, 0:1]; c_nm1 = wt[:, 1:2]; c_ss = wt[:, 2:3]; c_rs = wt[:, 3:4]
            c_G1 = wt[:, 4:5]; c_G2 = wt[:, 5:6]; c_T1 = wt[:, 6:7]; c_T2 = wt[:, 7:8]
            gm1 = wt[:, 8:12]; gs = wt[:, 12:16]; kp = wt[:, 16:20]; pen = wt[:, 20:24]

            p.dma("qs", gw[:], W["L0_ffn_gate_w"])
            p.dma("qs", gbr[:], W["L0_ffn_gate_b"].rearrange("(a n) -> a n", a=1))
            for dc in range(DC):
                k.trans(ps_tr[0:128, 0:16], gw[:, dc * 128:(dc + 1) * 128], C.ident[0:16, 0:16])
                k.cp(gwT[dc][:], ps_tr[0:128, 0:16])
            k.mm(ps_b0[:], C.ones_row_f[:], gbr[:])
            k.cp(gbb[:], ps_b0[:])
            k.red(gcs[:, 0:1], gw[:], ALU.add)
            k.trans(ps_tr[0:1, 0:16], gcs[0:16, 0:1], C.ident[0:16, 0:16])
            k.cp(gcsr[:], ps_tr[0:1, 0:16])
            k.mm(ps_b0[:], C.ones_row_f[:], gcsr[:])
            k.cp(csb[:], ps_b0[:])
            for tt in range(NTT):
                xn_sb = xn_sb2[tt % 2]
                for dc in range(DC):
                    p.dma("qs", xn_sb[dc][:], xi[dc * 128:(dc + 1) * 128, tt * TT:(tt + 1) * TT])
                p.dma("qa", mrow[:], mr_buf[0:1, tt * TT:(tt + 1) * TT])
                p.dma("qa", rrow[:], mr_buf[1:2, tt * TT:(tt + 1) * TT])
                for st in range(4):
                    k.trans(ps_tr[0:128, 0:1], mrow[0:1, st * 128:(st + 1) * 128], C.ident[0:1, 0:1])
                    k.cp(mrc[:, st:st + 1], ps_tr[0:128, 0:1])
                    k.trans(ps_tr[0:128, 0:1], rrow[0:1, st * 128:(st + 1) * 128], C.ident[0:1, 0:1])
                    k.cp(mrc[:, 4 + st:5 + st], ps_tr[0:128, 0:1])
                for st in range(4):
                    for dc in range(DC):
                        k.mm(ps_g[:], xn_sb[dc][:, st * 128:(st + 1) * 128], gwT[dc][:],
                             start=(dc == 0), stop=(dc == DC - 1))
                    k.ts(sc_t[:], csb[:], mrc[:, st:st + 1], op0=ALU.mult)
                    k.tt(sc_t[:], ps_g[:], sc_t[:], ALU.subtract)
                    k.ts(sc_t[:], sc_t[:], mrc[:, 4 + st:5 + st], op0=ALU.mult)
                    k.red(c_m1, sc_t[:], ALU.max)
                    k.ts(c_nm1, c_m1, -1.0)
                    k.act(esc[:], sc_t[:], AF.Exp, bias=c_nm1)
                    k.red(c_ss, esc[:], ALU.add)
                    k.act(c_rs, c_ss, AF.Ln)
                    k.act(c_rs, c_rs, AF.Exp, scale=-1.0)
                    k.ts(orig[:], esc[:], c_rs, op0=ALU.mult)
                    k.tt(biased[:], orig[:], gbb[:], ALU.add)
                    for g in range(G):
                        sl = biased[:, 4 * g:4 * g + 4]
                        k.red(gm1[:, g:g + 1], sl, ALU.max)
                        k.ts(me[:, 0:4], sl, gm1[:, g:g + 1], op0=ALU.is_equal)
                        k.stt(gwork[:, 0:4], me[:, 0:4], BIGN, sl, op0=ALU.mult, op1=ALU.add)
                        k.red(gs[:, g:g + 1], gwork[:, 0:4], ALU.max)
                        k.tt(gs[:, g:g + 1], gs[:, g:g + 1], gm1[:, g:g + 1], ALU.add)
                    k.red(c_G1, gs[:, 0:4], ALU.max)
                    k.ts(me[:, 0:4], gs[:, 0:4], c_G1, op0=ALU.is_equal)
                    k.stt(gwork[:, 0:4], me[:, 0:4], BIGN, gs[:, 0:4], op0=ALU.mult, op1=ALU.add)
                    k.red(c_G2, gwork[:, 0:4], ALU.max)
                    k.ts(kp[:, 0:4], gs[:, 0:4], c_G2, op0=ALU.is_ge)
                    k.ts(pen[:, 0:4], kp[:, 0:4], 1.0, -BIGN, op0=ALU.subtract, op1=ALU.mult)
                    for g in range(G):
                        k.ts(sgp[:, 4 * g:4 * g + 4], biased[:, 4 * g:4 * g + 4],
                             pen[:, g:g + 1], op0=ALU.add)
                    k.red(c_T1, sgp[:], ALU.max)
                    k.ts(me[:], sgp[:], c_T1, op0=ALU.is_equal)
                    k.stt(m3[:], me[:], BIGN, sgp[:], op0=ALU.mult, op1=ALU.add)
                    k.red(c_T2, m3[:], ALU.max)
                    k.ts(m3[:], m3[:], c_T2, op0=ALU.is_equal)
                    k.tt(me[:], me[:], m3[:], ALU.add)
                    k.tt(cmb[:], me[:], orig[:], ALU.mult)
                    k.trans(ps_tr[0:16, 0:128], cmb[:], C.ident[:])
                    k.cp(ctmp[:], ps_tr[0:16, 0:128])
                    p.dma("qs", comb_dram[:, (tt * 4 + st) * 128:(tt * 4 + st + 1) * 128],
                          ctmp[:])
        p.barrier()

        with ExitStack() as stk:
            ws1 = [stk.enter_context(nc.sbuf_tensor(f"{uid}_ws1_{i}", [128, D], BF16))
                   for i in range(DC)]
            ws3 = [stk.enter_context(nc.sbuf_tensor(f"{uid}_ws3_{i}", [128, D], BF16))
                   for i in range(DC)]
            ps_c1 = stk.enter_context(nc.psum_tensor(f"{uid}_zc1", [128, 1], F32))
            bs1c = load_vec_cols(p, k, C, W["L0_ffn_bs1"], D, uid + "b1", stk, ps_c1)
            bs3c = load_vec_cols(p, k, C, W["L0_ffn_bs3"], D, uid + "b3", stk, ps_c1)
            xn_sb2 = [[stk.enter_context(nc.sbuf_tensor(f"{uid}_z_xn{j}_{i}", [128, TT], BF16))
                       for i in range(DC)] for j in range(2)]
            sg1r = [stk.enter_context(nc.sbuf_tensor(f"{uid}_zsg{j}", [128, TT], BF16)) for j in range(2)]
            ztr = [stk.enter_context(nc.sbuf_tensor(f"{uid}_zt{j}", [128, TT], BF16)) for j in range(2)]
            ps_1r = [stk.enter_context(nc.psum_tensor(f"{uid}_zp1{j}", [128, TT], F32)) for j in range(2)]
            ps_3r = [stk.enter_context(nc.psum_tensor(f"{uid}_zp3{j}", [128, TT], F32)) for j in range(2)]
            ws1c = get_cast(p, k, C, W, "ws1c")
            ws3c = get_cast(p, k, C, W, "ws3c")
            for dc in range(DC):
                p.dma("qs", ws1[dc][:], ws1c[dc * 128:(dc + 1) * 128, :], group="ws13")
                p.dma("qs", ws3[dc][:], ws3c[dc * 128:(dc + 1) * 128, :], group="ws13")
            for tt in range(NTT):
                xn_sb = xn_sb2[tt % 2]
                for dc in range(DC):
                    p.dma("qs", xn_sb[dc][:], xn_buf[dc * 128:(dc + 1) * 128, tt * TT:(tt + 1) * TT])
                for ic in range(DC):
                    ij = ic % 2
                    ps_1, ps_3, sg1, zt = ps_1r[ij], ps_3r[ij], sg1r[ij], ztr[ij]
                    for dc in range(DC):
                        k.mm(ps_1[:], ws1[dc][:, ic * 128:(ic + 1) * 128], xn_sb[dc][:],
                             start=(dc == 0), stop=(dc == DC - 1))
                    k.act(sg1[:], ps_1[:], AF.Silu, bias=bs1c[:, ic:ic + 1])
                    for dc in range(DC):
                        k.mm(ps_3[:], ws3[dc][:, ic * 128:(ic + 1) * 128], xn_sb[dc][:],
                             start=(dc == 0), stop=(dc == DC - 1))
                    k.stt(zt[:], ps_3[:], bs3c[:, ic:ic + 1], sg1[:], op0=ALU.add, op1=ALU.mult)
                    p.dma("qa", zh_buf[ic * 128:(ic + 1) * 128, tt * TT:(tt + 1) * TT], zt[:])
        p.barrier()

        with ExitStack() as stk:
            xn_all = [stk.enter_context(nc.sbuf_tensor(f"{uid}_e_xn{i}", [128, S], BF16))
                      for i in range(DC)]
            for dc in range(DC):
                p.dma("qs", xn_all[dc][:], xn_buf[dc * 128:(dc + 1) * 128, :], group="xnall")
            ps_c1 = stk.enter_context(nc.psum_tensor(f"{uid}_ec1", [128, 1], F32))
            be1c = stk.enter_context(nc.sbuf_tensor(f"{uid}_be1c", [128, 4 * E], F32))
            be3c = stk.enter_context(nc.sbuf_tensor(f"{uid}_be3c", [128, 4 * E], F32))
            ber = stk.enter_context(nc.sbuf_tensor(f"{uid}_ber", [1, INTER], F32))
            for e in range(E):
                p.dma("qs", ber[:], W["L0_ffn_be1"][e:e + 1, :])
                for c in range(IC):
                    k.trans(ps_c1[:], ber[0:1, c * 128:(c + 1) * 128], C.ident[0:1, 0:1])
                    k.cp(be1c[:, e * IC + c:e * IC + c + 1], ps_c1[:])
                p.dma("qs", ber[:], W["L0_ffn_be3"][e:e + 1, :])
                for c in range(IC):
                    k.trans(ps_c1[:], ber[0:1, c * 128:(c + 1) * 128], C.ident[0:1, 0:1])
                    k.cp(be3c[:, e * IC + c:e * IC + c + 1], ps_c1[:])
            we1 = [[stk.enter_context(nc.sbuf_tensor(f"{uid}_we1_{j}_{i}", [128, INTER], BF16))
                    for i in range(DC)] for j in range(2)]
            we3 = [[stk.enter_context(nc.sbuf_tensor(f"{uid}_we3_{j}_{i}", [128, INTER], BF16))
                    for i in range(DC)] for j in range(2)]
            c_sb = [stk.enter_context(nc.sbuf_tensor(f"{uid}_ecsb{j}", [128, TT], F32)) for j in range(2)]
            crow = [stk.enter_context(nc.sbuf_tensor(f"{uid}_crow{j}", [1, S], BF16)) for j in range(2)]
            sg1r = [stk.enter_context(nc.sbuf_tensor(f"{uid}_esg{j}", [128, TT], BF16)) for j in range(2)]
            hhr = [stk.enter_context(nc.sbuf_tensor(f"{uid}_ehh{j}", [128, TT], F32)) for j in range(2)]
            hbtr = [stk.enter_context(nc.sbuf_tensor(f"{uid}_ehb{j}", [128, TT], BF16)) for j in range(2)]
            ps_1r = [stk.enter_context(nc.psum_tensor(f"{uid}_ep1{j}", [128, TT], F32)) for j in range(2)]
            ps_3r = [stk.enter_context(nc.psum_tensor(f"{uid}_ep3{j}", [128, TT], F32)) for j in range(2)]
            ps_c = [stk.enter_context(nc.psum_tensor(f"{uid}_epc{j}", [128, TT], F32)) for j in range(2)]
            We1c = get_cast(p, k, C, W, "We1c")
            We3c = get_cast(p, k, C, W, "We3c")
            for dc in range(DC):
                p.dma("qs", we1[0][dc][:], We1c[dc * 128:(dc + 1) * 128, :], group="we13a")
                p.dma("qs", we3[0][dc][:], We3c[dc * 128:(dc + 1) * 128, :], group="we13a")
            p.dma("qs", crow[0][:], comb_dram[0:1, :])
            for e in range(E):
                ej = e % 2
                if e + 1 < E:
                    p.dma("qs", crow[1 - ej][:], comb_dram[e + 1:e + 2, :])
                if e + 1 < E:
                    for dc in range(DC):
                        p.dma("qs", we1[1 - ej][dc][:],
                              We1c[(e + 1) * 1024 + dc * 128:(e + 1) * 1024 + (dc + 1) * 128, :],
                              group="we13" + ("a" if (1 - ej) == 0 else "b"))
                        p.dma("qs", we3[1 - ej][dc][:],
                              We3c[(e + 1) * 1024 + dc * 128:(e + 1) * 1024 + (dc + 1) * 128, :],
                              group="we13" + ("a" if (1 - ej) == 0 else "b"))
                for tt in range(NTT):
                    tj = tt % 2
                    k.mm(ps_c[tj][:], C.ones_row_b[:], crow[ej][0:1, tt * TT:(tt + 1) * TT])
                    k.cp(c_sb[tj][:], ps_c[tj][:])
                    for ic in range(IC):
                        ij = ic % 2
                        ps_1, ps_3, sg1, hh, hbt = ps_1r[ij], ps_3r[ij], sg1r[ij], hhr[ij], hbtr[ij]
                        for dc in range(DC):
                            k.mm(ps_1[:], we1[ej][dc][:, ic * 128:(ic + 1) * 128],
                                 xn_all[dc][:, tt * TT:(tt + 1) * TT],
                                 start=(dc == 0), stop=(dc == DC - 1))
                        k.act(sg1[:], ps_1[:], AF.Silu, bias=be1c[:, e * IC + ic:e * IC + ic + 1])
                        for dc in range(DC):
                            k.mm(ps_3[:], we3[ej][dc][:, ic * 128:(ic + 1) * 128],
                                 xn_all[dc][:, tt * TT:(tt + 1) * TT],
                                 start=(dc == 0), stop=(dc == DC - 1))
                        k.stt(hh[:], ps_3[:], be3c[:, e * IC + ic:e * IC + ic + 1], sg1[:],
                              op0=ALU.add, op1=ALU.mult)
                        k.tt(hbt[:], hh[:], c_sb[tj][:], ALU.mult, eng="gpsimd")
                        p.dma("qa", h_buf[e * INTER + ic * 128:e * INTER + (ic + 1) * 128,
                                          tt * TT:(tt + 1) * TT], hbt[:])
        p.barrier()

        with ExitStack() as stk:
            we2 = [stk.enter_context(nc.sbuf_tensor(f"{uid}_we2_{i}", [128, D], BF16))
                   for i in range(E * IC)]
            ws2 = [stk.enter_context(nc.sbuf_tensor(f"{uid}_ws2_{i}", [128, D], BF16))
                   for i in range(DC)]
            be2f = stk.enter_context(nc.sbuf_tensor(f"{uid}_be2f", [16, D], F32))
            be2 = stk.enter_context(nc.sbuf_tensor(f"{uid}_be2", [16, D], BF16))
            ps_y = [stk.enter_context(nc.psum_tensor(f"{uid}_psy{i}", [128, 2 * TT], F32))
                    for i in range(4)]
            bs2c = load_vec_cols(p, k, C, W["L0_ffn_bs2"], D, uid + "b2", stk, ps_y[0][:, 0:1])
            We2c = get_cast(p, k, C, W, "We2c")
            ws2c = get_cast(p, k, C, W, "ws2c")
            for e in range(E):
                for ic in range(IC):
                    p.dma("qs", we2[e * IC + ic][:],
                          We2c[e * 512 + ic * 128:e * 512 + (ic + 1) * 128, :], group="we2")
            for dc in range(DC):
                p.dma("qs", ws2[dc][:], ws2c[dc * 128:(dc + 1) * 128, :], group="ws2")
            p.dma("qs", be2f[:], W["L0_ffn_be2"])
            k.cp(be2[:], be2f[:])
            combT = stk.enter_context(nc.sbuf_tensor(f"{uid}_combT", [16, S], BF16))
            p.dma("qs", combT[:], comb_dram)
            hring = [stk.enter_context(nc.sbuf_tensor(f"{uid}_hr{i}", [128, TT], BF16))
                     for i in range(4)]
            zring = [stk.enter_context(nc.sbuf_tensor(f"{uid}_zr{i}", [128, TT], BF16))
                     for i in range(2)]
            xr = [stk.enter_context(nc.sbuf_tensor(f"{uid}_yxr{i}", [128, TT], F32)) for i in range(2)]
            ob = [stk.enter_context(nc.sbuf_tensor(f"{uid}_yob{i}", [128, TT], F32)) for i in range(2)]

            def py(oc):
                return ps_y[oc // 2][:, (oc % 2) * TT:(oc % 2) * TT + TT]
            for tt in range(NTT):
                first = True
                for e in range(E):
                    for ic in range(IC):
                        hb = hring[(e * IC + ic) % 4]
                        p.dma("qs", hb[:], h_buf[e * INTER + ic * 128:e * INTER + (ic + 1) * 128,
                                                 tt * TT:(tt + 1) * TT])
                        for oc in range(DC):
                            k.mm(py(oc), we2[e * IC + ic][:, oc * 128:(oc + 1) * 128], hb[:],
                                 start=first, stop=False)
                        first = False
                for ic2 in range(DC):
                    zb = zring[ic2 % 2]
                    p.dma("qs", zb[:], zh_buf[ic2 * 128:(ic2 + 1) * 128, tt * TT:(tt + 1) * TT])
                    for oc in range(DC):
                        k.mm(py(oc), ws2[ic2][:, oc * 128:(oc + 1) * 128], zb[:],
                             start=False, stop=False)
                for oc in range(DC):
                    k.mm(py(oc), be2[:, oc * 128:(oc + 1) * 128],
                         combT[:, tt * TT:(tt + 1) * TT], start=False, stop=True)
                for oc in range(DC):
                    xrb = xr[oc % 2]
                    p.dma("qs", xrb[:], xi[oc * 128:(oc + 1) * 128, tt * TT:(tt + 1) * TT])
                    obb = ob[oc % 2]
                    k.stt(obb[:], py(oc), bs2c[:, oc:oc + 1], xrb[:], op0=ALU.add, op1=ALU.add)
                    p.dma("qa", xo[oc * 128:(oc + 1) * 128, tt * TT:(tt + 1) * TT], obb[:])
    p.barrier()


PREP_SPECS = {
    "wgT": ("L1_ffn_w_gate", 3072, 1024, I32, None),
    "wuT": ("L1_ffn_w_up", 3072, 1024, I32, None),
    "wdT": ("L1_ffn_w_down", 1024, 3072, I32, None),
    "qkvT1b": ("L1_attn_b_W_qkv", 3072, 1024, F32, None),
    "projT1b": ("L1_attn_b_W_proj", 1024, 1024, F32, None),
    "WqT": ("L1_attn_a_Wq", 1024, 1024, F32, None),
    "WkT": ("L1_attn_a_Wk", 1024, 1024, F32, None),
    "WvT": ("L1_attn_a_Wv", 1024, 1024, F32, None),
    "wvT": ("L0_attn_a_w_qkv", 1024, 1024, F32, (2048, 3072)),
    "woT": ("L0_attn_a_w_o", 1024, 1024, F32, None),
}

def emit_prep_cast(p, k, C, src, dst, rows, cols, uid, B):
    n = 2048
    total = rows * cols
    assert total % (128 * n) == 0, (rows, cols)
    nb = total // (128 * n)
    sf = src.rearrange("r c -> (r c)").rearrange("(a p n) -> a p n", p=128, n=n)
    df = dst.rearrange("r c -> (r c)").rearrange("(a p n) -> a p n", p=128, n=n)
    for b in range(nb):
        sl, sb = B.slab_f[b % 3], B.slab_b[b % 3]
        p.dma("qs", sl[:, 0:n], sf[b])
        k.cp(sb[:, 0:n], sl[:, 0:n])
        p.dma("qa", df[b], sb[:, 0:n])


CAST_SPECS = {
    "We1c": ("L0_ffn_We1", 16 * 1024, 512),
    "We3c": ("L0_ffn_We3", 16 * 1024, 512),
    "We2c": ("L0_ffn_We2", 16 * 512, 1024),
    "ws1c": ("L0_ffn_ws1", 1024, 1024),
    "ws2c": ("L0_ffn_ws2", 1024, 1024),
    "ws3c": ("L0_ffn_ws3", 1024, 1024),
    "wqkv0c": ("L0_attn_b_W_qkv", 1024, 3072),
    "wproj0c": ("L0_attn_b_W_proj", 1024, 1024),
}

CASTS_FOR_BLOCK = {1: ["We1c", "We3c", "We2c", "ws1c", "ws2c", "ws3c"],
                   2: ["wqkv0c", "wproj0c"],
                   3: ["We1c", "We3c", "We2c", "ws1c", "ws2c", "ws3c"]}


def get_cast(p, k, C, W, key, B=None):
    if not hasattr(p, "cast_cache"):
        p.cast_cache = {}
    if key in p.cast_cache:
        return p.cast_cache[key]
    assert B is not None, f"cast {key} must be emitted in the prologue"
    name, rows, cols = CAST_SPECS[key]
    src = W[name]
    if len(INPUT_SPECS[name][0]) == 3:
        src = src.rearrange("h d e -> (h d) e")
    dst = p.nc.dram_tensor("cast_" + key, [rows, cols], BF16, kind="Internal").ap()
    emit_prep_cast(p, k, C, src, dst, rows, cols, "cc" + key, B)
    p.cast_cache[key] = dst
    return dst


PREPS_FOR_BLOCK = {0: ["wvT", "woT"], 4: ["WqT", "WkT", "WvT"],
                   5: ["wgT", "wuT", "wdT"], 6: ["qkvT1b", "projT1b"],
                   7: ["wgT", "wuT", "wdT"]}


def get_prep(p, k, C, W, key, B=None):
    if not hasattr(p, "prep_cache"):
        p.prep_cache = {}
    if key in p.prep_cache:
        return p.prep_cache[key]
    assert B is not None, f"prep {key} must be emitted in the prologue"
    name, rows, cols, dt, rslice = PREP_SPECS[key]
    src = W[name]
    if len(INPUT_SPECS[name][0]) == 3:
        src = src.rearrange("h d e -> (h d) e")
    if rslice is not None:
        src = src[rslice[0]:rslice[1], :]
    dst = p.nc.dram_tensor("prep_" + key, [cols, rows], BF16, kind="Internal").ap()
    emit_prep_transpose(p, k, C, src, dst, rows, cols, dt, "pp" + key, B)
    p.prep_cache[key] = dst
    return dst


def blk1(p, k, C, xi, xo, W, uid):
    emit_block1(p, k, C, xi, xo, W, uid)


def blk_l1ffn(p, k, C, xi, xo, W, uid, last=False):
    preps = {kk: get_prep(p, k, C, W, kk) for kk in ("wgT", "wuT", "wdT")}
    emit_l1ffn(p, k, C, xi, xo, W, uid, preps, last=last)


def _nyi(*a, **kw):
    raise NotImplementedError


L1FFN_INPUTS = ["L1_ffn_w_gate", "L1_ffn_w_up", "L1_ffn_w_down"]
L0FFN_INPUTS = ["L0_ffn_gate_w", "L0_ffn_gate_b", "L0_ffn_We1", "L0_ffn_be1",
                "L0_ffn_We2", "L0_ffn_be2", "L0_ffn_We3", "L0_ffn_be3",
                "L0_ffn_ws1", "L0_ffn_bs1", "L0_ffn_ws2", "L0_ffn_bs2",
                "L0_ffn_ws3", "L0_ffn_bs3"]

BLOCK_DEFS = [
    ("b1", blk1, ["L0_attn_a_w_qkv", "L0_attn_a_w_o"]),
    ("b2", lambda p, k, C, xi, xo, W, uid: emit_l0ffn(p, k, C, xi, xo, W, uid), L0FFN_INPUTS),
    ("b3", lambda p, k, C, xi, xo, W, uid: emit_attention(p, k, C, xi, xo, W, uid, "l0b"),
     ["L0_attn_b_W_qkv", "L0_attn_b_u_qkv", "L0_attn_b_sigma_qkv",
      "L0_attn_b_W_proj", "L0_attn_b_u_proj", "L0_attn_b_sigma_proj"]),
    ("b4", lambda p, k, C, xi, xo, W, uid: emit_l0ffn(p, k, C, xi, xo, W, uid), L0FFN_INPUTS),
    ("b5", lambda p, k, C, xi, xo, W, uid: emit_attention(p, k, C, xi, xo, W, uid, "l1a"),
     ["L1_attn_a_Wq", "L1_attn_a_bq", "L1_attn_a_Wk", "L1_attn_a_bk",
      "L1_attn_a_Wv", "L1_attn_a_bv"]),
    ("b6", lambda p, k, C, xi, xo, W, uid: blk_l1ffn(p, k, C, xi, xo, W, uid, last=False),
     L1FFN_INPUTS),
    ("b7", lambda p, k, C, xi, xo, W, uid: emit_attention(p, k, C, xi, xo, W, uid, "l1b"),
     ["L1_attn_b_W_qkv", "L1_attn_b_W_proj", "L1_attn_b_b_proj"]),
    ("b8", lambda p, k, C, xi, xo, W, uid: blk_l1ffn(p, k, C, xi, xo, W, uid, last=True),
     L1FFN_INPUTS),
]


def kernel(**inputs):
    n = 8
    x_full = np.ascontiguousarray(np.asarray(inputs["L0_attn_a_x"], dtype=np.float32))
    groups = [list(range(len(BLOCK_DEFS)))]
    cur = None
    out = None
    for g in groups:
        nc, needed = build_program(g)
        in_maps = []
        for c in range(n):
            m = {}
            for name in needed:
                if name == "L0_attn_a_x":
                    m[name] = x_full[c]
                else:
                    m[name] = np.asarray(inputs[name])
            if g[0] != 0:
                m["x_fm_in"] = cur[c]
            in_maps.append(m)
        res = run_bass_kernel_spmd(nc, in_maps, core_ids=list(range(n)))
        if g[-1] == 7:
            out = np.stack([res.results[c]["y"] for c in range(n)], axis=0)
        else:
            cur = [res.results[c]["x_fm_out"] for c in range(n)]
    return out


# version 16
# speedup vs baseline: 1.4413x; 1.0031x over previous
import numpy as np
import concourse.bass as bass
import concourse.mybir as mybir
from concourse.bass_utils import run_bass_kernel_spmd

F32 = mybir.dt.float32
BF16 = mybir.dt.bfloat16
I32 = mybir.dt.int32
AF = mybir.ActivationFunctionType
ALU = mybir.AluOpType
AX = mybir.AxisListType

S = 4096
D = 1024
DC = D // 128
TT = 512
NTT = S // TT
EPS = 1e-5

ENGINES = ("sync", "scalar", "vector", "tensor", "gpsimd")
QUEUE_ENG = {"qs": "sync", "qg": "gpsimd", "qa": "scalar"}
N_DMA_SEM_POOL = 12


def _free_span(ap):
    dims = ap.ap
    part_stride = dims[0][0]
    f0 = ap.offset % part_stride if part_stride > 0 else ap.offset
    span = 1
    for step, count in dims[1:]:
        span += abs(step) * (count - 1)
    return (f0, f0 + span)


def _part_span(ap):
    dims = ap.ap
    part_stride = dims[0][0]
    p0 = ap.offset // part_stride if part_stride > 0 else 0
    return (p0, p0 + dims[0][1])


def _flat_span(ap):
    span = 1
    for step, count in ap.ap:
        span += abs(step) * (count - 1)
    return (ap.offset, ap.offset + span)


def _region(ap):
    t = ap.tensor
    name = t.name
    kind = type(t).__name__
    if kind == "SBTensorHandle":
        p = _part_span(ap); f = _free_span(ap)
        return (name, p[0], p[1], f[0], f[1])
    if kind == "PSumTensorHandle":
        f = _free_span(ap)
        esz = mybir.dt.size(ap.dtype)
        bank = 2048 // esz
        return (name, 0, 128, (f[0] // bank) * bank,
                ((f[1] + bank - 1) // bank) * bank)
    flat = _flat_span(ap)
    return (name, 0, 1, flat[0], flat[1])


def _overlap(r1, r2):
    return (r1[0] == r2[0] and r1[1] < r2[2] and r2[1] < r1[2]
            and r1[3] < r2[4] and r2[3] < r1[4])


class Op:
    __slots__ = ("idx", "eng", "emit", "deps", "inc_needed", "sem", "val", "is_dma", "dbg")

    def __init__(self, idx, eng, emit, is_dma, dbg):
        self.idx = idx
        self.eng = eng
        self.emit = emit
        self.deps = []
        self.inc_needed = is_dma
        self.sem = None
        self.val = None
        self.is_dma = is_dma
        self.dbg = dbg


class P:

    def __init__(self):
        self.nc = bass.Bass("TRN2", target_bir_lowering=False)
        self.ops = []
        self.writers = {}
        self.readers = {}
        self.semh = {}
        self.dma_cnt = {}
        self.phase_idx = {q: 0 for q in QUEUE_ENG}
        self.dbg_tag = ""
        self.last_op = {}

    def barrier(self):
        self._dma_sem_map = {}
        sem_deps = [("sem", s, c * 16) for s, c in self.dma_cnt.items()]
        lasts = dict(self.last_op)
        for eng in ENGINES:
            deps = [("op", o) for e2, o in lasts.items() if e2 != eng and not o.is_dma]
            o = Op(len(self.ops), eng, lambda e: e.nop(), False, "barrier")
            o.deps = deps + list(sem_deps)
            for kind, *rest in deps:
                rest[0].inc_needed = True
            self.ops.append(o)
            self.last_op[eng] = o

    def sem(self, name):
        if name not in self.semh:
            self.semh[name] = self.nc.alloc_semaphore(name=name)
        return self.semh[name]

    def _record(self, op, r, w):
        regs_r = [_region(a) for a in r]
        regs_w = [_region(a) for a in w]
        deps = {}
        for reg in regs_r:
            for oreg, oop in self.writers.get(reg[0], ()):
                if _overlap(reg, oreg):
                    deps[oop.idx] = oop
        for reg in regs_w:
            for oreg, oop in self.writers.get(reg[0], ()):
                if _overlap(reg, oreg):
                    deps[oop.idx] = oop
            for oreg, oop in self.readers.get(reg[0], ()):
                if _overlap(reg, oreg):
                    deps[oop.idx] = oop
        for dep in deps.values():
            if dep is op:
                continue
            if dep.is_dma:
                op.deps.append(("sem", dep.sem, self.dma_cnt[dep.sem] * 16))
            else:
                if dep.eng == op.eng == "tensor" and not op.is_dma:
                    continue
                dep.inc_needed = True
                op.deps.append(("op", dep))
        for reg in regs_r:
            self.readers.setdefault(reg[0], []).append((reg, op))
        for reg in regs_w:
            wl = self.writers.setdefault(reg[0], [])
            wl[:] = [(orr, oo) for (orr, oo) in wl
                     if not (orr[1] >= reg[1] and orr[2] <= reg[2]
                             and orr[3] >= reg[3] and orr[4] <= reg[4])]
            wl.append((reg, op))
            rl = self.readers.get(reg[0])
            if rl:
                rl[:] = [(orr, oo) for (orr, oo) in rl if not _overlap(reg, orr)]

    def op(self, eng, emit, r=(), w=(), dbg=""):
        o = Op(len(self.ops), eng, emit, False, dbg or self.dbg_tag)
        self.ops.append(o)
        self._record(o, r, w)
        self.last_op[eng] = o
        return o

    def _dma_sem_for(self, out, in_):
        tname = None
        for ap in (out, in_):
            if type(ap.tensor).__name__ == "SBTensorHandle":
                tname = ap.tensor.name
                break
        if tname is None:
            tname = "@dram"
        return tname

    def dma(self, queue, out, in_, group=None, **kw):
        swdge = queue == "qg"
        key = (group or self._dma_sem_for(out, in_), swdge)
        if not hasattr(self, "_dma_sem_map"):
            self._dma_sem_map = {}
            self._sem_rr = {False: 0, True: 0}
        if key not in self._dma_sem_map:
            if swdge:
                self._dma_sem_map[key] = "dg%d" % (self._sem_rr[True] % 32)
            else:
                self._dma_sem_map[key] = "dh%d" % (self._sem_rr[False] % 64)
            self._sem_rr[swdge] += 1
        semname = self._dma_sem_map[key]
        self.sem(semname)

        def emit(e, out=out, in_=in_, kw=kw):
            return e.dma_start(out=out, in_=in_, **kw)
        o = Op(len(self.ops), QUEUE_ENG[queue], emit, True, self.dbg_tag)
        o.sem = semname
        self.ops.append(o)
        self._record(o, [in_], [out])
        self.dma_cnt[semname] = self.dma_cnt.get(semname, 0) + 1
        o.val = self.dma_cnt[semname] * 16
        self.last_op[o.eng] = o
        return o

    def phase(self, queue=None):
        for q in ([queue] if queue else list(QUEUE_ENG)):
            self.phase_idx[q] += 1

    def build(self):
        nc = self.nc
        eng_cnt = {e: 0 for e in ENGINES}
        th = {e: [] for e in ENGINES}
        watermark = {e: {} for e in ENGINES}
        n_waits = 0
        for o in self.ops:
            if not o.is_dma and o.inc_needed:
                eng_cnt[o.eng] += 1
                o.sem = "e_" + o.eng
                o.val = eng_cnt[o.eng]
                self.sem(o.sem)
                if o.eng == "gpsimd" and o.val > 1:
                    o.deps.append(("sem", o.sem, o.val - 1))
            for d in o.deps:
                if d[0] == "sem":
                    semname, val = d[1], d[2]
                else:
                    dep = d[1]
                    semname, val = dep.sem, dep.val
                    assert val is not None, (o.dbg, dep.dbg)
                wm = watermark[o.eng]
                if wm.get(semname, 0) >= val:
                    continue
                wm[semname] = val
                h = self.semh[semname]
                th[o.eng].append(lambda e, h=h, v=val: e.wait_ge(h, v))
                n_waits += 1
            if o.sem is not None and (o.is_dma or o.inc_needed):
                h = self.semh[o.sem]
                th[o.eng].append(
                    lambda e, o=o, h=h: o.emit(e).then_inc(h, 16 if o.is_dma else 1))
            else:
                th[o.eng].append(o.emit)
        for semname, cnt in self.dma_cnt.items():
            h = self.semh[semname]
            th["sync"].append(lambda e, h=h, v=cnt * 16: e.wait_ge(h, v))
        self._n_waits = n_waits
        with nc.Block() as block:
            for eng in ENGINES:
                lst = th[eng]
                if lst:
                    getattr(block, eng)(lambda e, lst=lst: [f(e) for f in lst])
        return nc


def is_ap(x):
    return hasattr(x, "tensor")


class K:

    def __init__(self, p):
        self.p = p
        self._consts = {}

    def constcol(self, val, dtype=F32):
        key = (float(val), dtype)
        if key not in self._consts:
            t = self.p.nc.alloc_sbuf_tensor(f"c_k{len(self._consts)}", [128, 1], dtype, side="right")
            self.memset(t[:], float(val))
            self._consts[key] = t
        return self._consts[key]

    def mm(self, out, lhsT, rhs, start=True, stop=True):
        return self.p.op("tensor",
                         lambda e, out=out, lhsT=lhsT, rhs=rhs, start=start, stop=stop:
                         e.matmul(out, lhsT, rhs, start=start, stop=stop, skip_group_check=True),
                         r=[lhsT, rhs], w=[out])

    def trans(self, out, in_, ident):
        return self.p.op("tensor",
                         lambda e, out=out, in_=in_, ident=ident: e.transpose(out, in_, ident),
                         r=[in_, ident], w=[out])

    def act(self, out, in_, func, bias=0.0, scale=1.0, accum=None):
        np0 = _part_span(out)
        if not is_ap(bias) and float(bias) != 0.0:
            bias = self.constcol(bias)[np0[0]:np0[1], :]
        if not is_ap(scale) and float(scale) != 1.0:
            scale = self.constcol(scale)[np0[0]:np0[1], :]
        r = [in_] + [x for x in (bias, scale) if is_ap(x)]
        w = [out] + ([accum] if accum is not None else [])
        kw = {}
        if accum is not None:
            kw["accum_out"] = accum
        return self.p.op("scalar",
                         lambda e, out=out, in_=in_, func=func, bias=bias, scale=scale, kw=kw:
                         e.activation(out, in_, func, bias=bias, scale=scale, **kw),
                         r=r, w=w)

    def ts(self, out, in0, s1, s2=None, op0=ALU.mult, op1=ALU.bypass, eng="vector"):
        r = [in0] + [x for x in (s1, s2) if is_ap(x)]
        if s2 is None:
            return self.p.op(eng,
                             lambda e, out=out, in0=in0, s1=s1, op0=op0:
                             e.tensor_scalar(out, in0, s1, None, op0=op0),
                             r=r, w=[out])
        return self.p.op(eng,
                         lambda e, out=out, in0=in0, s1=s1, s2=s2, op0=op0, op1=op1:
                         e.tensor_scalar(out, in0, s1, s2, op0=op0, op1=op1),
                         r=r, w=[out])

    def tt(self, out, in0, in1, op, eng="vector"):
        return self.p.op(eng,
                         lambda e, out=out, in0=in0, in1=in1, op=op:
                         e.tensor_tensor(out, in0, in1, op=op),
                         r=[in0, in1], w=[out])

    def stt(self, out, in0, s, in1, op0, op1, eng="vector"):
        r = [in0, in1] + ([s] if is_ap(s) else [])
        return self.p.op(eng,
                         lambda e, out=out, in0=in0, s=s, in1=in1, op0=op0, op1=op1:
                         e.scalar_tensor_tensor(out, in0, s, in1, op0=op0, op1=op1),
                         r=r, w=[out])

    def red(self, out, in_, op, eng="vector"):
        return self.p.op(eng,
                         lambda e, out=out, in_=in_, op=op:
                         e.tensor_reduce(out, in_, axis=AX.X, op=op),
                         r=[in_], w=[out])

    def cp(self, out, in_, eng="vector"):
        return self.p.op(eng,
                         lambda e, out=out, in_=in_: e.tensor_copy(out, in_),
                         r=[in_], w=[out])

    def memset(self, ap, c, eng="vector"):
        return self.p.op(eng, lambda e, ap=ap, c=c: e.memset(ap, c), r=[], w=[ap])

    def iota(self, ap, pattern, base=0, chan=0):
        return self.p.op("gpsimd",
                         lambda e, ap=ap, pattern=pattern, base=base, chan=chan:
                         e.iota(ap, pattern=pattern, base=base, channel_multiplier=chan,
                                allow_small_or_imprecise_dtypes=True),
                         r=[], w=[ap])


class Consts:
    def __init__(self, p, k):
        nc = p.nc
        self.ident = nc.alloc_sbuf_tensor("c_ident", [128, 128], F32, side="right")
        self.ones_col_b = nc.alloc_sbuf_tensor("c_ones_col_b", [128, 1], BF16, side="right")
        self.ones_col_f = nc.alloc_sbuf_tensor("c_ones_col_f", [128, 1], F32, side="right")
        self.ones_row_f = nc.alloc_sbuf_tensor("c_ones_row_f", [1, 128], F32, side="right")
        self.ones_row_b = nc.alloc_sbuf_tensor("c_ones_row_b", [1, 128], BF16, side="right")
        self.jrow = nc.alloc_sbuf_tensor("c_jrow", [128, TT], F32, side="right")
        self.pcol = nc.alloc_sbuf_tensor("c_pcol", [128, 1], F32, side="right")
        self.masks = nc.alloc_sbuf_tensor("c_masks", [128, 4 * TT], BF16, side="right")

        k.iota(self.jrow[:], [[1, TT]], 0, 0)
        k.iota(self.pcol[:], [[0, 1]], 0, 1)
        k.ts(self.ident[:], self.jrow[:, 0:128], self.pcol[:], op0=ALU.is_equal)
        k.memset(self.ones_col_b[:], 1.0)
        k.memset(self.ones_col_f[:], 1.0)
        k.memset(self.ones_row_f[:], 1.0)
        k.memset(self.ones_row_b[:], 1.0)
        for d in range(4):
            m = self.masks[:, d * TT:(d + 1) * TT]
            k.ts(m, self.jrow[:], self.pcol[:], float(128 * d),
                 op0=ALU.subtract, op1=ALU.subtract)
            k.ts(m, m, 0.0, op0=ALU.is_ge)


class PrologueBufs:

    def __init__(self, p, st):
        nc = p.nc
        self.slab_f = [st.enter_context(nc.sbuf_tensor(f"pl_sf{i}", [128, 3072], F32))
                       for i in range(3)]
        self.slab_i = [st.enter_context(nc.sbuf_tensor(f"pl_si{i}", [128, 3072], I32))
                       for i in range(3)]
        self.slab_c = [st.enter_context(nc.sbuf_tensor(f"pl_sc{i}", [128, 3072], F32))
                       for i in range(1)]
        self.slab_b = [st.enter_context(nc.sbuf_tensor(f"pl_sb{i}", [128, 3072], BF16))
                       for i in range(3)]
        self.pst = [st.enter_context(nc.psum_tensor(f"pl_ps{i}", [128, 128], F32))
                    for i in range(2)]
        self.band = [st.enter_context(nc.sbuf_tensor(f"pl_band{i}", [128, 24 * 512], BF16))
                     for i in range(1)]


def emit_prep_transpose(p, k, C, src, dst, rows, cols, src_dtype, uid, B):
    nb = rows // 128
    ncol = cols // 128
    slab = B.slab_f if src_dtype == F32 else B.slab_i
    for b in range(nb):
        sl = slab[b % 3]
        p.dma("qs", sl[:, 0:cols], src[b * 128:(b + 1) * 128, :])
        if src_dtype != F32:
            k.cp(B.slab_c[0][:, 0:cols], sl[:, 0:cols])
            sf = B.slab_c[0]
        else:
            sf = sl
        g = 0
        for c in range(ncol):
            k.trans(B.pst[c % 2][:], sf[:, c * 128:(c + 1) * 128], C.ident[:])
            k.cp(B.band[g][:, c * 512 + (b % 4) * 128:c * 512 + (b % 4) * 128 + 128],
                 B.pst[c % 2][:])
        if b % 4 == 3 or b == nb - 1:
            b0 = (b // 4) * 4
            w = (b - b0 + 1) * 128
            for c in range(ncol):
                p.dma("qa", dst[c * 128:(c + 1) * 128, b0 * 128:b0 * 128 + w],
                      B.band[g][:, c * 512:c * 512 + w])


def emit_ln_fm(p, k, C, x_src, xn_dst, uid, mr_dst=None, sumx_fp32=False):
    from contextlib import ExitStack
    nc = p.nc
    with ExitStack() as stk:
        xt = [[stk.enter_context(nc.sbuf_tensor(f"{uid}_x{j}_{i}", [128, TT], F32))
               for i in range(DC)] for j in range(2)]
        xn = [[stk.enter_context(nc.sbuf_tensor(f"{uid}_xn{j}_{i}", [128, TT], BF16))
               for i in range(DC)] for j in range(2)]
        xb = [stk.enter_context(nc.sbuf_tensor(f"{uid}_xb{i}", [128, TT], BF16)) for i in range(DC)]
        sq = [stk.enter_context(nc.sbuf_tensor(f"{uid}_sq{i}", [128, TT], BF16)) for i in range(DC)]
        sqf = [stk.enter_context(nc.sbuf_tensor(f"{uid}_sqf{i}", [128, TT], F32)) for i in range(DC)]             if sumx_fp32 else None
        rows2 = [stk.enter_context(nc.sbuf_tensor(f"{uid}_rows{j}", [1, 6 * TT], F32)) for j in range(2)]
        tmp = [stk.enter_context(nc.sbuf_tensor(f"{uid}_tmp{j}", [128, TT], F32)) for j in range(2)]
        ps_st = [stk.enter_context(nc.psum_tensor(f"{uid}_ps_st{j}", [1, 2 * TT], F32)) for j in range(2)]
        ps_b = [stk.enter_context(nc.psum_tensor(f"{uid}_ps_b{j}", [128, 2 * TT], F32)) for j in range(2)]

        for tt in range(NTT):
            j = tt % 2
            rows = rows2[j]
            r_m = rows[:, 0:TT]; r_ms = rows[:, TT:2 * TT]; r_var = rows[:, 2 * TT:3 * TT]
            r_ln = rows[:, 3 * TT:4 * TT]; r_r = rows[:, 4 * TT:5 * TT]; r_mr = rows[:, 5 * TT:6 * TT]
            for dc in range(DC):
                p.dma("qs", xt[j][dc][:], x_src[dc * 128:(dc + 1) * 128, tt * TT:(tt + 1) * TT])
            for dc in range(DC):
                if sumx_fp32:
                    k.act(sqf[dc][:], xt[j][dc][:], AF.Square)
                else:
                    k.act(sq[dc][:], xt[j][dc][:], AF.Square)
                k.cp(xb[dc][:], xt[j][dc][:], eng="gpsimd")
            for dc in range(DC):
                if sumx_fp32:
                    k.mm(ps_st[j][:, 0:TT], C.ones_col_f[:], xt[j][dc][:], start=(dc == 0), stop=(dc == DC - 1))
                else:
                    k.mm(ps_st[j][:, 0:TT], C.ones_col_b[:], xb[dc][:], start=(dc == 0), stop=(dc == DC - 1))
            for dc in range(DC):
                if sumx_fp32:
                    k.mm(ps_st[j][:, TT:2 * TT], C.ones_col_f[:], sqf[dc][:], start=(dc == 0), stop=(dc == DC - 1))
                else:
                    k.mm(ps_st[j][:, TT:2 * TT], C.ones_col_b[:], sq[dc][:], start=(dc == 0), stop=(dc == DC - 1))
            k.ts(r_m, ps_st[j][:, 0:TT], 1.0 / D)
            k.ts(r_ms, ps_st[j][:, TT:2 * TT], 1.0 / D)
            k.tt(r_var, r_m, r_m, ALU.mult)
            k.tt(r_var, r_ms, r_var, ALU.subtract)
            k.act(r_ln, r_var, AF.Ln, bias=float(EPS))
            k.act(r_r, r_ln, AF.Exp, scale=-0.5)
            k.tt(r_mr, r_m, r_r, ALU.mult)
            k.mm(ps_b[j][:, 0:TT], C.ones_row_f[:], r_r)
            k.mm(ps_b[j][:, TT:2 * TT], C.ones_row_f[:], r_mr)
            if mr_dst is not None:
                p.dma("qa", mr_dst[0:1, tt * TT:(tt + 1) * TT], r_m)
                p.dma("qa", mr_dst[1:2, tt * TT:(tt + 1) * TT], r_r)
            for dc in range(DC):
                k.tt(tmp[j][:], xt[j][dc][:], ps_b[j][:, 0:TT], ALU.mult)
                k.tt(xn[j][dc][:], tmp[j][:], ps_b[j][:, TT:2 * TT], ALU.subtract)
                p.dma("qa", xn_dst[dc * 128:(dc + 1) * 128, tt * TT:(tt + 1) * TT], xn[j][dc][:])
    p.barrier()


def emit_block1(p, k, C, xin, xout, W, uid="b1"):
    from contextlib import ExitStack
    nc = p.nc
    wvT = get_prep(p, k, C, W, "wvT")
    woT = get_prep(p, k, C, W, "woT")

    with ExitStack() as stk:
        ND = 4
        xt = [stk.enter_context(nc.sbuf_tensor(f"{uid}_x{i}", [128, D], F32)) for i in range(ND)]
        xnb4 = [stk.enter_context(nc.sbuf_tensor(f"{uid}_xnb{i}", [128, D], BF16)) for i in range(ND)]
        sqd4 = [stk.enter_context(nc.sbuf_tensor(f"{uid}_sqd{i}", [128, D], BF16)) for i in range(ND)]
        cols4 = [stk.enter_context(nc.sbuf_tensor(f"{uid}_cols{i}", [128, 8], F32)) for i in range(ND)]
        xfb = [stk.enter_context(nc.sbuf_tensor(f"{uid}_xfb{i}", [128, 128], F32)) for i in range(4)]
        ps_cs = stk.enter_context(nc.psum_tensor(f"{uid}_ps_cs", [1, D], F32))
        ps_tr = stk.enter_context(nc.psum_tensor(f"{uid}_ps_tr", [128, 128], F32))
        ps_mv = stk.enter_context(nc.psum_tensor(f"{uid}_ps_mv", [1, D], F32))
        ps_c1 = stk.enter_context(nc.psum_tensor(f"{uid}_ps_c1", [128, 1], F32))

        for i in range(S // 128):
            x0 = xt[i % ND]
            xnb = xnb4[i % ND]
            sqd = sqd4[i % ND]
            cols = cols4[i % ND]
            s_col = cols[:, 0:1]; m_col = cols[:, 1:2]; acc = cols[:, 2:3]; v1 = cols[:, 3:4]
            m2 = cols[:, 4:5]; var = cols[:, 5:6]; rstd = cols[:, 6:7]; nmr = cols[:, 7:8]
            p.dma("qs", x0[:], xin[i * 128:(i + 1) * 128, :])
            k.red(s_col, x0[:], ALU.add)
            k.ts(m_col, s_col, 1.0 / D)
            k.act(sqd[:], x0[:], AF.Square, accum=acc)
            k.ts(v1, acc, 1.0 / D)
            k.tt(m2, m_col, m_col, ALU.mult)
            k.tt(var, v1, m2, ALU.subtract)
            k.act(var, var, AF.Ln, bias=float(EPS))
            k.act(rstd, var, AF.Exp, scale=-0.5)
            k.tt(nmr, m_col, rstd, ALU.mult)
            k.ts(nmr, nmr, -1.0)
            k.act(xnb[:], x0[:], AF.Identity, bias=nmr, scale=rstd)
            for c in range(8):
                k.trans(ps_tr[:], x0[:, c * 128:(c + 1) * 128], C.ident[:])
                f = xfb[c % 4]
                k.cp(f[:], ps_tr[:])
                p.dma("qs", xout[c * 128:(c + 1) * 128, i * 128:(i + 1) * 128], f[:])
            if i > 0:
                xp = xnb4[(i - 1) % ND]
                k.mm(ps_cs[:, 0:512], C.ones_col_b[:], xp[:, 0:512],
                     start=(i == 1), stop=False)
                k.mm(ps_cs[:, 512:1024], C.ones_col_b[:], xp[:, 512:1024],
                     start=(i == 1), stop=False)
            if i == S // 128 - 1:
                k.mm(ps_cs[:, 0:512], C.ones_col_b[:], xnb[:, 0:512],
                     start=False, stop=True)
                k.mm(ps_cs[:, 512:1024], C.ones_col_b[:], xnb[:, 512:1024],
                     start=False, stop=True)

        rows = stk.enter_context(nc.sbuf_tensor(f"{uid}_rows", [1, 3 * D], F32))
        cs_row = rows[:, 0:D]; vs_row = rows[:, D:2 * D]; or_row = rows[:, 2 * D:3 * D]
        ccols = stk.enter_context(nc.sbuf_tensor(f"{uid}_ccols", [128, 16], BF16))
        cs_colb = ccols[:, 0:8]; vs_colb = ccols[:, 8:16]
        or_col = stk.enter_context(nc.sbuf_tensor(f"{uid}_orcol", [128, 8], F32))
        wrow = [stk.enter_context(nc.sbuf_tensor(f"{uid}_w{i}", [128, D], BF16)) for i in range(2)]

        k.cp(cs_row, ps_cs[:])
        for c in range(8):
            k.trans(ps_c1[:], cs_row[0:1, c * 128:(c + 1) * 128], C.ident[0:1, 0:1])
            k.cp(cs_colb[:, c:c + 1], ps_c1[:])
        for dc in range(8):
            wb = wrow[dc % 2]
            p.dma("qs", wb[:], wvT[dc * 128:(dc + 1) * 128, :])
            k.mm(ps_mv[:, 0:512], cs_colb[:, dc:dc + 1], wb[:, 0:512],
                 start=(dc == 0), stop=(dc == 7))
            k.mm(ps_mv[:, 512:1024], cs_colb[:, dc:dc + 1], wb[:, 512:1024],
                 start=(dc == 0), stop=(dc == 7))
        k.cp(vs_row, ps_mv[:])
        for c in range(8):
            k.trans(ps_c1[:], vs_row[0:1, c * 128:(c + 1) * 128], C.ident[0:1, 0:1])
            k.cp(vs_colb[:, c:c + 1], ps_c1[:])
        for dc in range(8):
            wb = wrow[dc % 2]
            p.dma("qs", wb[:], woT[dc * 128:(dc + 1) * 128, :])
            k.mm(ps_mv[:, 0:512], vs_colb[:, dc:dc + 1], wb[:, 0:512],
                 start=(dc == 0), stop=(dc == 7))
            k.mm(ps_mv[:, 512:1024], vs_colb[:, dc:dc + 1], wb[:, 512:1024],
                 start=(dc == 0), stop=(dc == 7))
        k.cp(or_row, ps_mv[:])
        for c in range(8):
            k.trans(ps_c1[:], or_row[0:1, c * 128:(c + 1) * 128], C.ident[0:1, 0:1])
            k.cp(or_col[:, c:c + 1], ps_c1[:])

        rt = [stk.enter_context(nc.sbuf_tensor(f"{uid}_rt{i}", [128, TT], F32)) for i in range(2)]
        for dc in range(DC):
            for tt in range(NTT):
                bbuf = rt[tt % 2]
                p.dma("qs", bbuf[:], xout[dc * 128:(dc + 1) * 128, tt * TT:(tt + 1) * TT])
                k.ts(bbuf[:], bbuf[:], or_col[:, dc:dc + 1], op0=ALU.add)
                p.dma("qs", xout[dc * 128:(dc + 1) * 128, tt * TT:(tt + 1) * TT], bbuf[:])
    p.barrier()


def emit_l1ffn(p, k, C, xin, xout, W, uid, preps, last=False):
    from contextlib import ExitStack
    nc = p.nc
    wgT, wuT, wdT = preps["wgT"], preps["wuT"], preps["wdT"]
    xn_buf = nc.dram_tensor(f"{uid}_xn", [D, S], BF16, kind="Internal").ap()
    emit_ln_fm(p, k, C, xin, xn_buf, uid + "ln")
    HB = 3072
    HC = HB // 128
    INV = 1.0 / 127.0
    with ExitStack() as stk:
        wg_sb = [stk.enter_context(nc.sbuf_tensor(f"{uid}_wg{i}", [128, HB], BF16)) for i in range(DC)]
        wu_sb = [stk.enter_context(nc.sbuf_tensor(f"{uid}_wu{i}", [128, HB], BF16)) for i in range(DC)]
        wd_sb = [stk.enter_context(nc.sbuf_tensor(f"{uid}_wd{i}", [128, D], BF16)) for i in range(HC)]
        for dc in range(DC):
            p.dma("qs", wg_sb[dc][:], wgT[dc * 128:(dc + 1) * 128, :], group="wgu")
            p.dma("qs", wu_sb[dc][:], wuT[dc * 128:(dc + 1) * 128, :], group="wug")
        for hc in range(HC):
            p.dma("qs", wd_sb[hc][:], wdT[hc * 128:(hc + 1) * 128, :], group="wd")
        xn_sb = [[stk.enter_context(nc.sbuf_tensor(f"{uid}_xn{j}_{i}", [128, TT], BF16))
                  for i in range(DC)] for j in range(2)]
        sg = [stk.enter_context(nc.sbuf_tensor(f"{uid}_sg{j}", [128, TT], BF16)) for j in range(2)]
        h_sb = [stk.enter_context(nc.sbuf_tensor(f"{uid}_h{i}", [128, TT], BF16)) for i in range(HC)]
        xo = [stk.enter_context(nc.sbuf_tensor(f"{uid}_xo{i}", [128, TT], F32)) for i in range(2)]
        xr = [stk.enter_context(nc.sbuf_tensor(f"{uid}_xr{i}", [128, TT], F32)) for i in range(2)]
        yt = [stk.enter_context(nc.sbuf_tensor(f"{uid}_yt{i}", [128, 128], F32)) for i in range(2)]
        ps_gu = [stk.enter_context(nc.psum_tensor(f"{uid}_psgu{j}", [128, TT], F32)) for j in range(2)]
        ps_y = [stk.enter_context(nc.psum_tensor(f"{uid}_psy{i}", [128, 1024], F32)) for i in range(2)]
        ps_tr = [stk.enter_context(nc.psum_tensor(f"{uid}_pst{i}", [128, 128], F32)) for i in range(2)]             if last else None

        def py(oc):
            return ps_y[(oc % 4) // 2][:, (oc % 2) * TT:(oc % 2) * TT + TT]

        def evac(oc, tt):
            xrb = xr[oc % 2]
            p.dma("qs", xrb[:], xin[oc * 128:(oc + 1) * 128, tt * TT:(tt + 1) * TT])
            ob = xo[oc % 2]
            k.stt(ob[:], py(oc), INV, xrb[:], op0=ALU.mult, op1=ALU.add)
            if not last:
                p.dma("qa", xout[oc * 128:(oc + 1) * 128, tt * TT:(tt + 1) * TT], ob[:])
            else:
                for q in range(TT // 128):
                    k.trans(ps_tr[q % 2][:], ob[:, q * 128:(q + 1) * 128], C.ident[:])
                    k.cp(yt[q % 2][:], ps_tr[q % 2][:])
                    t0 = tt * TT + q * 128
                    p.dma("qa", xout[t0:t0 + 128, oc * 128:(oc + 1) * 128], yt[q % 2][:])

        for tt in range(NTT):
            j = tt % 2
            for dc in range(DC):
                p.dma("qs", xn_sb[j][dc][:],
                      xn_buf[dc * 128:(dc + 1) * 128, tt * TT:(tt + 1) * TT])
            for hc in range(HC):
                hj = hc % 2
                gu = ps_gu[hj]
                for dc in range(DC):
                    k.mm(gu[:], wg_sb[dc][:, hc * 128:(hc + 1) * 128], xn_sb[j][dc][:],
                         start=(dc == 0), stop=(dc == DC - 1))
                k.act(sg[hj][:], gu[:], AF.Silu, scale=INV)
                for dc in range(DC):
                    k.mm(gu[:], wu_sb[dc][:, hc * 128:(hc + 1) * 128], xn_sb[j][dc][:],
                         start=(dc == 0), stop=(dc == DC - 1))
                k.stt(h_sb[hc][:], gu[:], INV, sg[hj][:], op0=ALU.mult, op1=ALU.mult)
                for oc in range(4):
                    k.mm(py(oc), wd_sb[hc][:, oc * 128:(oc + 1) * 128], h_sb[hc][:],
                         start=(hc == 0), stop=(hc == HC - 1))
            for oc in range(4):
                evac(oc, tt)
            for oc in range(4, DC):
                for hc in range(HC):
                    k.mm(py(oc), wd_sb[hc][:, oc * 128:(oc + 1) * 128], h_sb[hc][:],
                         start=(hc == 0), stop=(hc == HC - 1))
            for oc in range(4, DC):
                evac(oc, tt)
    p.barrier()


INPUT_SPECS = {
    "L0_attn_a_x": ((8, 4096, 1024), F32),
    "L0_attn_a_w_qkv": ((3072, 1024), F32),
    "L0_attn_a_w_o": ((1024, 1024), F32),
    "L0_attn_b_W_qkv": ((1024, 3072), F32),
    "L0_attn_b_u_qkv": ((3072,), F32),
    "L0_attn_b_sigma_qkv": ((1,), F32),
    "L0_attn_b_W_proj": ((1024, 1024), F32),
    "L0_attn_b_u_proj": ((1024,), F32),
    "L0_attn_b_sigma_proj": ((1,), F32),
    "L0_ffn_gate_w": ((16, 1024), F32),
    "L0_ffn_gate_b": ((16,), F32),
    "L0_ffn_We1": ((16, 1024, 512), F32),
    "L0_ffn_be1": ((16, 512), F32),
    "L0_ffn_We2": ((16, 512, 1024), F32),
    "L0_ffn_be2": ((16, 1024), F32),
    "L0_ffn_We3": ((16, 1024, 512), F32),
    "L0_ffn_be3": ((16, 512), F32),
    "L0_ffn_ws1": ((1024, 1024), F32),
    "L0_ffn_bs1": ((1024,), F32),
    "L0_ffn_ws2": ((1024, 1024), F32),
    "L0_ffn_bs2": ((1024,), F32),
    "L0_ffn_ws3": ((1024, 1024), F32),
    "L0_ffn_bs3": ((1024,), F32),
    "L1_attn_a_Wq": ((16, 64, 1024), F32),
    "L1_attn_a_bq": ((16, 64), F32),
    "L1_attn_a_Wk": ((16, 64, 1024), F32),
    "L1_attn_a_bk": ((16, 64), F32),
    "L1_attn_a_Wv": ((16, 64, 1024), F32),
    "L1_attn_a_bv": ((16, 64), F32),
    "L1_attn_b_W_qkv": ((3072, 1024), F32),
    "L1_attn_b_W_proj": ((1024, 1024), F32),
    "L1_attn_b_b_proj": ((1024,), F32),
    "L1_ffn_w_gate": ((3072, 1024), I32),
    "L1_ffn_w_up": ((3072, 1024), I32),
    "L1_ffn_w_down": ((1024, 3072), I32),
}


def build_program(block_ids):
    p = P()
    k = K(p)
    C = Consts(p, k)
    nc = p.nc
    needed = set()
    defs = BLOCK_DEFS
    if block_ids[0] == 0:
        xin = nc.dram_tensor("L0_attn_a_x", [S, D], F32, kind="ExternalInput").ap()
        needed.add("L0_attn_a_x")
    else:
        xin = nc.dram_tensor("x_fm_in", [D, S], F32, kind="ExternalInput").ap()
    n_interior = len(block_ids) - 1
    interiors = [nc.dram_tensor(f"xi{i}", [D, S], F32, kind="Internal").ap()
                 for i in range(n_interior)]
    if block_ids[-1] == 7:
        xout_last = nc.dram_tensor("y", [S, D], F32, kind="ExternalOutput").ap()
    else:
        xout_last = nc.dram_tensor("x_fm_out", [D, S], F32, kind="ExternalOutput").ap()
    W = {}
    for bi in block_ids:
        for name in defs[bi][2]:
            if name not in W:
                shape, dt = INPUT_SPECS[name]
                W[name] = nc.dram_tensor(name, list(shape), dt, kind="ExternalInput").ap()
                needed.add(name)
    from contextlib import ExitStack
    with ExitStack() as plst:
        B = PrologueBufs(p, plst)
        if 0 in block_ids:
            for key in PREPS_FOR_BLOCK.get(0, []):
                get_prep(p, k, C, W, key, B)
        if block_ids and block_ids[0] == 0:
            xo0 = interiors[0] if n_interior > 0 else xout_last
            defs[0][1](p, k, C, xin, xo0, W, uid="B0")
        for bi in block_ids:
            for key in PREPS_FOR_BLOCK.get(bi, []):
                get_prep(p, k, C, W, key, B)
        for bi in block_ids:
            for key in CASTS_FOR_BLOCK.get(bi, []):
                get_cast(p, k, C, W, key, B)
    p.barrier()
    for j, bi in enumerate(block_ids):
        if bi == 0:
            continue
        xo = interiors[j] if j < n_interior else xout_last
        xi = xin if j == 0 else interiors[j - 1]
        defs[bi][1](p, k, C, xi, xo, W, uid=f"B{bi}")
        p.barrier()
    return p.build(), sorted(needed)


def load_vec_cols(p, k, C, src_vec, n, uid, stk, ps_c1):
    nc = p.nc
    row = stk.enter_context(nc.sbuf_tensor(f"{uid}_vr", [1, n], F32))
    col = stk.enter_context(nc.sbuf_tensor(f"{uid}_vc", [128, n // 128], F32))
    p.dma("qs", row[:], src_vec.rearrange("(a n) -> a n", a=1))
    for c in range(n // 128):
        k.trans(ps_c1[:], row[0:1, c * 128:(c + 1) * 128], C.ident[0:1, 0:1])
        k.cp(col[:, c:c + 1], ps_c1[:])
    return col


def emit_sn_scale(p, k, C, W_ap, u_ap, sigma_ap, n_out, uid, s_col):
    from contextlib import ExitStack
    nc = p.nc
    NP = n_out // 512
    with ExitStack() as stk:
        wch = [stk.enter_context(nc.sbuf_tensor(f"{uid}_w{i}", [128, n_out], F32)) for i in range(2)]
        urow = stk.enter_context(nc.sbuf_tensor(f"{uid}_u", [1, n_out], F32))
        ub = stk.enter_context(nc.sbuf_tensor(f"{uid}_ub", [128, n_out], F32))
        tmp = stk.enter_context(nc.sbuf_tensor(f"{uid}_t", [128, n_out], F32))
        vcol = stk.enter_context(nc.sbuf_tensor(f"{uid}_vcol", [128, 8], F32))
        sq8 = stk.enter_context(nc.sbuf_tensor(f"{uid}_sq8", [128, 8], F32))
        vs2 = stk.enter_context(nc.sbuf_tensor(f"{uid}_vs2", [128, 1], F32))
        u2row = stk.enter_context(nc.sbuf_tensor(f"{uid}_u2", [1, n_out], F32))
        sqrow = stk.enter_context(nc.sbuf_tensor(f"{uid}_sqr", [1, n_out], F32))
        sc = stk.enter_context(nc.sbuf_tensor(f"{uid}_sc", [1, 8], F32))
        nv2 = sc[:, 0:1]; nu2 = sc[:, 1:2]; lnv = sc[:, 2:3]; lnu = sc[:, 3:4]
        dif = sc[:, 4:5]; s0 = sc[:, 5:6]; sig = sc[:, 6:7]; s1 = sc[:, 7:8]
        ps_b = stk.enter_context(nc.psum_tensor(f"{uid}_psb", [128, 512], F32))
        ps_u2 = stk.enter_context(nc.psum_tensor(f"{uid}_psu", [1, n_out], F32))
        ps_s = ps_b[0:1, 0:1]
        ps_c = ps_b[:, 0:1]

        p.dma("qs", urow[:], u_ap.rearrange("(a n) -> a n", a=1))
        p.dma("qs", sig[0:1, :], sigma_ap.rearrange("(a n) -> a n", a=1))
        for pc in range(NP):
            k.mm(ps_b[:], C.ones_row_f[:], urow[:, pc * 512:(pc + 1) * 512])
            k.cp(ub[:, pc * 512:(pc + 1) * 512], ps_b[:])
        for rc in range(8):
            w = wch[rc % 2]
            p.dma("qs", w[:], W_ap[rc * 128:(rc + 1) * 128, :])
            k.tt(tmp[:], w[:], ub[:], ALU.mult)
            k.red(vcol[:, rc:rc + 1], tmp[:], ALU.add)
            for pc in range(NP):
                k.mm(ps_u2[:, pc * 512:(pc + 1) * 512], vcol[:, rc:rc + 1],
                     w[:, pc * 512:(pc + 1) * 512], start=(rc == 0), stop=(rc == 7))
        k.act(sq8[:], vcol[:], AF.Square, accum=vs2[:])
        k.mm(ps_s[:], vs2[:], C.ones_col_f[:])
        k.cp(nv2[0:1, :], ps_s[:])
        k.cp(u2row[:], ps_u2[:])
        k.act(sqrow[:], u2row[:], AF.Square, accum=nu2[0:1, :])
        k.act(lnv[0:1, :], nv2[0:1, :], AF.Ln)
        k.act(lnu[0:1, :], nu2[0:1, :], AF.Ln)
        k.tt(dif[0:1, :], lnv[0:1, :], lnu[0:1, :], ALU.subtract)
        k.act(s0[0:1, :], dif[0:1, :], AF.Exp, scale=self_half(k))
        k.tt(s1[0:1, :], s0[0:1, :], sig[0:1, :], ALU.mult)
        k.mm(ps_c[:], C.ones_row_f[:], s1[0:1, :])
        k.cp(s_col, ps_c[:])
    p.barrier()


def self_half(k):
    return 0.5


def emit_attention(p, k, C, xi, xo, W, uid, flavor):
    from contextlib import ExitStack
    nc = p.nc
    causal = flavor in ("l0b", "l1a")
    has_proj = flavor in ("l0b", "l1b")
    H, HD = 16, 64
    xn_buf = nc.dram_tensor(f"{uid}_xn", [D, S], BF16, kind="Internal").ap()
    q_buf = nc.dram_tensor(f"{uid}_q", [D, S], BF16, kind="Internal").ap()
    k_buf = nc.dram_tensor(f"{uid}_k", [D, S], BF16, kind="Internal").ap()
    ao_buf = nc.dram_tensor(f"{uid}_ao", [D, S], BF16, kind="Internal").ap() if has_proj else None
    emit_ln_fm(p, k, C, xi, xn_buf, uid + "ln")

    with ExitStack() as outer:
        v_aug = outer.enter_context(nc.sbuf_tensor(f"{uid}_vaug", [128, 32 * H * 65], BF16))
        vv4 = v_aug.ap().rearrange("p (t h j) -> p t h j", h=H, j=65)
        k.memset(vv4[:, :, :, 64:65], 1.0)
        scol = outer.enter_context(nc.sbuf_tensor(f"{uid}_scol", [128, 4], F32))
        s_q = scol[:, 0:1]; s_k = scol[:, 1:2]; s_p = scol[:, 2:3]

        if flavor == "l0b":
            emit_sn_scale(p, k, C, W["L0_attn_b_W_qkv"], W["L0_attn_b_u_qkv"],
                          W["L0_attn_b_sigma_qkv"], 3072, uid + "snq", s_k)
            emit_sn_scale(p, k, C, W["L0_attn_b_W_proj"], W["L0_attn_b_u_proj"],
                          W["L0_attn_b_sigma_proj"], 1024, uid + "snp", s_p)
            k.ts(s_q, s_k, 0.125)

        with ExitStack() as stk:
            wqkv = [stk.enter_context(nc.sbuf_tensor(f"{uid}_wqkv{i}", [128, 3072], BF16))
                    for i in range(DC)]
            if flavor == "l0b":
                wqkv0c = get_cast(p, k, C, W, "wqkv0c")
                for dc in range(DC):
                    p.dma("qs", wqkv[dc][:], wqkv0c[dc * 128:(dc + 1) * 128, :])
            elif flavor == "l1b":
                qkvT = get_prep(p, k, C, W, "qkvT1b")
                for dc in range(DC):
                    p.dma("qs", wqkv[dc][:], qkvT[dc * 128:(dc + 1) * 128, :])
            else:
                WqT = get_prep(p, k, C, W, "WqT")
                WkT = get_prep(p, k, C, W, "WkT")
                WvT = get_prep(p, k, C, W, "WvT")
                for dc in range(DC):
                    p.dma("qs", wqkv[dc][:, 0:1024], WqT[dc * 128:(dc + 1) * 128, :])
                    p.dma("qs", wqkv[dc][:, 1024:2048], WkT[dc * 128:(dc + 1) * 128, :])
                    p.dma("qs", wqkv[dc][:, 2048:3072], WvT[dc * 128:(dc + 1) * 128, :])
            bqc = bkc = bvb = None
            if flavor == "l1a":
                ps_c1 = stk.enter_context(nc.psum_tensor(f"{uid}_psc1", [128, 1], F32))
                bqc = load_vec_cols(p, k, C, W["L1_attn_a_bq"].rearrange("h d -> (h d)"),
                                    D, uid + "bq", stk, ps_c1)
                bkc = load_vec_cols(p, k, C, W["L1_attn_a_bk"].rearrange("h d -> (h d)"),
                                    D, uid + "bk", stk, ps_c1)
                bvr = stk.enter_context(nc.sbuf_tensor(f"{uid}_bvr", [1, D], F32))
                bvb = stk.enter_context(nc.sbuf_tensor(f"{uid}_bvb", [128, D], F32))
                ps_bb = stk.enter_context(nc.psum_tensor(f"{uid}_psbb", [128, D], F32))
                p.dma("qs", bvr[:], W["L1_attn_a_bv"].rearrange("h d -> (h d)").rearrange("(a n) -> a n", a=1))
                k.mm(ps_bb[:, 0:512], C.ones_row_f[:], bvr[:, 0:512])
                k.mm(ps_bb[:, 512:1024], C.ones_row_f[:], bvr[:, 512:1024])
                k.cp(bvb[:], ps_bb[:])

            xn_sb2 = [[stk.enter_context(nc.sbuf_tensor(f"{uid}_pxn{j}_{i}", [128, TT], BF16))
                       for i in range(DC)] for j in range(2)]
            qe2 = [stk.enter_context(nc.sbuf_tensor(f"{uid}_qe{j}", [128, TT], BF16)) for j in range(2)]
            ke2 = [stk.enter_context(nc.sbuf_tensor(f"{uid}_ke{j}", [128, TT], BF16)) for j in range(2)]
            ps_p = stk.enter_context(nc.psum_tensor(f"{uid}_psp", [128, 2 * TT], F32))
            ps_v = stk.enter_context(nc.psum_tensor(f"{uid}_psv", [128, 2 * TT], F32))
            for tt in range(NTT):
                xn_sb = xn_sb2[tt % 2]
                for dc in range(DC):
                    p.dma("qs", xn_sb[dc][:], xn_buf[dc * 128:(dc + 1) * 128, tt * TT:(tt + 1) * TT])
                for fc in range(16):
                    pp = ps_p[:, (fc % 2) * TT:(fc % 2) * TT + TT]
                    for dc in range(DC):
                        k.mm(pp, wqkv[dc][:, fc * 128:(fc + 1) * 128], xn_sb[dc][:],
                             start=(dc == 0), stop=(dc == DC - 1))
                    isq = fc < 8
                    dstb = (qe2 if isq else ke2)[(fc // 2) % 2]
                    if flavor == "l0b":
                        k.ts(dstb[:], pp, s_q if isq else s_k, op0=ALU.mult)
                    elif flavor == "l1a":
                        col = (bqc if isq else bkc)[:, fc % 8:fc % 8 + 1]
                        if isq:
                            k.ts(dstb[:], pp, col, 0.125, op0=ALU.add, op1=ALU.mult)
                        else:
                            k.ts(dstb[:], pp, col, op0=ALU.add)
                    else:
                        if isq:
                            k.ts(dstb[:], pp, 0.125, op0=ALU.mult)
                        else:
                            k.cp(dstb[:], pp)
                    buf = q_buf if isq else k_buf
                    p.dma("qa", buf[(fc % 8) * 128:(fc % 8) * 128 + 128, tt * TT:(tt + 1) * TT], dstb[:])
                for st in range(4):
                    ktile = tt * 4 + st
                    for vv in range(2):
                        pv = ps_v[:, vv * TT:(vv + 1) * TT]
                        for dc in range(DC):
                            k.mm(pv, xn_sb[dc][:, st * 128:(st + 1) * 128],
                                 wqkv[dc][:, 2048 + vv * 512:2048 + (vv + 1) * 512],
                                 start=(dc == 0), stop=(dc == DC - 1))
                        dst = vv4[:, ktile, vv * 8:(vv + 1) * 8, 0:64]
                        pvr = pv.rearrange("p (h j) -> p h j", j=64)
                        if flavor == "l0b":
                            k.ts(dst, pvr, s_k, op0=ALU.mult)
                        elif flavor == "l1a":
                            k.tt(dst, pvr, bvb[:, vv * 512:(vv + 1) * 512].rearrange(
                                "p (h j) -> p h j", j=64), ALU.add)
                        else:
                            k.cp(dst, pvr)
        p.barrier()

        with ExitStack() as stk:
            q2 = [stk.enter_context(nc.sbuf_tensor(f"{uid}_qh{j}", [128, S], BF16)) for j in range(2)]
            k2 = [stk.enter_context(nc.sbuf_tensor(f"{uid}_kh{j}", [128, S], BF16)) for j in range(2)]
            aA = [stk.enter_context(nc.sbuf_tensor(f"{uid}_aA{i}", [128, 2 * TT], BF16)) for i in range(2)]
            aB = [stk.enter_context(nc.sbuf_tensor(f"{uid}_aB{i}", [128, 2 * TT], BF16)) for i in range(2)]
            t1 = [stk.enter_context(nc.sbuf_tensor(f"{uid}_t1{j}", [64, TT], F32)) for j in range(2)]
            t2 = [stk.enter_context(nc.sbuf_tensor(f"{uid}_t2{j}", [64, TT], F32)) for j in range(2)]
            aoe = [stk.enter_context(nc.sbuf_tensor(f"{uid}_aoe{j}", [64, TT], BF16)) for j in range(2)]
            xrow = [stk.enter_context(nc.sbuf_tensor(f"{uid}_xrow{j}", [64, TT], F32)) for j in range(2)]
            lrow = [stk.enter_context(nc.sbuf_tensor(f"{uid}_lrow{j}", [1, 2 * TT], F32)) for j in range(2)]
            ps_sA = stk.enter_context(nc.psum_tensor(f"{uid}_psA", [128, 2 * TT], F32))
            ps_sB = stk.enter_context(nc.psum_tensor(f"{uid}_psB", [128, 2 * TT], F32))
            ps_o2 = [stk.enter_context(nc.psum_tensor(f"{uid}_pso{j}", [128, 2 * TT], F32))
                     for j in range(2)]
            for hp in range(H // 2):
                hj = hp % 2
                p.dma("qs", q2[hj][:], q_buf[hp * 128:(hp + 1) * 128, :])
                p.dma("qs", k2[hj][:], k_buf[hp * 128:(hp + 1) * 128, :])
                qs0, qs1 = q2[hj][0:64, :], q2[hj][64:128, :]
                ks0, ks1 = k2[hj][0:64, :], k2[hj][64:128, :]
                for qt in range(NTT):
                    nkt = (qt + 1) * 4 if causal else 32
                    qsl = slice(qt * TT, (qt + 1) * TT)
                    ps_o = ps_o2[qt % 2]
                    ps_r = ps_o2[1 - qt % 2]

                    def st_stage(ktp):
                        kt0, kt1 = 2 * ktp, 2 * ktp + 1
                        j = ktp % 2
                        k.mm(ps_sA[:, 0:TT], ks0[:, kt0 * 128:(kt0 + 1) * 128], qs0[:, qsl])
                        k.mm(ps_sA[:, TT:2 * TT], ks0[:, kt1 * 128:(kt1 + 1) * 128], qs0[:, qsl])
                        k.mm(ps_sB[:, 0:TT], ks1[:, kt0 * 128:(kt0 + 1) * 128], qs1[:, qsl])
                        k.mm(ps_sB[:, TT:2 * TT], ks1[:, kt1 * 128:(kt1 + 1) * 128], qs1[:, qsl])
                        k.act(aA[j][:], ps_sA[:], AF.Exp)
                        k.act(aB[j][:], ps_sB[:], AF.Exp)
                        if causal and kt1 >= 4 * qt:
                            dlt = kt0 - 4 * qt
                            msl = C.masks[:, dlt * TT:(dlt + 2) * TT]
                            k.tt(aA[j][:], aA[j][:], msl, ALU.mult)
                            k.tt(aB[j][:], aB[j][:], msl, ALU.mult, eng="gpsimd")

                    def av_stage(ktp, last):
                        kt0, kt1 = 2 * ktp, 2 * ktp + 1
                        j = ktp % 2
                        k.mm(ps_o[0:65, 0:TT], vv4[:, kt0, 2 * hp, :], aA[j][:, 0:TT],
                             start=(ktp == 0), stop=False)
                        k.mm(ps_o[0:65, 0:TT], vv4[:, kt1, 2 * hp, :], aA[j][:, TT:2 * TT],
                             start=False, stop=last)
                        k.mm(ps_o[0:65, TT:2 * TT], vv4[:, kt0, 2 * hp + 1, :], aB[j][:, 0:TT],
                             start=(ktp == 0), stop=False)
                        k.mm(ps_o[0:65, TT:2 * TT], vv4[:, kt1, 2 * hp + 1, :], aB[j][:, TT:2 * TT],
                             start=False, stop=last)

                    nktp = nkt // 2
                    st_stage(0)
                    for ktp in range(1, nktp):
                        st_stage(ktp)
                        av_stage(ktp - 1, False)
                    av_stage(nktp - 1, True)
                    for sub in range(2):
                        h = 2 * hp + sub
                        osl = ps_o[:, sub * TT:(sub + 1) * TT]
                        ps_rv = ps_r[0:64, sub * TT:(sub + 1) * TT]
                        l_ln = lrow[sub][:, 0:TT]; r_row = lrow[sub][:, TT:2 * TT]
                        k.act(l_ln, osl[64:65, :], AF.Ln)
                        k.act(r_row, l_ln, AF.Exp, scale=-1.0)
                        k.mm(ps_rv, C.ones_row_f[0:1, 0:64], r_row)
                        k.cp(t1[sub][:], osl[0:64, :])
                        if has_proj:
                            k.tt(aoe[sub][:], t1[sub][:], ps_rv, ALU.mult)
                            p.dma("qs", ao_buf[h * 64:(h + 1) * 64, qt * TT:(qt + 1) * TT],
                                  aoe[sub][:])
                        else:
                            k.tt(t2[sub][:], t1[sub][:], ps_rv, ALU.mult)
                            p.dma("qs", xrow[sub][:], xi[h * 64:(h + 1) * 64, qt * TT:(qt + 1) * TT])
                            k.tt(t2[sub][:], t2[sub][:], xrow[sub][:], ALU.add)
                            p.dma("qa", xo[h * 64:(h + 1) * 64, qt * TT:(qt + 1) * TT], t2[sub][:])
        p.barrier()

    if has_proj:
        with ExitStack() as stk:
            wp = [stk.enter_context(nc.sbuf_tensor(f"{uid}_wp{i}", [128, D], BF16))
                  for i in range(DC)]
            ps_c2 = stk.enter_context(nc.psum_tensor(f"{uid}_psc2", [128, 1], F32))
            bpc = None
            if flavor == "l0b":
                wproj0c = get_cast(p, k, C, W, "wproj0c")
                for fc in range(DC):
                    p.dma("qs", wp[fc][:], wproj0c[fc * 128:(fc + 1) * 128, :])
                spc = nc.alloc_sbuf_tensor(f"{uid}_spc", [128, 1], F32, side="right")
                k.cp(spc[:], s_p)
            else:
                projT = get_prep(p, k, C, W, "projT1b")
                for fc in range(DC):
                    p.dma("qs", wp[fc][:], projT[fc * 128:(fc + 1) * 128, :])
                bpc = load_vec_cols(p, k, C, W["L1_attn_b_b_proj"], D, uid + "bp", stk, ps_c2)
            ao_sb2 = [[stk.enter_context(nc.sbuf_tensor(f"{uid}_aos{j}_{i}", [128, TT], BF16))
                       for i in range(DC)] for j in range(2)]
            xr = [stk.enter_context(nc.sbuf_tensor(f"{uid}_pxr{i}", [128, TT], F32)) for i in range(2)]
            ob = [stk.enter_context(nc.sbuf_tensor(f"{uid}_pob{i}", [128, TT], F32)) for i in range(2)]
            ps_y = stk.enter_context(nc.psum_tensor(f"{uid}_psy", [128, 2 * TT], F32))
            for tt in range(NTT):
                ao_sb = ao_sb2[tt % 2]
                for fc in range(DC):
                    p.dma("qs", ao_sb[fc][:], ao_buf[fc * 128:(fc + 1) * 128, tt * TT:(tt + 1) * TT])
                for oc in range(DC):
                    py = ps_y[:, (oc % 2) * TT:(oc % 2) * TT + TT]
                    for fc in range(DC):
                        k.mm(py, wp[fc][:, oc * 128:(oc + 1) * 128], ao_sb[fc][:],
                             start=(fc == 0), stop=(fc == DC - 1))
                    xrb = xr[oc % 2]
                    p.dma("qs", xrb[:], xi[oc * 128:(oc + 1) * 128, tt * TT:(tt + 1) * TT])
                    obb = ob[oc % 2]
                    if flavor == "l0b":
                        k.stt(obb[:], py, spc[:], xrb[:], op0=ALU.mult, op1=ALU.add)
                    else:
                        k.stt(obb[:], py, bpc[:, oc:oc + 1], xrb[:], op0=ALU.add, op1=ALU.add)
                    p.dma("qa", xo[oc * 128:(oc + 1) * 128, tt * TT:(tt + 1) * TT], obb[:])
    p.barrier()


def emit_l0ffn(p, k, C, xi, xo, W, uid):
    from contextlib import ExitStack
    nc = p.nc
    E, G, INTER = 16, 4, 512
    IC = INTER // 128
    BIGN = -1.0e30
    xn_buf = nc.dram_tensor(f"{uid}_xn", [D, S], BF16, kind="Internal").ap()
    h_buf = nc.dram_tensor(f"{uid}_h", [E * INTER, S], BF16, kind="Internal").ap()
    zh_buf = nc.dram_tensor(f"{uid}_zh", [D, S], BF16, kind="Internal").ap()
    mr_buf = nc.dram_tensor(f"{uid}_mr", [2, S], F32, kind="Internal").ap()
    emit_ln_fm(p, k, C, xi, xn_buf, uid + "ln", mr_dst=mr_buf, sumx_fp32=True)

    comb_dram = nc.dram_tensor(f"{uid}_comb", [16, S], BF16, kind="Internal").ap()
    with ExitStack() as outer:

        with ExitStack() as stk:
            gw = stk.enter_context(nc.sbuf_tensor(f"{uid}_gw", [16, D], F32))
            gwT = [stk.enter_context(nc.sbuf_tensor(f"{uid}_gwT{i}", [128, 16], F32))
                   for i in range(DC)]
            gbb = stk.enter_context(nc.sbuf_tensor(f"{uid}_gbb", [128, 16], F32))
            csb = stk.enter_context(nc.sbuf_tensor(f"{uid}_csb0", [128, 16], F32))
            gbr = stk.enter_context(nc.sbuf_tensor(f"{uid}_gbr", [1, 16], F32))
            gcs = stk.enter_context(nc.sbuf_tensor(f"{uid}_gcs", [16, 1], F32))
            gcsr = stk.enter_context(nc.sbuf_tensor(f"{uid}_gcsr", [1, 16], F32))
            mrow = stk.enter_context(nc.sbuf_tensor(f"{uid}_mrow", [1, TT], F32))
            rrow = stk.enter_context(nc.sbuf_tensor(f"{uid}_rrow", [1, TT], F32))
            mrc = stk.enter_context(nc.sbuf_tensor(f"{uid}_mrc", [128, 8], F32))
            sc_t = stk.enter_context(nc.sbuf_tensor(f"{uid}_sct", [128, 16], F32))
            xn_sb2 = [[stk.enter_context(nc.sbuf_tensor(f"{uid}_g_xn{j}_{i}", [128, TT], F32))
                       for i in range(DC)] for j in range(2)]
            wt = stk.enter_context(nc.sbuf_tensor(f"{uid}_wt", [128, 24], F32))
            esc = stk.enter_context(nc.sbuf_tensor(f"{uid}_esc", [128, 16], F32))
            orig = stk.enter_context(nc.sbuf_tensor(f"{uid}_orig", [128, 16], F32))
            biased = stk.enter_context(nc.sbuf_tensor(f"{uid}_biased", [128, 16], F32))
            sgp = stk.enter_context(nc.sbuf_tensor(f"{uid}_sgp", [128, 16], F32))
            m3 = stk.enter_context(nc.sbuf_tensor(f"{uid}_m3", [128, 16], F32))
            me = stk.enter_context(nc.sbuf_tensor(f"{uid}_me", [128, 16], F32))
            cmb = stk.enter_context(nc.sbuf_tensor(f"{uid}_cmb", [128, 16], F32))
            gwork = stk.enter_context(nc.sbuf_tensor(f"{uid}_gwork", [128, 16], F32))
            ctmp = stk.enter_context(nc.sbuf_tensor(f"{uid}_ctmp", [16, 128], BF16))
            ps_g = stk.enter_context(nc.psum_tensor(f"{uid}_psg0", [128, 16], F32))
            ps_tr = stk.enter_context(nc.psum_tensor(f"{uid}_pstr0", [128, 128], F32))
            ps_b0 = stk.enter_context(nc.psum_tensor(f"{uid}_psb0", [128, 16], F32))
            c_m1 = wt[:, 0:1]; c_nm1 = wt[:, 1:2]; c_ss = wt[:, 2:3]; c_rs = wt[:, 3:4]
            c_G1 = wt[:, 4:5]; c_G2 = wt[:, 5:6]; c_T1 = wt[:, 6:7]; c_T2 = wt[:, 7:8]
            gm1 = wt[:, 8:12]; gs = wt[:, 12:16]; kp = wt[:, 16:20]; pen = wt[:, 20:24]

            p.dma("qs", gw[:], W["L0_ffn_gate_w"])
            p.dma("qs", gbr[:], W["L0_ffn_gate_b"].rearrange("(a n) -> a n", a=1))
            for dc in range(DC):
                k.trans(ps_tr[0:128, 0:16], gw[:, dc * 128:(dc + 1) * 128], C.ident[0:16, 0:16])
                k.cp(gwT[dc][:], ps_tr[0:128, 0:16])
            k.mm(ps_b0[:], C.ones_row_f[:], gbr[:])
            k.cp(gbb[:], ps_b0[:])
            k.red(gcs[:, 0:1], gw[:], ALU.add)
            k.trans(ps_tr[0:1, 0:16], gcs[0:16, 0:1], C.ident[0:16, 0:16])
            k.cp(gcsr[:], ps_tr[0:1, 0:16])
            k.mm(ps_b0[:], C.ones_row_f[:], gcsr[:])
            k.cp(csb[:], ps_b0[:])
            for tt in range(NTT):
                xn_sb = xn_sb2[tt % 2]
                for dc in range(DC):
                    p.dma("qs", xn_sb[dc][:], xi[dc * 128:(dc + 1) * 128, tt * TT:(tt + 1) * TT])
                p.dma("qa", mrow[:], mr_buf[0:1, tt * TT:(tt + 1) * TT])
                p.dma("qa", rrow[:], mr_buf[1:2, tt * TT:(tt + 1) * TT])
                for st in range(4):
                    k.trans(ps_tr[0:128, 0:1], mrow[0:1, st * 128:(st + 1) * 128], C.ident[0:1, 0:1])
                    k.cp(mrc[:, st:st + 1], ps_tr[0:128, 0:1])
                    k.trans(ps_tr[0:128, 0:1], rrow[0:1, st * 128:(st + 1) * 128], C.ident[0:1, 0:1])
                    k.cp(mrc[:, 4 + st:5 + st], ps_tr[0:128, 0:1])
                for st in range(4):
                    for dc in range(DC):
                        k.mm(ps_g[:], xn_sb[dc][:, st * 128:(st + 1) * 128], gwT[dc][:],
                             start=(dc == 0), stop=(dc == DC - 1))
                    k.ts(sc_t[:], csb[:], mrc[:, st:st + 1], op0=ALU.mult)
                    k.tt(sc_t[:], ps_g[:], sc_t[:], ALU.subtract)
                    k.ts(sc_t[:], sc_t[:], mrc[:, 4 + st:5 + st], op0=ALU.mult)
                    k.red(c_m1, sc_t[:], ALU.max)
                    k.ts(c_nm1, c_m1, -1.0)
                    k.act(esc[:], sc_t[:], AF.Exp, bias=c_nm1)
                    k.red(c_ss, esc[:], ALU.add)
                    k.act(c_rs, c_ss, AF.Ln)
                    k.act(c_rs, c_rs, AF.Exp, scale=-1.0)
                    k.ts(orig[:], esc[:], c_rs, op0=ALU.mult)
                    k.tt(biased[:], orig[:], gbb[:], ALU.add)
                    for g in range(G):
                        sl = biased[:, 4 * g:4 * g + 4]
                        k.red(gm1[:, g:g + 1], sl, ALU.max)
                        k.ts(me[:, 0:4], sl, gm1[:, g:g + 1], op0=ALU.is_equal)
                        k.stt(gwork[:, 0:4], me[:, 0:4], BIGN, sl, op0=ALU.mult, op1=ALU.add)
                        k.red(gs[:, g:g + 1], gwork[:, 0:4], ALU.max)
                        k.tt(gs[:, g:g + 1], gs[:, g:g + 1], gm1[:, g:g + 1], ALU.add)
                    k.red(c_G1, gs[:, 0:4], ALU.max)
                    k.ts(me[:, 0:4], gs[:, 0:4], c_G1, op0=ALU.is_equal)
                    k.stt(gwork[:, 0:4], me[:, 0:4], BIGN, gs[:, 0:4], op0=ALU.mult, op1=ALU.add)
                    k.red(c_G2, gwork[:, 0:4], ALU.max)
                    k.ts(kp[:, 0:4], gs[:, 0:4], c_G2, op0=ALU.is_ge)
                    k.ts(pen[:, 0:4], kp[:, 0:4], 1.0, -BIGN, op0=ALU.subtract, op1=ALU.mult)
                    for g in range(G):
                        k.ts(sgp[:, 4 * g:4 * g + 4], biased[:, 4 * g:4 * g + 4],
                             pen[:, g:g + 1], op0=ALU.add)
                    k.red(c_T1, sgp[:], ALU.max)
                    k.ts(me[:], sgp[:], c_T1, op0=ALU.is_equal)
                    k.stt(m3[:], me[:], BIGN, sgp[:], op0=ALU.mult, op1=ALU.add)
                    k.red(c_T2, m3[:], ALU.max)
                    k.ts(m3[:], m3[:], c_T2, op0=ALU.is_equal)
                    k.tt(me[:], me[:], m3[:], ALU.add)
                    k.tt(cmb[:], me[:], orig[:], ALU.mult)
                    k.trans(ps_tr[0:16, 0:128], cmb[:], C.ident[:])
                    k.cp(ctmp[:], ps_tr[0:16, 0:128])
                    p.dma("qs", comb_dram[:, (tt * 4 + st) * 128:(tt * 4 + st + 1) * 128],
                          ctmp[:])
        p.barrier()

        with ExitStack() as stk:
            ws1 = [stk.enter_context(nc.sbuf_tensor(f"{uid}_ws1_{i}", [128, D], BF16))
                   for i in range(DC)]
            ws3 = [stk.enter_context(nc.sbuf_tensor(f"{uid}_ws3_{i}", [128, D], BF16))
                   for i in range(DC)]
            ps_c1 = stk.enter_context(nc.psum_tensor(f"{uid}_zc1", [128, 1], F32))
            bs1c = load_vec_cols(p, k, C, W["L0_ffn_bs1"], D, uid + "b1", stk, ps_c1)
            bs3c = load_vec_cols(p, k, C, W["L0_ffn_bs3"], D, uid + "b3", stk, ps_c1)
            xn_sb2 = [[stk.enter_context(nc.sbuf_tensor(f"{uid}_z_xn{j}_{i}", [128, TT], BF16))
                       for i in range(DC)] for j in range(2)]
            sg1r = [stk.enter_context(nc.sbuf_tensor(f"{uid}_zsg{j}", [128, TT], BF16)) for j in range(2)]
            ztr = [stk.enter_context(nc.sbuf_tensor(f"{uid}_zt{j}", [128, TT], BF16)) for j in range(2)]
            ps_1r = [stk.enter_context(nc.psum_tensor(f"{uid}_zp1{j}", [128, TT], F32)) for j in range(2)]
            ps_3r = [stk.enter_context(nc.psum_tensor(f"{uid}_zp3{j}", [128, TT], F32)) for j in range(2)]
            ws1c = get_cast(p, k, C, W, "ws1c")
            ws3c = get_cast(p, k, C, W, "ws3c")
            for dc in range(DC):
                p.dma("qs", ws1[dc][:], ws1c[dc * 128:(dc + 1) * 128, :], group="ws13")
                p.dma("qs", ws3[dc][:], ws3c[dc * 128:(dc + 1) * 128, :], group="ws13")
            for tt in range(NTT):
                xn_sb = xn_sb2[tt % 2]
                for dc in range(DC):
                    p.dma("qs", xn_sb[dc][:], xn_buf[dc * 128:(dc + 1) * 128, tt * TT:(tt + 1) * TT])
                for ic in range(DC):
                    ij = ic % 2
                    ps_1, ps_3, sg1, zt = ps_1r[ij], ps_3r[ij], sg1r[ij], ztr[ij]
                    for dc in range(DC):
                        k.mm(ps_1[:], ws1[dc][:, ic * 128:(ic + 1) * 128], xn_sb[dc][:],
                             start=(dc == 0), stop=(dc == DC - 1))
                    k.act(sg1[:], ps_1[:], AF.Silu, bias=bs1c[:, ic:ic + 1])
                    for dc in range(DC):
                        k.mm(ps_3[:], ws3[dc][:, ic * 128:(ic + 1) * 128], xn_sb[dc][:],
                             start=(dc == 0), stop=(dc == DC - 1))
                    k.stt(zt[:], ps_3[:], bs3c[:, ic:ic + 1], sg1[:], op0=ALU.add, op1=ALU.mult)
                    p.dma("qa", zh_buf[ic * 128:(ic + 1) * 128, tt * TT:(tt + 1) * TT], zt[:])
        p.barrier()

        with ExitStack() as stk:
            xn_all = [stk.enter_context(nc.sbuf_tensor(f"{uid}_e_xn{i}", [128, S], BF16))
                      for i in range(DC)]
            for dc in range(DC):
                p.dma("qs", xn_all[dc][:], xn_buf[dc * 128:(dc + 1) * 128, :], group="xnall")
            ps_c1 = stk.enter_context(nc.psum_tensor(f"{uid}_ec1", [128, 1], F32))
            be1c = stk.enter_context(nc.sbuf_tensor(f"{uid}_be1c", [128, 4 * E], F32))
            be3c = stk.enter_context(nc.sbuf_tensor(f"{uid}_be3c", [128, 4 * E], F32))
            ber = stk.enter_context(nc.sbuf_tensor(f"{uid}_ber", [1, INTER], F32))
            for e in range(E):
                p.dma("qs", ber[:], W["L0_ffn_be1"][e:e + 1, :])
                for c in range(IC):
                    k.trans(ps_c1[:], ber[0:1, c * 128:(c + 1) * 128], C.ident[0:1, 0:1])
                    k.cp(be1c[:, e * IC + c:e * IC + c + 1], ps_c1[:])
                p.dma("qs", ber[:], W["L0_ffn_be3"][e:e + 1, :])
                for c in range(IC):
                    k.trans(ps_c1[:], ber[0:1, c * 128:(c + 1) * 128], C.ident[0:1, 0:1])
                    k.cp(be3c[:, e * IC + c:e * IC + c + 1], ps_c1[:])
            we1 = [[stk.enter_context(nc.sbuf_tensor(f"{uid}_we1_{j}_{i}", [128, INTER], BF16))
                    for i in range(DC)] for j in range(2)]
            we3 = [[stk.enter_context(nc.sbuf_tensor(f"{uid}_we3_{j}_{i}", [128, INTER], BF16))
                    for i in range(DC)] for j in range(2)]
            c_sb = [stk.enter_context(nc.sbuf_tensor(f"{uid}_ecsb{j}", [128, TT], F32)) for j in range(2)]
            crow = [stk.enter_context(nc.sbuf_tensor(f"{uid}_crow{j}", [1, S], BF16)) for j in range(2)]
            sg1r = [stk.enter_context(nc.sbuf_tensor(f"{uid}_esg{j}", [128, TT], BF16)) for j in range(2)]
            hhr = [stk.enter_context(nc.sbuf_tensor(f"{uid}_ehh{j}", [128, TT], F32)) for j in range(2)]
            hbtr = [stk.enter_context(nc.sbuf_tensor(f"{uid}_ehb{j}", [128, TT], BF16)) for j in range(2)]
            ps_1r = [stk.enter_context(nc.psum_tensor(f"{uid}_ep1{j}", [128, TT], F32)) for j in range(2)]
            ps_3r = [stk.enter_context(nc.psum_tensor(f"{uid}_ep3{j}", [128, TT], F32)) for j in range(2)]
            ps_c = [stk.enter_context(nc.psum_tensor(f"{uid}_epc{j}", [128, TT], F32)) for j in range(2)]
            We1c = get_cast(p, k, C, W, "We1c")
            We3c = get_cast(p, k, C, W, "We3c")
            for dc in range(DC):
                p.dma("qs", we1[0][dc][:], We1c[dc * 128:(dc + 1) * 128, :], group="we13a")
                p.dma("qs", we3[0][dc][:], We3c[dc * 128:(dc + 1) * 128, :], group="we13a")
            p.dma("qs", crow[0][:], comb_dram[0:1, :])
            for e in range(E):
                ej = e % 2
                if e + 1 < E:
                    p.dma("qs", crow[1 - ej][:], comb_dram[e + 1:e + 2, :])
                if e + 1 < E:
                    for dc in range(DC):
                        p.dma("qs", we1[1 - ej][dc][:],
                              We1c[(e + 1) * 1024 + dc * 128:(e + 1) * 1024 + (dc + 1) * 128, :],
                              group="we13" + ("a" if (1 - ej) == 0 else "b"))
                        p.dma("qs", we3[1 - ej][dc][:],
                              We3c[(e + 1) * 1024 + dc * 128:(e + 1) * 1024 + (dc + 1) * 128, :],
                              group="we13" + ("a" if (1 - ej) == 0 else "b"))
                for tt in range(NTT):
                    tj = tt % 2
                    k.mm(ps_c[tj][:], C.ones_row_b[:], crow[ej][0:1, tt * TT:(tt + 1) * TT])
                    k.cp(c_sb[tj][:], ps_c[tj][:])
                    for ic in range(IC):
                        ij = ic % 2
                        ps_1, ps_3, sg1, hh, hbt = ps_1r[ij], ps_3r[ij], sg1r[ij], hhr[ij], hbtr[ij]
                        for dc in range(DC):
                            k.mm(ps_1[:], we1[ej][dc][:, ic * 128:(ic + 1) * 128],
                                 xn_all[dc][:, tt * TT:(tt + 1) * TT],
                                 start=(dc == 0), stop=(dc == DC - 1))
                        k.act(sg1[:], ps_1[:], AF.Silu, bias=be1c[:, e * IC + ic:e * IC + ic + 1])
                        for dc in range(DC):
                            k.mm(ps_3[:], we3[ej][dc][:, ic * 128:(ic + 1) * 128],
                                 xn_all[dc][:, tt * TT:(tt + 1) * TT],
                                 start=(dc == 0), stop=(dc == DC - 1))
                        k.stt(hh[:], ps_3[:], be3c[:, e * IC + ic:e * IC + ic + 1], sg1[:],
                              op0=ALU.add, op1=ALU.mult)
                        k.tt(hbt[:], hh[:], c_sb[tj][:], ALU.mult, eng="gpsimd")
                        p.dma("qa", h_buf[e * INTER + ic * 128:e * INTER + (ic + 1) * 128,
                                          tt * TT:(tt + 1) * TT], hbt[:])
        p.barrier()

        with ExitStack() as stk:
            we2 = [stk.enter_context(nc.sbuf_tensor(f"{uid}_we2_{i}", [128, D], BF16))
                   for i in range(E * IC)]
            ws2 = [stk.enter_context(nc.sbuf_tensor(f"{uid}_ws2_{i}", [128, D], BF16))
                   for i in range(DC)]
            be2f = stk.enter_context(nc.sbuf_tensor(f"{uid}_be2f", [16, D], F32))
            be2 = stk.enter_context(nc.sbuf_tensor(f"{uid}_be2", [16, D], BF16))
            ps_y = [stk.enter_context(nc.psum_tensor(f"{uid}_psy{i}", [128, 2 * TT], F32))
                    for i in range(4)]
            bs2c = load_vec_cols(p, k, C, W["L0_ffn_bs2"], D, uid + "b2", stk, ps_y[0][:, 0:1])
            We2c = get_cast(p, k, C, W, "We2c")
            ws2c = get_cast(p, k, C, W, "ws2c")
            for e in range(E):
                for ic in range(IC):
                    p.dma("qs", we2[e * IC + ic][:],
                          We2c[e * 512 + ic * 128:e * 512 + (ic + 1) * 128, :], group="we2")
            for dc in range(DC):
                p.dma("qs", ws2[dc][:], ws2c[dc * 128:(dc + 1) * 128, :], group="ws2")
            p.dma("qs", be2f[:], W["L0_ffn_be2"])
            k.cp(be2[:], be2f[:])
            combT = stk.enter_context(nc.sbuf_tensor(f"{uid}_combT", [16, S], BF16))
            p.dma("qs", combT[:], comb_dram)
            hring = [stk.enter_context(nc.sbuf_tensor(f"{uid}_hr{i}", [128, TT], BF16))
                     for i in range(4)]
            zring = [stk.enter_context(nc.sbuf_tensor(f"{uid}_zr{i}", [128, TT], BF16))
                     for i in range(2)]
            xr = [stk.enter_context(nc.sbuf_tensor(f"{uid}_yxr{i}", [128, TT], F32)) for i in range(2)]
            ob = [stk.enter_context(nc.sbuf_tensor(f"{uid}_yob{i}", [128, TT], F32)) for i in range(2)]

            def py(oc):
                return ps_y[oc // 2][:, (oc % 2) * TT:(oc % 2) * TT + TT]
            for tt in range(NTT):
                first = True
                for e in range(E):
                    for ic in range(IC):
                        hb = hring[(e * IC + ic) % 4]
                        p.dma("qs", hb[:], h_buf[e * INTER + ic * 128:e * INTER + (ic + 1) * 128,
                                                 tt * TT:(tt + 1) * TT])
                        for oc in range(DC):
                            k.mm(py(oc), we2[e * IC + ic][:, oc * 128:(oc + 1) * 128], hb[:],
                                 start=first, stop=False)
                        first = False
                for ic2 in range(DC):
                    zb = zring[ic2 % 2]
                    p.dma("qs", zb[:], zh_buf[ic2 * 128:(ic2 + 1) * 128, tt * TT:(tt + 1) * TT])
                    for oc in range(DC):
                        k.mm(py(oc), ws2[ic2][:, oc * 128:(oc + 1) * 128], zb[:],
                             start=False, stop=False)
                for oc in range(DC):
                    k.mm(py(oc), be2[:, oc * 128:(oc + 1) * 128],
                         combT[:, tt * TT:(tt + 1) * TT], start=False, stop=True)
                for oc in range(DC):
                    xrb = xr[oc % 2]
                    p.dma("qs", xrb[:], xi[oc * 128:(oc + 1) * 128, tt * TT:(tt + 1) * TT])
                    obb = ob[oc % 2]
                    k.stt(obb[:], py(oc), bs2c[:, oc:oc + 1], xrb[:], op0=ALU.add, op1=ALU.add)
                    p.dma("qa", xo[oc * 128:(oc + 1) * 128, tt * TT:(tt + 1) * TT], obb[:])
    p.barrier()


PREP_SPECS = {
    "wgT": ("L1_ffn_w_gate", 3072, 1024, I32, None),
    "wuT": ("L1_ffn_w_up", 3072, 1024, I32, None),
    "wdT": ("L1_ffn_w_down", 1024, 3072, I32, None),
    "qkvT1b": ("L1_attn_b_W_qkv", 3072, 1024, F32, None),
    "projT1b": ("L1_attn_b_W_proj", 1024, 1024, F32, None),
    "WqT": ("L1_attn_a_Wq", 1024, 1024, F32, None),
    "WkT": ("L1_attn_a_Wk", 1024, 1024, F32, None),
    "WvT": ("L1_attn_a_Wv", 1024, 1024, F32, None),
    "wvT": ("L0_attn_a_w_qkv", 1024, 1024, F32, (2048, 3072)),
    "woT": ("L0_attn_a_w_o", 1024, 1024, F32, None),
}

def emit_prep_cast(p, k, C, src, dst, rows, cols, uid, B):
    n = 2048
    total = rows * cols
    assert total % (128 * n) == 0, (rows, cols)
    nb = total // (128 * n)
    sf = src.rearrange("r c -> (r c)").rearrange("(a p n) -> a p n", p=128, n=n)
    df = dst.rearrange("r c -> (r c)").rearrange("(a p n) -> a p n", p=128, n=n)
    for b in range(nb):
        sl, sb = B.slab_f[b % 3], B.slab_b[b % 3]
        p.dma("qs", sl[:, 0:n], sf[b])
        k.cp(sb[:, 0:n], sl[:, 0:n])
        p.dma("qa", df[b], sb[:, 0:n])


CAST_SPECS = {
    "We1c": ("L0_ffn_We1", 16 * 1024, 512),
    "We3c": ("L0_ffn_We3", 16 * 1024, 512),
    "We2c": ("L0_ffn_We2", 16 * 512, 1024),
    "ws1c": ("L0_ffn_ws1", 1024, 1024),
    "ws2c": ("L0_ffn_ws2", 1024, 1024),
    "ws3c": ("L0_ffn_ws3", 1024, 1024),
    "wqkv0c": ("L0_attn_b_W_qkv", 1024, 3072),
    "wproj0c": ("L0_attn_b_W_proj", 1024, 1024),
}

CASTS_FOR_BLOCK = {1: ["We1c", "We3c", "We2c", "ws1c", "ws2c", "ws3c"],
                   2: ["wqkv0c", "wproj0c"],
                   3: ["We1c", "We3c", "We2c", "ws1c", "ws2c", "ws3c"]}


def get_cast(p, k, C, W, key, B=None):
    if not hasattr(p, "cast_cache"):
        p.cast_cache = {}
    if key in p.cast_cache:
        return p.cast_cache[key]
    assert B is not None, f"cast {key} must be emitted in the prologue"
    name, rows, cols = CAST_SPECS[key]
    src = W[name]
    if len(INPUT_SPECS[name][0]) == 3:
        src = src.rearrange("h d e -> (h d) e")
    dst = p.nc.dram_tensor("cast_" + key, [rows, cols], BF16, kind="Internal").ap()
    emit_prep_cast(p, k, C, src, dst, rows, cols, "cc" + key, B)
    p.cast_cache[key] = dst
    return dst


PREPS_FOR_BLOCK = {0: ["wvT", "woT"], 4: ["WqT", "WkT", "WvT"],
                   5: ["wgT", "wuT", "wdT"], 6: ["qkvT1b", "projT1b"],
                   7: ["wgT", "wuT", "wdT"]}


def get_prep(p, k, C, W, key, B=None):
    if not hasattr(p, "prep_cache"):
        p.prep_cache = {}
    if key in p.prep_cache:
        return p.prep_cache[key]
    assert B is not None, f"prep {key} must be emitted in the prologue"
    name, rows, cols, dt, rslice = PREP_SPECS[key]
    src = W[name]
    if len(INPUT_SPECS[name][0]) == 3:
        src = src.rearrange("h d e -> (h d) e")
    if rslice is not None:
        src = src[rslice[0]:rslice[1], :]
    dst = p.nc.dram_tensor("prep_" + key, [cols, rows], BF16, kind="Internal").ap()
    emit_prep_transpose(p, k, C, src, dst, rows, cols, dt, "pp" + key, B)
    p.prep_cache[key] = dst
    return dst


def blk1(p, k, C, xi, xo, W, uid):
    emit_block1(p, k, C, xi, xo, W, uid)


def blk_l1ffn(p, k, C, xi, xo, W, uid, last=False):
    preps = {kk: get_prep(p, k, C, W, kk) for kk in ("wgT", "wuT", "wdT")}
    emit_l1ffn(p, k, C, xi, xo, W, uid, preps, last=last)


def _nyi(*a, **kw):
    raise NotImplementedError


L1FFN_INPUTS = ["L1_ffn_w_gate", "L1_ffn_w_up", "L1_ffn_w_down"]
L0FFN_INPUTS = ["L0_ffn_gate_w", "L0_ffn_gate_b", "L0_ffn_We1", "L0_ffn_be1",
                "L0_ffn_We2", "L0_ffn_be2", "L0_ffn_We3", "L0_ffn_be3",
                "L0_ffn_ws1", "L0_ffn_bs1", "L0_ffn_ws2", "L0_ffn_bs2",
                "L0_ffn_ws3", "L0_ffn_bs3"]

BLOCK_DEFS = [
    ("b1", blk1, ["L0_attn_a_w_qkv", "L0_attn_a_w_o"]),
    ("b2", lambda p, k, C, xi, xo, W, uid: emit_l0ffn(p, k, C, xi, xo, W, uid), L0FFN_INPUTS),
    ("b3", lambda p, k, C, xi, xo, W, uid: emit_attention(p, k, C, xi, xo, W, uid, "l0b"),
     ["L0_attn_b_W_qkv", "L0_attn_b_u_qkv", "L0_attn_b_sigma_qkv",
      "L0_attn_b_W_proj", "L0_attn_b_u_proj", "L0_attn_b_sigma_proj"]),
    ("b4", lambda p, k, C, xi, xo, W, uid: emit_l0ffn(p, k, C, xi, xo, W, uid), L0FFN_INPUTS),
    ("b5", lambda p, k, C, xi, xo, W, uid: emit_attention(p, k, C, xi, xo, W, uid, "l1a"),
     ["L1_attn_a_Wq", "L1_attn_a_bq", "L1_attn_a_Wk", "L1_attn_a_bk",
      "L1_attn_a_Wv", "L1_attn_a_bv"]),
    ("b6", lambda p, k, C, xi, xo, W, uid: blk_l1ffn(p, k, C, xi, xo, W, uid, last=False),
     L1FFN_INPUTS),
    ("b7", lambda p, k, C, xi, xo, W, uid: emit_attention(p, k, C, xi, xo, W, uid, "l1b"),
     ["L1_attn_b_W_qkv", "L1_attn_b_W_proj", "L1_attn_b_b_proj"]),
    ("b8", lambda p, k, C, xi, xo, W, uid: blk_l1ffn(p, k, C, xi, xo, W, uid, last=True),
     L1FFN_INPUTS),
]


def kernel(**inputs):
    n = 8
    x_full = np.ascontiguousarray(np.asarray(inputs["L0_attn_a_x"], dtype=np.float32))
    groups = [list(range(len(BLOCK_DEFS)))]
    cur = None
    out = None
    for g in groups:
        nc, needed = build_program(g)
        in_maps = []
        for c in range(n):
            m = {}
            for name in needed:
                if name == "L0_attn_a_x":
                    m[name] = x_full[c]
                else:
                    m[name] = np.asarray(inputs[name])
            if g[0] != 0:
                m["x_fm_in"] = cur[c]
            in_maps.append(m)
        res = run_bass_kernel_spmd(nc, in_maps, core_ids=list(range(n)))
        if g[-1] == 7:
            out = np.stack([res.results[c]["y"] for c in range(n)], axis=0)
        else:
            cur = [res.results[c]["x_fm_out"] for c in range(n)]
    return out


# version 17
# speedup vs baseline: 1.4457x; 1.0031x over previous
import numpy as np
import concourse.bass as bass
import concourse.mybir as mybir
from concourse.bass_utils import run_bass_kernel_spmd

F32 = mybir.dt.float32
BF16 = mybir.dt.bfloat16
I32 = mybir.dt.int32
AF = mybir.ActivationFunctionType
ALU = mybir.AluOpType
AX = mybir.AxisListType

S = 4096
D = 1024
DC = D // 128
TT = 512
NTT = S // TT
EPS = 1e-5

ENGINES = ("sync", "scalar", "vector", "tensor", "gpsimd")
QUEUE_ENG = {"qs": "sync", "qg": "gpsimd", "qa": "scalar"}
N_DMA_SEM_POOL = 12


def _free_span(ap):
    dims = ap.ap
    part_stride = dims[0][0]
    f0 = ap.offset % part_stride if part_stride > 0 else ap.offset
    span = 1
    for step, count in dims[1:]:
        span += abs(step) * (count - 1)
    return (f0, f0 + span)


def _part_span(ap):
    dims = ap.ap
    part_stride = dims[0][0]
    p0 = ap.offset // part_stride if part_stride > 0 else 0
    return (p0, p0 + dims[0][1])


def _flat_span(ap):
    span = 1
    for step, count in ap.ap:
        span += abs(step) * (count - 1)
    return (ap.offset, ap.offset + span)


def _region(ap):
    t = ap.tensor
    name = t.name
    kind = type(t).__name__
    if kind == "SBTensorHandle":
        p = _part_span(ap); f = _free_span(ap)
        return (name, p[0], p[1], f[0], f[1])
    if kind == "PSumTensorHandle":
        f = _free_span(ap)
        esz = mybir.dt.size(ap.dtype)
        bank = 2048 // esz
        return (name, 0, 128, (f[0] // bank) * bank,
                ((f[1] + bank - 1) // bank) * bank)
    flat = _flat_span(ap)
    return (name, 0, 1, flat[0], flat[1])


def _overlap(r1, r2):
    return (r1[0] == r2[0] and r1[1] < r2[2] and r2[1] < r1[2]
            and r1[3] < r2[4] and r2[3] < r1[4])


class Op:
    __slots__ = ("idx", "eng", "emit", "deps", "inc_needed", "sem", "val", "is_dma", "dbg")

    def __init__(self, idx, eng, emit, is_dma, dbg):
        self.idx = idx
        self.eng = eng
        self.emit = emit
        self.deps = []
        self.inc_needed = is_dma
        self.sem = None
        self.val = None
        self.is_dma = is_dma
        self.dbg = dbg


class P:

    def __init__(self):
        self.nc = bass.Bass("TRN2", target_bir_lowering=False)
        self.ops = []
        self.writers = {}
        self.readers = {}
        self.semh = {}
        self.dma_cnt = {}
        self.phase_idx = {q: 0 for q in QUEUE_ENG}
        self.dbg_tag = ""
        self.last_op = {}

    def barrier(self):
        self._dma_sem_map = {}
        sem_deps = [("sem", s, c * 16) for s, c in self.dma_cnt.items()]
        lasts = dict(self.last_op)
        for eng in ENGINES:
            deps = [("op", o) for e2, o in lasts.items() if e2 != eng and not o.is_dma]
            o = Op(len(self.ops), eng, lambda e: e.nop(), False, "barrier")
            o.deps = deps + list(sem_deps)
            for kind, *rest in deps:
                rest[0].inc_needed = True
            self.ops.append(o)
            self.last_op[eng] = o

    def sem(self, name):
        if name not in self.semh:
            self.semh[name] = self.nc.alloc_semaphore(name=name)
        return self.semh[name]

    def _record(self, op, r, w):
        regs_r = [_region(a) for a in r]
        regs_w = [_region(a) for a in w]
        deps = {}
        for reg in regs_r:
            for oreg, oop in self.writers.get(reg[0], ()):
                if _overlap(reg, oreg):
                    deps[oop.idx] = oop
        for reg in regs_w:
            for oreg, oop in self.writers.get(reg[0], ()):
                if _overlap(reg, oreg):
                    deps[oop.idx] = oop
            for oreg, oop in self.readers.get(reg[0], ()):
                if _overlap(reg, oreg):
                    deps[oop.idx] = oop
        for dep in deps.values():
            if dep is op:
                continue
            if dep.is_dma:
                op.deps.append(("sem", dep.sem, self.dma_cnt[dep.sem] * 16))
            else:
                if dep.eng == op.eng == "tensor" and not op.is_dma:
                    continue
                dep.inc_needed = True
                op.deps.append(("op", dep))
        for reg in regs_r:
            self.readers.setdefault(reg[0], []).append((reg, op))
        for reg in regs_w:
            wl = self.writers.setdefault(reg[0], [])
            wl[:] = [(orr, oo) for (orr, oo) in wl
                     if not (orr[1] >= reg[1] and orr[2] <= reg[2]
                             and orr[3] >= reg[3] and orr[4] <= reg[4])]
            wl.append((reg, op))
            rl = self.readers.get(reg[0])
            if rl:
                rl[:] = [(orr, oo) for (orr, oo) in rl if not _overlap(reg, orr)]

    def op(self, eng, emit, r=(), w=(), dbg=""):
        o = Op(len(self.ops), eng, emit, False, dbg or self.dbg_tag)
        self.ops.append(o)
        self._record(o, r, w)
        self.last_op[eng] = o
        return o

    def _dma_sem_for(self, out, in_):
        tname = None
        for ap in (out, in_):
            if type(ap.tensor).__name__ == "SBTensorHandle":
                tname = ap.tensor.name
                break
        if tname is None:
            tname = "@dram"
        return tname

    def dma(self, queue, out, in_, group=None, **kw):
        swdge = queue == "qg"
        key = (group or self._dma_sem_for(out, in_), swdge)
        if not hasattr(self, "_dma_sem_map"):
            self._dma_sem_map = {}
            self._sem_rr = {False: 0, True: 0}
        if key not in self._dma_sem_map:
            if swdge:
                self._dma_sem_map[key] = "dg%d" % (self._sem_rr[True] % 32)
            else:
                self._dma_sem_map[key] = "dh%d" % (self._sem_rr[False] % 64)
            self._sem_rr[swdge] += 1
        semname = self._dma_sem_map[key]
        self.sem(semname)

        def emit(e, out=out, in_=in_, kw=kw):
            return e.dma_start(out=out, in_=in_, **kw)
        o = Op(len(self.ops), QUEUE_ENG[queue], emit, True, self.dbg_tag)
        o.sem = semname
        self.ops.append(o)
        self._record(o, [in_], [out])
        self.dma_cnt[semname] = self.dma_cnt.get(semname, 0) + 1
        o.val = self.dma_cnt[semname] * 16
        self.last_op[o.eng] = o
        return o

    def phase(self, queue=None):
        for q in ([queue] if queue else list(QUEUE_ENG)):
            self.phase_idx[q] += 1

    def build(self):
        nc = self.nc
        eng_cnt = {e: 0 for e in ENGINES}
        th = {e: [] for e in ENGINES}
        watermark = {e: {} for e in ENGINES}
        n_waits = 0
        for o in self.ops:
            if not o.is_dma and o.inc_needed:
                eng_cnt[o.eng] += 1
                o.sem = "e_" + o.eng
                o.val = eng_cnt[o.eng]
                self.sem(o.sem)
                if o.eng == "gpsimd" and o.val > 1:
                    o.deps.append(("sem", o.sem, o.val - 1))
            for d in o.deps:
                if d[0] == "sem":
                    semname, val = d[1], d[2]
                else:
                    dep = d[1]
                    semname, val = dep.sem, dep.val
                    assert val is not None, (o.dbg, dep.dbg)
                wm = watermark[o.eng]
                if wm.get(semname, 0) >= val:
                    continue
                wm[semname] = val
                h = self.semh[semname]
                th[o.eng].append(lambda e, h=h, v=val: e.wait_ge(h, v))
                n_waits += 1
            if o.sem is not None and (o.is_dma or o.inc_needed):
                h = self.semh[o.sem]
                th[o.eng].append(
                    lambda e, o=o, h=h: o.emit(e).then_inc(h, 16 if o.is_dma else 1))
            else:
                th[o.eng].append(o.emit)
        for semname, cnt in self.dma_cnt.items():
            h = self.semh[semname]
            th["sync"].append(lambda e, h=h, v=cnt * 16: e.wait_ge(h, v))
        self._n_waits = n_waits
        with nc.Block() as block:
            for eng in ENGINES:
                lst = th[eng]
                if lst:
                    getattr(block, eng)(lambda e, lst=lst: [f(e) for f in lst])
        return nc


def is_ap(x):
    return hasattr(x, "tensor")


class K:

    def __init__(self, p):
        self.p = p
        self._consts = {}

    def constcol(self, val, dtype=F32):
        key = (float(val), dtype)
        if key not in self._consts:
            t = self.p.nc.alloc_sbuf_tensor(f"c_k{len(self._consts)}", [128, 1], dtype, side="right")
            self.memset(t[:], float(val))
            self._consts[key] = t
        return self._consts[key]

    def mm(self, out, lhsT, rhs, start=True, stop=True):
        return self.p.op("tensor",
                         lambda e, out=out, lhsT=lhsT, rhs=rhs, start=start, stop=stop:
                         e.matmul(out, lhsT, rhs, start=start, stop=stop, skip_group_check=True),
                         r=[lhsT, rhs], w=[out])

    def trans(self, out, in_, ident):
        return self.p.op("tensor",
                         lambda e, out=out, in_=in_, ident=ident: e.transpose(out, in_, ident),
                         r=[in_, ident], w=[out])

    def act(self, out, in_, func, bias=0.0, scale=1.0, accum=None):
        np0 = _part_span(out)
        if not is_ap(bias) and float(bias) != 0.0:
            bias = self.constcol(bias)[np0[0]:np0[1], :]
        if not is_ap(scale) and float(scale) != 1.0:
            scale = self.constcol(scale)[np0[0]:np0[1], :]
        r = [in_] + [x for x in (bias, scale) if is_ap(x)]
        w = [out] + ([accum] if accum is not None else [])
        kw = {}
        if accum is not None:
            kw["accum_out"] = accum
        return self.p.op("scalar",
                         lambda e, out=out, in_=in_, func=func, bias=bias, scale=scale, kw=kw:
                         e.activation(out, in_, func, bias=bias, scale=scale, **kw),
                         r=r, w=w)

    def ts(self, out, in0, s1, s2=None, op0=ALU.mult, op1=ALU.bypass, eng="vector"):
        r = [in0] + [x for x in (s1, s2) if is_ap(x)]
        if s2 is None:
            return self.p.op(eng,
                             lambda e, out=out, in0=in0, s1=s1, op0=op0:
                             e.tensor_scalar(out, in0, s1, None, op0=op0),
                             r=r, w=[out])
        return self.p.op(eng,
                         lambda e, out=out, in0=in0, s1=s1, s2=s2, op0=op0, op1=op1:
                         e.tensor_scalar(out, in0, s1, s2, op0=op0, op1=op1),
                         r=r, w=[out])

    def tt(self, out, in0, in1, op, eng="vector"):
        return self.p.op(eng,
                         lambda e, out=out, in0=in0, in1=in1, op=op:
                         e.tensor_tensor(out, in0, in1, op=op),
                         r=[in0, in1], w=[out])

    def stt(self, out, in0, s, in1, op0, op1, eng="vector"):
        r = [in0, in1] + ([s] if is_ap(s) else [])
        return self.p.op(eng,
                         lambda e, out=out, in0=in0, s=s, in1=in1, op0=op0, op1=op1:
                         e.scalar_tensor_tensor(out, in0, s, in1, op0=op0, op1=op1),
                         r=r, w=[out])

    def red(self, out, in_, op, eng="vector"):
        return self.p.op(eng,
                         lambda e, out=out, in_=in_, op=op:
                         e.tensor_reduce(out, in_, axis=AX.X, op=op),
                         r=[in_], w=[out])

    def cp(self, out, in_, eng="vector"):
        return self.p.op(eng,
                         lambda e, out=out, in_=in_: e.tensor_copy(out, in_),
                         r=[in_], w=[out])

    def memset(self, ap, c, eng="vector"):
        return self.p.op(eng, lambda e, ap=ap, c=c: e.memset(ap, c), r=[], w=[ap])

    def iota(self, ap, pattern, base=0, chan=0):
        return self.p.op("gpsimd",
                         lambda e, ap=ap, pattern=pattern, base=base, chan=chan:
                         e.iota(ap, pattern=pattern, base=base, channel_multiplier=chan,
                                allow_small_or_imprecise_dtypes=True),
                         r=[], w=[ap])


class Consts:
    def __init__(self, p, k):
        nc = p.nc
        self.ident = nc.alloc_sbuf_tensor("c_ident", [128, 128], F32, side="right")
        self.ones_col_b = nc.alloc_sbuf_tensor("c_ones_col_b", [128, 1], BF16, side="right")
        self.ones_col_f = nc.alloc_sbuf_tensor("c_ones_col_f", [128, 1], F32, side="right")
        self.ones_row_f = nc.alloc_sbuf_tensor("c_ones_row_f", [1, 128], F32, side="right")
        self.ones_row_b = nc.alloc_sbuf_tensor("c_ones_row_b", [1, 128], BF16, side="right")
        self.jrow = nc.alloc_sbuf_tensor("c_jrow", [128, TT], F32, side="right")
        self.pcol = nc.alloc_sbuf_tensor("c_pcol", [128, 1], F32, side="right")
        self.masks = nc.alloc_sbuf_tensor("c_masks", [128, 4 * TT], BF16, side="right")

        k.iota(self.jrow[:], [[1, TT]], 0, 0)
        k.iota(self.pcol[:], [[0, 1]], 0, 1)
        k.ts(self.ident[:], self.jrow[:, 0:128], self.pcol[:], op0=ALU.is_equal)
        k.memset(self.ones_col_b[:], 1.0)
        k.memset(self.ones_col_f[:], 1.0)
        k.memset(self.ones_row_f[:], 1.0)
        k.memset(self.ones_row_b[:], 1.0)
        for d in range(4):
            m = self.masks[:, d * TT:(d + 1) * TT]
            k.ts(m, self.jrow[:], self.pcol[:], float(128 * d),
                 op0=ALU.subtract, op1=ALU.subtract)
            k.ts(m, m, 0.0, op0=ALU.is_ge)


class PrologueBufs:

    def __init__(self, p, st):
        nc = p.nc
        self.slab_f = [st.enter_context(nc.sbuf_tensor(f"pl_sf{i}", [128, 3072], F32))
                       for i in range(3)]
        self.slab_i = [st.enter_context(nc.sbuf_tensor(f"pl_si{i}", [128, 3072], I32))
                       for i in range(3)]
        self.slab_c = [st.enter_context(nc.sbuf_tensor(f"pl_sc{i}", [128, 3072], F32))
                       for i in range(1)]
        self.slab_b = [st.enter_context(nc.sbuf_tensor(f"pl_sb{i}", [128, 3072], BF16))
                       for i in range(3)]
        self.pst = [st.enter_context(nc.psum_tensor(f"pl_ps{i}", [128, 128], F32))
                    for i in range(2)]
        self.band = [st.enter_context(nc.sbuf_tensor(f"pl_band{i}", [128, 24 * 512], BF16))
                     for i in range(1)]


def emit_prep_transpose(p, k, C, src, dst, rows, cols, src_dtype, uid, B):
    nb = rows // 128
    ncol = cols // 128
    slab = B.slab_f if src_dtype == F32 else B.slab_i
    for b in range(nb):
        sl = slab[b % 3]
        p.dma("qs", sl[:, 0:cols], src[b * 128:(b + 1) * 128, :])
        if src_dtype != F32:
            k.cp(B.slab_c[0][:, 0:cols], sl[:, 0:cols])
            sf = B.slab_c[0]
        else:
            sf = sl
        g = 0
        for c in range(ncol):
            k.trans(B.pst[c % 2][:], sf[:, c * 128:(c + 1) * 128], C.ident[:])
            k.cp(B.band[g][:, c * 512 + (b % 4) * 128:c * 512 + (b % 4) * 128 + 128],
                 B.pst[c % 2][:])
        if b % 4 == 3 or b == nb - 1:
            b0 = (b // 4) * 4
            w = (b - b0 + 1) * 128
            for c in range(ncol):
                p.dma("qa", dst[c * 128:(c + 1) * 128, b0 * 128:b0 * 128 + w],
                      B.band[g][:, c * 512:c * 512 + w])


def emit_ln_fm(p, k, C, x_src, xn_dst, uid, mr_dst=None, sumx_fp32=False):
    from contextlib import ExitStack
    nc = p.nc
    with ExitStack() as stk:
        xt = [[stk.enter_context(nc.sbuf_tensor(f"{uid}_x{j}_{i}", [128, TT], F32))
               for i in range(DC)] for j in range(2)]
        xn = [[stk.enter_context(nc.sbuf_tensor(f"{uid}_xn{j}_{i}", [128, TT], BF16))
               for i in range(DC)] for j in range(2)]
        xb = [stk.enter_context(nc.sbuf_tensor(f"{uid}_xb{i}", [128, TT], BF16)) for i in range(DC)]
        sq = [stk.enter_context(nc.sbuf_tensor(f"{uid}_sq{i}", [128, TT], BF16)) for i in range(DC)]
        sqf = [stk.enter_context(nc.sbuf_tensor(f"{uid}_sqf{i}", [128, TT], F32)) for i in range(DC)]             if sumx_fp32 else None
        rows2 = [stk.enter_context(nc.sbuf_tensor(f"{uid}_rows{j}", [1, 6 * TT], F32)) for j in range(2)]
        tmp = [stk.enter_context(nc.sbuf_tensor(f"{uid}_tmp{j}", [128, TT], F32)) for j in range(2)]
        ps_st = [stk.enter_context(nc.psum_tensor(f"{uid}_ps_st{j}", [1, 2 * TT], F32)) for j in range(2)]
        ps_b = [stk.enter_context(nc.psum_tensor(f"{uid}_ps_b{j}", [128, 2 * TT], F32)) for j in range(2)]

        for tt in range(NTT):
            j = tt % 2
            rows = rows2[j]
            r_m = rows[:, 0:TT]; r_ms = rows[:, TT:2 * TT]; r_var = rows[:, 2 * TT:3 * TT]
            r_ln = rows[:, 3 * TT:4 * TT]; r_r = rows[:, 4 * TT:5 * TT]; r_mr = rows[:, 5 * TT:6 * TT]
            for dc in range(DC):
                p.dma("qs", xt[j][dc][:], x_src[dc * 128:(dc + 1) * 128, tt * TT:(tt + 1) * TT])
            for dc in range(DC):
                if sumx_fp32:
                    k.act(sqf[dc][:], xt[j][dc][:], AF.Square)
                else:
                    k.act(sq[dc][:], xt[j][dc][:], AF.Square)
                k.cp(xb[dc][:], xt[j][dc][:], eng="gpsimd")
            for dc in range(DC):
                if sumx_fp32:
                    k.mm(ps_st[j][:, 0:TT], C.ones_col_f[:], xt[j][dc][:], start=(dc == 0), stop=(dc == DC - 1))
                else:
                    k.mm(ps_st[j][:, 0:TT], C.ones_col_b[:], xb[dc][:], start=(dc == 0), stop=(dc == DC - 1))
            for dc in range(DC):
                if sumx_fp32:
                    k.mm(ps_st[j][:, TT:2 * TT], C.ones_col_f[:], sqf[dc][:], start=(dc == 0), stop=(dc == DC - 1))
                else:
                    k.mm(ps_st[j][:, TT:2 * TT], C.ones_col_b[:], sq[dc][:], start=(dc == 0), stop=(dc == DC - 1))
            k.ts(r_m, ps_st[j][:, 0:TT], 1.0 / D)
            k.ts(r_ms, ps_st[j][:, TT:2 * TT], 1.0 / D)
            k.tt(r_var, r_m, r_m, ALU.mult)
            k.tt(r_var, r_ms, r_var, ALU.subtract)
            k.act(r_ln, r_var, AF.Ln, bias=float(EPS))
            k.act(r_r, r_ln, AF.Exp, scale=-0.5)
            k.tt(r_mr, r_m, r_r, ALU.mult)
            k.mm(ps_b[j][:, 0:TT], C.ones_row_f[:], r_r)
            k.mm(ps_b[j][:, TT:2 * TT], C.ones_row_f[:], r_mr)
            if mr_dst is not None:
                p.dma("qa", mr_dst[0:1, tt * TT:(tt + 1) * TT], r_m)
                p.dma("qa", mr_dst[1:2, tt * TT:(tt + 1) * TT], r_r)
            for dc in range(DC):
                k.tt(tmp[j][:], xt[j][dc][:], ps_b[j][:, 0:TT], ALU.mult)
                k.tt(xn[j][dc][:], tmp[j][:], ps_b[j][:, TT:2 * TT], ALU.subtract)
                p.dma("qa", xn_dst[dc * 128:(dc + 1) * 128, tt * TT:(tt + 1) * TT], xn[j][dc][:])
    p.barrier()


def emit_block1(p, k, C, xin, xout, W, uid="b1"):
    from contextlib import ExitStack
    nc = p.nc
    wvT = get_prep(p, k, C, W, "wvT")
    woT = get_prep(p, k, C, W, "woT")

    with ExitStack() as stk:
        ND = 4
        xt = [stk.enter_context(nc.sbuf_tensor(f"{uid}_x{i}", [128, D], F32)) for i in range(ND)]
        xnb4 = [stk.enter_context(nc.sbuf_tensor(f"{uid}_xnb{i}", [128, D], BF16)) for i in range(ND)]
        sqd4 = [stk.enter_context(nc.sbuf_tensor(f"{uid}_sqd{i}", [128, D], BF16)) for i in range(ND)]
        cols4 = [stk.enter_context(nc.sbuf_tensor(f"{uid}_cols{i}", [128, 8], F32)) for i in range(ND)]
        xfb = [stk.enter_context(nc.sbuf_tensor(f"{uid}_xfb{i}", [128, 128], F32)) for i in range(4)]
        ps_cs = stk.enter_context(nc.psum_tensor(f"{uid}_ps_cs", [1, D], F32))
        ps_tr = stk.enter_context(nc.psum_tensor(f"{uid}_ps_tr", [128, 128], F32))
        ps_mv = stk.enter_context(nc.psum_tensor(f"{uid}_ps_mv", [1, D], F32))
        ps_c1 = stk.enter_context(nc.psum_tensor(f"{uid}_ps_c1", [128, 1], F32))

        for i in range(S // 128):
            x0 = xt[i % ND]
            xnb = xnb4[i % ND]
            sqd = sqd4[i % ND]
            cols = cols4[i % ND]
            s_col = cols[:, 0:1]; m_col = cols[:, 1:2]; acc = cols[:, 2:3]; v1 = cols[:, 3:4]
            m2 = cols[:, 4:5]; var = cols[:, 5:6]; rstd = cols[:, 6:7]; nmr = cols[:, 7:8]
            p.dma("qs", x0[:], xin[i * 128:(i + 1) * 128, :])
            k.red(s_col, x0[:], ALU.add)
            k.ts(m_col, s_col, 1.0 / D)
            k.act(sqd[:], x0[:], AF.Square, accum=acc)
            k.ts(v1, acc, 1.0 / D)
            k.tt(m2, m_col, m_col, ALU.mult)
            k.tt(var, v1, m2, ALU.subtract)
            k.act(var, var, AF.Ln, bias=float(EPS))
            k.act(rstd, var, AF.Exp, scale=-0.5)
            k.tt(nmr, m_col, rstd, ALU.mult)
            k.ts(nmr, nmr, -1.0)
            k.act(xnb[:], x0[:], AF.Identity, bias=nmr, scale=rstd)
            for c in range(8):
                k.trans(ps_tr[:], x0[:, c * 128:(c + 1) * 128], C.ident[:])
                f = xfb[c % 4]
                k.cp(f[:], ps_tr[:])
                p.dma("qs", xout[c * 128:(c + 1) * 128, i * 128:(i + 1) * 128], f[:])
            if i > 0:
                xp = xnb4[(i - 1) % ND]
                k.mm(ps_cs[:, 0:512], C.ones_col_b[:], xp[:, 0:512],
                     start=(i == 1), stop=False)
                k.mm(ps_cs[:, 512:1024], C.ones_col_b[:], xp[:, 512:1024],
                     start=(i == 1), stop=False)
            if i == S // 128 - 1:
                k.mm(ps_cs[:, 0:512], C.ones_col_b[:], xnb[:, 0:512],
                     start=False, stop=True)
                k.mm(ps_cs[:, 512:1024], C.ones_col_b[:], xnb[:, 512:1024],
                     start=False, stop=True)

        rows = stk.enter_context(nc.sbuf_tensor(f"{uid}_rows", [1, 3 * D], F32))
        cs_row = rows[:, 0:D]; vs_row = rows[:, D:2 * D]; or_row = rows[:, 2 * D:3 * D]
        ccols = stk.enter_context(nc.sbuf_tensor(f"{uid}_ccols", [128, 16], BF16))
        cs_colb = ccols[:, 0:8]; vs_colb = ccols[:, 8:16]
        or_col = stk.enter_context(nc.sbuf_tensor(f"{uid}_orcol", [128, 8], F32))
        wrow = [stk.enter_context(nc.sbuf_tensor(f"{uid}_w{i}", [128, D], BF16)) for i in range(2)]

        k.cp(cs_row, ps_cs[:])
        for c in range(8):
            k.trans(ps_c1[:], cs_row[0:1, c * 128:(c + 1) * 128], C.ident[0:1, 0:1])
            k.cp(cs_colb[:, c:c + 1], ps_c1[:])
        for dc in range(8):
            wb = wrow[dc % 2]
            p.dma("qs", wb[:], wvT[dc * 128:(dc + 1) * 128, :])
            k.mm(ps_mv[:, 0:512], cs_colb[:, dc:dc + 1], wb[:, 0:512],
                 start=(dc == 0), stop=(dc == 7))
            k.mm(ps_mv[:, 512:1024], cs_colb[:, dc:dc + 1], wb[:, 512:1024],
                 start=(dc == 0), stop=(dc == 7))
        k.cp(vs_row, ps_mv[:])
        for c in range(8):
            k.trans(ps_c1[:], vs_row[0:1, c * 128:(c + 1) * 128], C.ident[0:1, 0:1])
            k.cp(vs_colb[:, c:c + 1], ps_c1[:])
        for dc in range(8):
            wb = wrow[dc % 2]
            p.dma("qs", wb[:], woT[dc * 128:(dc + 1) * 128, :])
            k.mm(ps_mv[:, 0:512], vs_colb[:, dc:dc + 1], wb[:, 0:512],
                 start=(dc == 0), stop=(dc == 7))
            k.mm(ps_mv[:, 512:1024], vs_colb[:, dc:dc + 1], wb[:, 512:1024],
                 start=(dc == 0), stop=(dc == 7))
        k.cp(or_row, ps_mv[:])
        for c in range(8):
            k.trans(ps_c1[:], or_row[0:1, c * 128:(c + 1) * 128], C.ident[0:1, 0:1])
            k.cp(or_col[:, c:c + 1], ps_c1[:])

        rt = [stk.enter_context(nc.sbuf_tensor(f"{uid}_rt{i}", [128, TT], F32)) for i in range(2)]
        for dc in range(DC):
            for tt in range(NTT):
                bbuf = rt[tt % 2]
                p.dma("qs", bbuf[:], xout[dc * 128:(dc + 1) * 128, tt * TT:(tt + 1) * TT])
                k.ts(bbuf[:], bbuf[:], or_col[:, dc:dc + 1], op0=ALU.add)
                p.dma("qs", xout[dc * 128:(dc + 1) * 128, tt * TT:(tt + 1) * TT], bbuf[:])
    p.barrier()


def emit_l1ffn(p, k, C, xin, xout, W, uid, preps, last=False):
    from contextlib import ExitStack
    nc = p.nc
    wgT, wuT, wdT = preps["wgT"], preps["wuT"], preps["wdT"]
    xn_buf = nc.dram_tensor(f"{uid}_xn", [D, S], BF16, kind="Internal").ap()
    emit_ln_fm(p, k, C, xin, xn_buf, uid + "ln")
    HB = 3072
    HC = HB // 128
    INV = 1.0 / 127.0
    with ExitStack() as stk:
        wg_sb = [stk.enter_context(nc.sbuf_tensor(f"{uid}_wg{i}", [128, HB], BF16)) for i in range(DC)]
        wu_sb = [stk.enter_context(nc.sbuf_tensor(f"{uid}_wu{i}", [128, HB], BF16)) for i in range(DC)]
        wd_sb = [stk.enter_context(nc.sbuf_tensor(f"{uid}_wd{i}", [128, D], BF16)) for i in range(HC)]
        for dc in range(DC):
            p.dma("qs", wg_sb[dc][:], wgT[dc * 128:(dc + 1) * 128, :], group="wgu")
            p.dma("qs", wu_sb[dc][:], wuT[dc * 128:(dc + 1) * 128, :], group="wug")
        for hc in range(HC):
            p.dma("qs", wd_sb[hc][:], wdT[hc * 128:(hc + 1) * 128, :], group="wd")
        xn_sb = [[stk.enter_context(nc.sbuf_tensor(f"{uid}_xn{j}_{i}", [128, TT], BF16))
                  for i in range(DC)] for j in range(2)]
        sg = [stk.enter_context(nc.sbuf_tensor(f"{uid}_sg{j}", [128, TT], BF16)) for j in range(2)]
        h_sb = [stk.enter_context(nc.sbuf_tensor(f"{uid}_h{i}", [128, TT], BF16)) for i in range(HC)]
        xo = [stk.enter_context(nc.sbuf_tensor(f"{uid}_xo{i}", [128, TT], F32)) for i in range(2)]
        xr = [stk.enter_context(nc.sbuf_tensor(f"{uid}_xr{i}", [128, TT], F32)) for i in range(2)]
        yt = [stk.enter_context(nc.sbuf_tensor(f"{uid}_yt{i}", [128, 128], F32)) for i in range(2)]
        ps_gu = [stk.enter_context(nc.psum_tensor(f"{uid}_psgu{j}", [128, TT], F32)) for j in range(2)]
        ps_y = [stk.enter_context(nc.psum_tensor(f"{uid}_psy{i}", [128, 1024], F32)) for i in range(2)]
        ps_tr = [stk.enter_context(nc.psum_tensor(f"{uid}_pst{i}", [128, 128], F32)) for i in range(2)]             if last else None

        def py(oc):
            return ps_y[(oc % 4) // 2][:, (oc % 2) * TT:(oc % 2) * TT + TT]

        def evac(oc, tt):
            xrb = xr[oc % 2]
            p.dma("qs", xrb[:], xin[oc * 128:(oc + 1) * 128, tt * TT:(tt + 1) * TT])
            ob = xo[oc % 2]
            k.stt(ob[:], py(oc), INV, xrb[:], op0=ALU.mult, op1=ALU.add)
            if not last:
                p.dma("qa", xout[oc * 128:(oc + 1) * 128, tt * TT:(tt + 1) * TT], ob[:])
            else:
                for q in range(TT // 128):
                    k.trans(ps_tr[q % 2][:], ob[:, q * 128:(q + 1) * 128], C.ident[:])
                    k.cp(yt[q % 2][:], ps_tr[q % 2][:])
                    t0 = tt * TT + q * 128
                    p.dma("qa", xout[t0:t0 + 128, oc * 128:(oc + 1) * 128], yt[q % 2][:])

        for tt in range(NTT):
            j = tt % 2
            for dc in range(DC):
                p.dma("qs", xn_sb[j][dc][:],
                      xn_buf[dc * 128:(dc + 1) * 128, tt * TT:(tt + 1) * TT])
            for hc in range(HC):
                hj = hc % 2
                gu = ps_gu[hj]
                for dc in range(DC):
                    k.mm(gu[:], wg_sb[dc][:, hc * 128:(hc + 1) * 128], xn_sb[j][dc][:],
                         start=(dc == 0), stop=(dc == DC - 1))
                k.act(sg[hj][:], gu[:], AF.Silu, scale=INV)
                for dc in range(DC):
                    k.mm(gu[:], wu_sb[dc][:, hc * 128:(hc + 1) * 128], xn_sb[j][dc][:],
                         start=(dc == 0), stop=(dc == DC - 1))
                k.stt(h_sb[hc][:], gu[:], INV, sg[hj][:], op0=ALU.mult, op1=ALU.mult)
                for oc in range(4):
                    k.mm(py(oc), wd_sb[hc][:, oc * 128:(oc + 1) * 128], h_sb[hc][:],
                         start=(hc == 0), stop=(hc == HC - 1))
            for oc in range(4):
                evac(oc, tt)
            for oc in range(4, DC):
                for hc in range(HC):
                    k.mm(py(oc), wd_sb[hc][:, oc * 128:(oc + 1) * 128], h_sb[hc][:],
                         start=(hc == 0), stop=(hc == HC - 1))
            for oc in range(4, DC):
                evac(oc, tt)
    p.barrier()


INPUT_SPECS = {
    "L0_attn_a_x": ((8, 4096, 1024), F32),
    "L0_attn_a_w_qkv": ((3072, 1024), F32),
    "L0_attn_a_w_o": ((1024, 1024), F32),
    "L0_attn_b_W_qkv": ((1024, 3072), F32),
    "L0_attn_b_u_qkv": ((3072,), F32),
    "L0_attn_b_sigma_qkv": ((1,), F32),
    "L0_attn_b_W_proj": ((1024, 1024), F32),
    "L0_attn_b_u_proj": ((1024,), F32),
    "L0_attn_b_sigma_proj": ((1,), F32),
    "L0_ffn_gate_w": ((16, 1024), F32),
    "L0_ffn_gate_b": ((16,), F32),
    "L0_ffn_We1": ((16, 1024, 512), F32),
    "L0_ffn_be1": ((16, 512), F32),
    "L0_ffn_We2": ((16, 512, 1024), F32),
    "L0_ffn_be2": ((16, 1024), F32),
    "L0_ffn_We3": ((16, 1024, 512), F32),
    "L0_ffn_be3": ((16, 512), F32),
    "L0_ffn_ws1": ((1024, 1024), F32),
    "L0_ffn_bs1": ((1024,), F32),
    "L0_ffn_ws2": ((1024, 1024), F32),
    "L0_ffn_bs2": ((1024,), F32),
    "L0_ffn_ws3": ((1024, 1024), F32),
    "L0_ffn_bs3": ((1024,), F32),
    "L1_attn_a_Wq": ((16, 64, 1024), F32),
    "L1_attn_a_bq": ((16, 64), F32),
    "L1_attn_a_Wk": ((16, 64, 1024), F32),
    "L1_attn_a_bk": ((16, 64), F32),
    "L1_attn_a_Wv": ((16, 64, 1024), F32),
    "L1_attn_a_bv": ((16, 64), F32),
    "L1_attn_b_W_qkv": ((3072, 1024), F32),
    "L1_attn_b_W_proj": ((1024, 1024), F32),
    "L1_attn_b_b_proj": ((1024,), F32),
    "L1_ffn_w_gate": ((3072, 1024), I32),
    "L1_ffn_w_up": ((3072, 1024), I32),
    "L1_ffn_w_down": ((1024, 3072), I32),
}


def build_program(block_ids):
    p = P()
    k = K(p)
    C = Consts(p, k)
    nc = p.nc
    needed = set()
    defs = BLOCK_DEFS
    if block_ids[0] == 0:
        xin = nc.dram_tensor("L0_attn_a_x", [S, D], F32, kind="ExternalInput").ap()
        needed.add("L0_attn_a_x")
    else:
        xin = nc.dram_tensor("x_fm_in", [D, S], F32, kind="ExternalInput").ap()
    n_interior = len(block_ids) - 1
    interiors = [nc.dram_tensor(f"xi{i}", [D, S], F32, kind="Internal").ap()
                 for i in range(n_interior)]
    if block_ids[-1] == 7:
        xout_last = nc.dram_tensor("y", [S, D], F32, kind="ExternalOutput").ap()
    else:
        xout_last = nc.dram_tensor("x_fm_out", [D, S], F32, kind="ExternalOutput").ap()
    W = {}
    for bi in block_ids:
        for name in defs[bi][2]:
            if name not in W:
                shape, dt = INPUT_SPECS[name]
                W[name] = nc.dram_tensor(name, list(shape), dt, kind="ExternalInput").ap()
                needed.add(name)
    from contextlib import ExitStack
    with ExitStack() as plst:
        B = PrologueBufs(p, plst)
        if 0 in block_ids:
            for key in PREPS_FOR_BLOCK.get(0, []):
                get_prep(p, k, C, W, key, B)
        if block_ids and block_ids[0] == 0:
            xo0 = interiors[0] if n_interior > 0 else xout_last
            defs[0][1](p, k, C, xin, xo0, W, uid="B0")
        for bi in block_ids:
            for key in PREPS_FOR_BLOCK.get(bi, []):
                get_prep(p, k, C, W, key, B)
        for bi in block_ids:
            for key in CASTS_FOR_BLOCK.get(bi, []):
                get_cast(p, k, C, W, key, B)
    p.barrier()
    for j, bi in enumerate(block_ids):
        if bi == 0:
            continue
        xo = interiors[j] if j < n_interior else xout_last
        xi = xin if j == 0 else interiors[j - 1]
        defs[bi][1](p, k, C, xi, xo, W, uid=f"B{bi}")
        p.barrier()
    return p.build(), sorted(needed)


def load_vec_cols(p, k, C, src_vec, n, uid, stk, ps_c1):
    nc = p.nc
    row = stk.enter_context(nc.sbuf_tensor(f"{uid}_vr", [1, n], F32))
    col = stk.enter_context(nc.sbuf_tensor(f"{uid}_vc", [128, n // 128], F32))
    p.dma("qs", row[:], src_vec.rearrange("(a n) -> a n", a=1))
    for c in range(n // 128):
        k.trans(ps_c1[:], row[0:1, c * 128:(c + 1) * 128], C.ident[0:1, 0:1])
        k.cp(col[:, c:c + 1], ps_c1[:])
    return col


def emit_sn_scale(p, k, C, W_ap, u_ap, sigma_ap, n_out, uid, s_col):
    from contextlib import ExitStack
    nc = p.nc
    NP = n_out // 512
    with ExitStack() as stk:
        wch = [stk.enter_context(nc.sbuf_tensor(f"{uid}_w{i}", [128, n_out], F32)) for i in range(2)]
        urow = stk.enter_context(nc.sbuf_tensor(f"{uid}_u", [1, n_out], F32))
        ub = stk.enter_context(nc.sbuf_tensor(f"{uid}_ub", [128, n_out], F32))
        tmp = stk.enter_context(nc.sbuf_tensor(f"{uid}_t", [128, n_out], F32))
        vcol = stk.enter_context(nc.sbuf_tensor(f"{uid}_vcol", [128, 8], F32))
        sq8 = stk.enter_context(nc.sbuf_tensor(f"{uid}_sq8", [128, 8], F32))
        vs2 = stk.enter_context(nc.sbuf_tensor(f"{uid}_vs2", [128, 1], F32))
        u2row = stk.enter_context(nc.sbuf_tensor(f"{uid}_u2", [1, n_out], F32))
        sqrow = stk.enter_context(nc.sbuf_tensor(f"{uid}_sqr", [1, n_out], F32))
        sc = stk.enter_context(nc.sbuf_tensor(f"{uid}_sc", [1, 8], F32))
        nv2 = sc[:, 0:1]; nu2 = sc[:, 1:2]; lnv = sc[:, 2:3]; lnu = sc[:, 3:4]
        dif = sc[:, 4:5]; s0 = sc[:, 5:6]; sig = sc[:, 6:7]; s1 = sc[:, 7:8]
        ps_b = stk.enter_context(nc.psum_tensor(f"{uid}_psb", [128, 512], F32))
        ps_u2 = stk.enter_context(nc.psum_tensor(f"{uid}_psu", [1, n_out], F32))
        ps_s = ps_b[0:1, 0:1]
        ps_c = ps_b[:, 0:1]

        p.dma("qs", urow[:], u_ap.rearrange("(a n) -> a n", a=1))
        p.dma("qs", sig[0:1, :], sigma_ap.rearrange("(a n) -> a n", a=1))
        for pc in range(NP):
            k.mm(ps_b[:], C.ones_row_f[:], urow[:, pc * 512:(pc + 1) * 512])
            k.cp(ub[:, pc * 512:(pc + 1) * 512], ps_b[:])
        for rc in range(8):
            w = wch[rc % 2]
            p.dma("qs", w[:], W_ap[rc * 128:(rc + 1) * 128, :])
            k.tt(tmp[:], w[:], ub[:], ALU.mult)
            k.red(vcol[:, rc:rc + 1], tmp[:], ALU.add)
            for pc in range(NP):
                k.mm(ps_u2[:, pc * 512:(pc + 1) * 512], vcol[:, rc:rc + 1],
                     w[:, pc * 512:(pc + 1) * 512], start=(rc == 0), stop=(rc == 7))
        k.act(sq8[:], vcol[:], AF.Square, accum=vs2[:])
        k.mm(ps_s[:], vs2[:], C.ones_col_f[:])
        k.cp(nv2[0:1, :], ps_s[:])
        k.cp(u2row[:], ps_u2[:])
        k.act(sqrow[:], u2row[:], AF.Square, accum=nu2[0:1, :])
        k.act(lnv[0:1, :], nv2[0:1, :], AF.Ln)
        k.act(lnu[0:1, :], nu2[0:1, :], AF.Ln)
        k.tt(dif[0:1, :], lnv[0:1, :], lnu[0:1, :], ALU.subtract)
        k.act(s0[0:1, :], dif[0:1, :], AF.Exp, scale=self_half(k))
        k.tt(s1[0:1, :], s0[0:1, :], sig[0:1, :], ALU.mult)
        k.mm(ps_c[:], C.ones_row_f[:], s1[0:1, :])
        k.cp(s_col, ps_c[:])
    p.barrier()


def self_half(k):
    return 0.5


def emit_attention(p, k, C, xi, xo, W, uid, flavor):
    from contextlib import ExitStack
    nc = p.nc
    causal = flavor in ("l0b", "l1a")
    has_proj = flavor in ("l0b", "l1b")
    H, HD = 16, 64
    xn_buf = nc.dram_tensor(f"{uid}_xn", [D, S], BF16, kind="Internal").ap()
    q_buf = nc.dram_tensor(f"{uid}_q", [D, S], BF16, kind="Internal").ap()
    k_buf = nc.dram_tensor(f"{uid}_k", [D, S], BF16, kind="Internal").ap()
    ao_buf = nc.dram_tensor(f"{uid}_ao", [D, S], BF16, kind="Internal").ap() if has_proj else None
    emit_ln_fm(p, k, C, xi, xn_buf, uid + "ln")

    with ExitStack() as outer:
        v_aug = outer.enter_context(nc.sbuf_tensor(f"{uid}_vaug", [128, 32 * H * 65], BF16))
        vv4 = v_aug.ap().rearrange("p (t h j) -> p t h j", h=H, j=65)
        k.memset(vv4[:, :, :, 64:65], 1.0)
        scol = outer.enter_context(nc.sbuf_tensor(f"{uid}_scol", [128, 4], F32))
        s_q = scol[:, 0:1]; s_k = scol[:, 1:2]; s_p = scol[:, 2:3]

        if flavor == "l0b":
            emit_sn_scale(p, k, C, W["L0_attn_b_W_qkv"], W["L0_attn_b_u_qkv"],
                          W["L0_attn_b_sigma_qkv"], 3072, uid + "snq", s_k)
            emit_sn_scale(p, k, C, W["L0_attn_b_W_proj"], W["L0_attn_b_u_proj"],
                          W["L0_attn_b_sigma_proj"], 1024, uid + "snp", s_p)
            k.ts(s_q, s_k, 0.125)

        with ExitStack() as stk:
            wqkv = [stk.enter_context(nc.sbuf_tensor(f"{uid}_wqkv{i}", [128, 3072], BF16))
                    for i in range(DC)]
            if flavor == "l0b":
                wqkv0c = get_cast(p, k, C, W, "wqkv0c")
                for dc in range(DC):
                    p.dma("qs", wqkv[dc][:], wqkv0c[dc * 128:(dc + 1) * 128, :])
            elif flavor == "l1b":
                qkvT = get_prep(p, k, C, W, "qkvT1b")
                for dc in range(DC):
                    p.dma("qs", wqkv[dc][:], qkvT[dc * 128:(dc + 1) * 128, :])
            else:
                WqT = get_prep(p, k, C, W, "WqT")
                WkT = get_prep(p, k, C, W, "WkT")
                WvT = get_prep(p, k, C, W, "WvT")
                for dc in range(DC):
                    p.dma("qs", wqkv[dc][:, 0:1024], WqT[dc * 128:(dc + 1) * 128, :])
                    p.dma("qs", wqkv[dc][:, 1024:2048], WkT[dc * 128:(dc + 1) * 128, :])
                    p.dma("qs", wqkv[dc][:, 2048:3072], WvT[dc * 128:(dc + 1) * 128, :])
            bqc = bkc = bvb = None
            if flavor == "l1a":
                ps_c1 = stk.enter_context(nc.psum_tensor(f"{uid}_psc1", [128, 1], F32))
                bqc = load_vec_cols(p, k, C, W["L1_attn_a_bq"].rearrange("h d -> (h d)"),
                                    D, uid + "bq", stk, ps_c1)
                bkc = load_vec_cols(p, k, C, W["L1_attn_a_bk"].rearrange("h d -> (h d)"),
                                    D, uid + "bk", stk, ps_c1)
                bvr = stk.enter_context(nc.sbuf_tensor(f"{uid}_bvr", [1, D], F32))
                bvb = stk.enter_context(nc.sbuf_tensor(f"{uid}_bvb", [128, D], F32))
                ps_bb = stk.enter_context(nc.psum_tensor(f"{uid}_psbb", [128, D], F32))
                p.dma("qs", bvr[:], W["L1_attn_a_bv"].rearrange("h d -> (h d)").rearrange("(a n) -> a n", a=1))
                k.mm(ps_bb[:, 0:512], C.ones_row_f[:], bvr[:, 0:512])
                k.mm(ps_bb[:, 512:1024], C.ones_row_f[:], bvr[:, 512:1024])
                k.cp(bvb[:], ps_bb[:])

            xn_sb2 = [[stk.enter_context(nc.sbuf_tensor(f"{uid}_pxn{j}_{i}", [128, TT], BF16))
                       for i in range(DC)] for j in range(2)]
            qe2 = [stk.enter_context(nc.sbuf_tensor(f"{uid}_qe{j}", [128, TT], BF16)) for j in range(2)]
            ke2 = [stk.enter_context(nc.sbuf_tensor(f"{uid}_ke{j}", [128, TT], BF16)) for j in range(2)]
            ps_p = stk.enter_context(nc.psum_tensor(f"{uid}_psp", [128, 2 * TT], F32))
            ps_v = stk.enter_context(nc.psum_tensor(f"{uid}_psv", [128, 2 * TT], F32))
            for tt in range(NTT):
                xn_sb = xn_sb2[tt % 2]
                for dc in range(DC):
                    p.dma("qs", xn_sb[dc][:], xn_buf[dc * 128:(dc + 1) * 128, tt * TT:(tt + 1) * TT])
                for fc in range(16):
                    pp = ps_p[:, (fc % 2) * TT:(fc % 2) * TT + TT]
                    for dc in range(DC):
                        k.mm(pp, wqkv[dc][:, fc * 128:(fc + 1) * 128], xn_sb[dc][:],
                             start=(dc == 0), stop=(dc == DC - 1))
                    isq = fc < 8
                    dstb = (qe2 if isq else ke2)[(fc // 2) % 2]
                    if flavor == "l0b":
                        k.ts(dstb[:], pp, s_q if isq else s_k, op0=ALU.mult)
                    elif flavor == "l1a":
                        col = (bqc if isq else bkc)[:, fc % 8:fc % 8 + 1]
                        if isq:
                            k.ts(dstb[:], pp, col, 0.125, op0=ALU.add, op1=ALU.mult)
                        else:
                            k.ts(dstb[:], pp, col, op0=ALU.add)
                    else:
                        if isq:
                            k.ts(dstb[:], pp, 0.125, op0=ALU.mult)
                        else:
                            k.cp(dstb[:], pp)
                    buf = q_buf if isq else k_buf
                    p.dma("qa", buf[(fc % 8) * 128:(fc % 8) * 128 + 128, tt * TT:(tt + 1) * TT], dstb[:])
                for st in range(4):
                    ktile = tt * 4 + st
                    for vv in range(2):
                        pv = ps_v[:, vv * TT:(vv + 1) * TT]
                        for dc in range(DC):
                            k.mm(pv, xn_sb[dc][:, st * 128:(st + 1) * 128],
                                 wqkv[dc][:, 2048 + vv * 512:2048 + (vv + 1) * 512],
                                 start=(dc == 0), stop=(dc == DC - 1))
                        dst = vv4[:, ktile, vv * 8:(vv + 1) * 8, 0:64]
                        pvr = pv.rearrange("p (h j) -> p h j", j=64)
                        if flavor == "l0b":
                            k.ts(dst, pvr, s_k, op0=ALU.mult)
                        elif flavor == "l1a":
                            k.tt(dst, pvr, bvb[:, vv * 512:(vv + 1) * 512].rearrange(
                                "p (h j) -> p h j", j=64), ALU.add)
                        else:
                            k.cp(dst, pvr)
        p.barrier()

        with ExitStack() as stk:
            q2 = [stk.enter_context(nc.sbuf_tensor(f"{uid}_qh{j}", [128, S], BF16)) for j in range(2)]
            k2 = [stk.enter_context(nc.sbuf_tensor(f"{uid}_kh{j}", [128, S], BF16)) for j in range(2)]
            aA = [stk.enter_context(nc.sbuf_tensor(f"{uid}_aA{i}", [128, 2 * TT], BF16)) for i in range(2)]
            aB = [stk.enter_context(nc.sbuf_tensor(f"{uid}_aB{i}", [128, 2 * TT], BF16)) for i in range(2)]
            t1 = [stk.enter_context(nc.sbuf_tensor(f"{uid}_t1{j}", [64, TT], F32)) for j in range(2)]
            t2 = [stk.enter_context(nc.sbuf_tensor(f"{uid}_t2{j}", [64, TT], F32)) for j in range(2)]
            aoe = [stk.enter_context(nc.sbuf_tensor(f"{uid}_aoe{j}", [64, TT], BF16)) for j in range(2)]
            xrow = [stk.enter_context(nc.sbuf_tensor(f"{uid}_xrow{j}", [64, TT], F32)) for j in range(2)]
            lrow = [stk.enter_context(nc.sbuf_tensor(f"{uid}_lrow{j}", [1, 2 * TT], F32)) for j in range(2)]
            ps_sA = stk.enter_context(nc.psum_tensor(f"{uid}_psA", [128, 2 * TT], F32))
            ps_sB = stk.enter_context(nc.psum_tensor(f"{uid}_psB", [128, 2 * TT], F32))
            ps_o = stk.enter_context(nc.psum_tensor(f"{uid}_pso", [128, 2 * TT], F32))
            ps_r = stk.enter_context(nc.psum_tensor(f"{uid}_psr", [64, 2 * TT], F32))
            for hp in range(H // 2):
                hj = hp % 2
                p.dma("qs", q2[hj][:], q_buf[hp * 128:(hp + 1) * 128, :])
                p.dma("qs", k2[hj][:], k_buf[hp * 128:(hp + 1) * 128, :])
                qs0, qs1 = q2[hj][0:64, :], q2[hj][64:128, :]
                ks0, ks1 = k2[hj][0:64, :], k2[hj][64:128, :]
                for qt in range(NTT):
                    nkt = (qt + 1) * 4 if causal else 32
                    qsl = slice(qt * TT, (qt + 1) * TT)

                    def st_stage(ktp):
                        kt0, kt1 = 2 * ktp, 2 * ktp + 1
                        j = ktp % 2
                        k.mm(ps_sA[:, 0:TT], ks0[:, kt0 * 128:(kt0 + 1) * 128], qs0[:, qsl])
                        k.mm(ps_sA[:, TT:2 * TT], ks0[:, kt1 * 128:(kt1 + 1) * 128], qs0[:, qsl])
                        k.mm(ps_sB[:, 0:TT], ks1[:, kt0 * 128:(kt0 + 1) * 128], qs1[:, qsl])
                        k.mm(ps_sB[:, TT:2 * TT], ks1[:, kt1 * 128:(kt1 + 1) * 128], qs1[:, qsl])
                        k.act(aA[j][:], ps_sA[:], AF.Exp)
                        k.act(aB[j][:], ps_sB[:], AF.Exp)
                        if causal and kt1 >= 4 * qt:
                            dlt = kt0 - 4 * qt
                            msl = C.masks[:, dlt * TT:(dlt + 2) * TT]
                            k.tt(aA[j][:], aA[j][:], msl, ALU.mult)
                            k.tt(aB[j][:], aB[j][:], msl, ALU.mult, eng="gpsimd")

                    def av_stage(ktp, last):
                        kt0, kt1 = 2 * ktp, 2 * ktp + 1
                        j = ktp % 2
                        k.mm(ps_o[0:65, 0:TT], vv4[:, kt0, 2 * hp, :], aA[j][:, 0:TT],
                             start=(ktp == 0), stop=False)
                        k.mm(ps_o[0:65, 0:TT], vv4[:, kt1, 2 * hp, :], aA[j][:, TT:2 * TT],
                             start=False, stop=last)
                        k.mm(ps_o[0:65, TT:2 * TT], vv4[:, kt0, 2 * hp + 1, :], aB[j][:, 0:TT],
                             start=(ktp == 0), stop=False)
                        k.mm(ps_o[0:65, TT:2 * TT], vv4[:, kt1, 2 * hp + 1, :], aB[j][:, TT:2 * TT],
                             start=False, stop=last)

                    nktp = nkt // 2
                    st_stage(0)
                    for ktp in range(1, nktp):
                        st_stage(ktp)
                        av_stage(ktp - 1, False)
                    av_stage(nktp - 1, True)
                    for sub in range(2):
                        h = 2 * hp + sub
                        osl = ps_o[:, sub * TT:(sub + 1) * TT]
                        ps_rv = ps_r[:, sub * TT:(sub + 1) * TT]
                        l_ln = lrow[sub][:, 0:TT]; r_row = lrow[sub][:, TT:2 * TT]
                        k.act(l_ln, osl[64:65, :], AF.Ln)
                        k.act(r_row, l_ln, AF.Exp, scale=-1.0)
                        k.mm(ps_rv, C.ones_row_f[0:1, 0:64], r_row)
                        k.cp(t1[sub][:], osl[0:64, :])
                        if has_proj:
                            k.tt(aoe[sub][:], t1[sub][:], ps_rv, ALU.mult)
                            p.dma("qs", ao_buf[h * 64:(h + 1) * 64, qt * TT:(qt + 1) * TT],
                                  aoe[sub][:])
                        else:
                            k.tt(t2[sub][:], t1[sub][:], ps_rv, ALU.mult)
                            p.dma("qs", xrow[sub][:], xi[h * 64:(h + 1) * 64, qt * TT:(qt + 1) * TT])
                            k.tt(t2[sub][:], t2[sub][:], xrow[sub][:], ALU.add)
                            p.dma("qa", xo[h * 64:(h + 1) * 64, qt * TT:(qt + 1) * TT], t2[sub][:])
        p.barrier()

    if has_proj:
        with ExitStack() as stk:
            wp = [stk.enter_context(nc.sbuf_tensor(f"{uid}_wp{i}", [128, D], BF16))
                  for i in range(DC)]
            ps_c2 = stk.enter_context(nc.psum_tensor(f"{uid}_psc2", [128, 1], F32))
            bpc = None
            if flavor == "l0b":
                wproj0c = get_cast(p, k, C, W, "wproj0c")
                for fc in range(DC):
                    p.dma("qs", wp[fc][:], wproj0c[fc * 128:(fc + 1) * 128, :])
                spc = nc.alloc_sbuf_tensor(f"{uid}_spc", [128, 1], F32, side="right")
                k.cp(spc[:], s_p)
            else:
                projT = get_prep(p, k, C, W, "projT1b")
                for fc in range(DC):
                    p.dma("qs", wp[fc][:], projT[fc * 128:(fc + 1) * 128, :])
                bpc = load_vec_cols(p, k, C, W["L1_attn_b_b_proj"], D, uid + "bp", stk, ps_c2)
            ao_sb2 = [[stk.enter_context(nc.sbuf_tensor(f"{uid}_aos{j}_{i}", [128, TT], BF16))
                       for i in range(DC)] for j in range(2)]
            xr = [stk.enter_context(nc.sbuf_tensor(f"{uid}_pxr{i}", [128, TT], F32)) for i in range(2)]
            ob = [stk.enter_context(nc.sbuf_tensor(f"{uid}_pob{i}", [128, TT], F32)) for i in range(2)]
            ps_y = stk.enter_context(nc.psum_tensor(f"{uid}_psy", [128, 2 * TT], F32))
            for tt in range(NTT):
                ao_sb = ao_sb2[tt % 2]
                for fc in range(DC):
                    p.dma("qs", ao_sb[fc][:], ao_buf[fc * 128:(fc + 1) * 128, tt * TT:(tt + 1) * TT])
                for oc in range(DC):
                    py = ps_y[:, (oc % 2) * TT:(oc % 2) * TT + TT]
                    for fc in range(DC):
                        k.mm(py, wp[fc][:, oc * 128:(oc + 1) * 128], ao_sb[fc][:],
                             start=(fc == 0), stop=(fc == DC - 1))
                    xrb = xr[oc % 2]
                    p.dma("qs", xrb[:], xi[oc * 128:(oc + 1) * 128, tt * TT:(tt + 1) * TT])
                    obb = ob[oc % 2]
                    if flavor == "l0b":
                        k.stt(obb[:], py, spc[:], xrb[:], op0=ALU.mult, op1=ALU.add)
                    else:
                        k.stt(obb[:], py, bpc[:, oc:oc + 1], xrb[:], op0=ALU.add, op1=ALU.add)
                    p.dma("qa", xo[oc * 128:(oc + 1) * 128, tt * TT:(tt + 1) * TT], obb[:])
    p.barrier()


def emit_l0ffn(p, k, C, xi, xo, W, uid):
    from contextlib import ExitStack
    nc = p.nc
    E, G, INTER = 16, 4, 512
    IC = INTER // 128
    BIGN = -1.0e30
    xn_buf = nc.dram_tensor(f"{uid}_xn", [D, S], BF16, kind="Internal").ap()
    h_buf = nc.dram_tensor(f"{uid}_h", [E * INTER, S], BF16, kind="Internal").ap()
    zh_buf = nc.dram_tensor(f"{uid}_zh", [D, S], BF16, kind="Internal").ap()
    mr_buf = nc.dram_tensor(f"{uid}_mr", [2, S], F32, kind="Internal").ap()
    emit_ln_fm(p, k, C, xi, xn_buf, uid + "ln", mr_dst=mr_buf, sumx_fp32=True)

    comb_dram = nc.dram_tensor(f"{uid}_comb", [16, S], BF16, kind="Internal").ap()
    with ExitStack() as outer:

        with ExitStack() as stk:
            gw = stk.enter_context(nc.sbuf_tensor(f"{uid}_gw", [16, D], F32))
            gwT = [stk.enter_context(nc.sbuf_tensor(f"{uid}_gwT{i}", [128, 16], F32))
                   for i in range(DC)]
            gbb = stk.enter_context(nc.sbuf_tensor(f"{uid}_gbb", [128, 16], F32))
            csb = stk.enter_context(nc.sbuf_tensor(f"{uid}_csb0", [128, 16], F32))
            gbr = stk.enter_context(nc.sbuf_tensor(f"{uid}_gbr", [1, 16], F32))
            gcs = stk.enter_context(nc.sbuf_tensor(f"{uid}_gcs", [16, 1], F32))
            gcsr = stk.enter_context(nc.sbuf_tensor(f"{uid}_gcsr", [1, 16], F32))
            mrow = stk.enter_context(nc.sbuf_tensor(f"{uid}_mrow", [1, TT], F32))
            rrow = stk.enter_context(nc.sbuf_tensor(f"{uid}_rrow", [1, TT], F32))
            mrc = stk.enter_context(nc.sbuf_tensor(f"{uid}_mrc", [128, 8], F32))
            sc_t = stk.enter_context(nc.sbuf_tensor(f"{uid}_sct", [128, 16], F32))
            xn_sb2 = [[stk.enter_context(nc.sbuf_tensor(f"{uid}_g_xn{j}_{i}", [128, TT], F32))
                       for i in range(DC)] for j in range(2)]
            wt = stk.enter_context(nc.sbuf_tensor(f"{uid}_wt", [128, 24], F32))
            esc = stk.enter_context(nc.sbuf_tensor(f"{uid}_esc", [128, 16], F32))
            orig = stk.enter_context(nc.sbuf_tensor(f"{uid}_orig", [128, 16], F32))
            biased = stk.enter_context(nc.sbuf_tensor(f"{uid}_biased", [128, 16], F32))
            sgp = stk.enter_context(nc.sbuf_tensor(f"{uid}_sgp", [128, 16], F32))
            m3 = stk.enter_context(nc.sbuf_tensor(f"{uid}_m3", [128, 16], F32))
            me = stk.enter_context(nc.sbuf_tensor(f"{uid}_me", [128, 16], F32))
            cmb = stk.enter_context(nc.sbuf_tensor(f"{uid}_cmb", [128, 16], F32))
            gwork = stk.enter_context(nc.sbuf_tensor(f"{uid}_gwork", [128, 16], F32))
            ctmp = stk.enter_context(nc.sbuf_tensor(f"{uid}_ctmp", [16, 128], BF16))
            ps_g = stk.enter_context(nc.psum_tensor(f"{uid}_psg0", [128, 16], F32))
            ps_tr = stk.enter_context(nc.psum_tensor(f"{uid}_pstr0", [128, 128], F32))
            ps_b0 = stk.enter_context(nc.psum_tensor(f"{uid}_psb0", [128, 16], F32))
            c_m1 = wt[:, 0:1]; c_nm1 = wt[:, 1:2]; c_ss = wt[:, 2:3]; c_rs = wt[:, 3:4]
            c_G1 = wt[:, 4:5]; c_G2 = wt[:, 5:6]; c_T1 = wt[:, 6:7]; c_T2 = wt[:, 7:8]
            gm1 = wt[:, 8:12]; gs = wt[:, 12:16]; kp = wt[:, 16:20]; pen = wt[:, 20:24]

            p.dma("qs", gw[:], W["L0_ffn_gate_w"])
            p.dma("qs", gbr[:], W["L0_ffn_gate_b"].rearrange("(a n) -> a n", a=1))
            for dc in range(DC):
                k.trans(ps_tr[0:128, 0:16], gw[:, dc * 128:(dc + 1) * 128], C.ident[0:16, 0:16])
                k.cp(gwT[dc][:], ps_tr[0:128, 0:16])
            k.mm(ps_b0[:], C.ones_row_f[:], gbr[:])
            k.cp(gbb[:], ps_b0[:])
            k.red(gcs[:, 0:1], gw[:], ALU.add)
            k.trans(ps_tr[0:1, 0:16], gcs[0:16, 0:1], C.ident[0:16, 0:16])
            k.cp(gcsr[:], ps_tr[0:1, 0:16])
            k.mm(ps_b0[:], C.ones_row_f[:], gcsr[:])
            k.cp(csb[:], ps_b0[:])
            for tt in range(NTT):
                xn_sb = xn_sb2[tt % 2]
                for dc in range(DC):
                    p.dma("qs", xn_sb[dc][:], xi[dc * 128:(dc + 1) * 128, tt * TT:(tt + 1) * TT])
                p.dma("qa", mrow[:], mr_buf[0:1, tt * TT:(tt + 1) * TT])
                p.dma("qa", rrow[:], mr_buf[1:2, tt * TT:(tt + 1) * TT])
                for st in range(4):
                    k.trans(ps_tr[0:128, 0:1], mrow[0:1, st * 128:(st + 1) * 128], C.ident[0:1, 0:1])
                    k.cp(mrc[:, st:st + 1], ps_tr[0:128, 0:1])
                    k.trans(ps_tr[0:128, 0:1], rrow[0:1, st * 128:(st + 1) * 128], C.ident[0:1, 0:1])
                    k.cp(mrc[:, 4 + st:5 + st], ps_tr[0:128, 0:1])
                for st in range(4):
                    for dc in range(DC):
                        k.mm(ps_g[:], xn_sb[dc][:, st * 128:(st + 1) * 128], gwT[dc][:],
                             start=(dc == 0), stop=(dc == DC - 1))
                    k.ts(sc_t[:], csb[:], mrc[:, st:st + 1], op0=ALU.mult)
                    k.tt(sc_t[:], ps_g[:], sc_t[:], ALU.subtract)
                    k.ts(sc_t[:], sc_t[:], mrc[:, 4 + st:5 + st], op0=ALU.mult)
                    k.red(c_m1, sc_t[:], ALU.max)
                    k.ts(c_nm1, c_m1, -1.0)
                    k.act(esc[:], sc_t[:], AF.Exp, bias=c_nm1)
                    k.red(c_ss, esc[:], ALU.add)
                    k.act(c_rs, c_ss, AF.Ln)
                    k.act(c_rs, c_rs, AF.Exp, scale=-1.0)
                    k.ts(orig[:], esc[:], c_rs, op0=ALU.mult)
                    k.tt(biased[:], orig[:], gbb[:], ALU.add)
                    for g in range(G):
                        sl = biased[:, 4 * g:4 * g + 4]
                        k.red(gm1[:, g:g + 1], sl, ALU.max)
                        k.ts(me[:, 0:4], sl, gm1[:, g:g + 1], op0=ALU.is_equal)
                        k.stt(gwork[:, 0:4], me[:, 0:4], BIGN, sl, op0=ALU.mult, op1=ALU.add)
                        k.red(gs[:, g:g + 1], gwork[:, 0:4], ALU.max)
                        k.tt(gs[:, g:g + 1], gs[:, g:g + 1], gm1[:, g:g + 1], ALU.add)
                    k.red(c_G1, gs[:, 0:4], ALU.max)
                    k.ts(me[:, 0:4], gs[:, 0:4], c_G1, op0=ALU.is_equal)
                    k.stt(gwork[:, 0:4], me[:, 0:4], BIGN, gs[:, 0:4], op0=ALU.mult, op1=ALU.add)
                    k.red(c_G2, gwork[:, 0:4], ALU.max)
                    k.ts(kp[:, 0:4], gs[:, 0:4], c_G2, op0=ALU.is_ge)
                    k.ts(pen[:, 0:4], kp[:, 0:4], 1.0, -BIGN, op0=ALU.subtract, op1=ALU.mult)
                    for g in range(G):
                        k.ts(sgp[:, 4 * g:4 * g + 4], biased[:, 4 * g:4 * g + 4],
                             pen[:, g:g + 1], op0=ALU.add)
                    k.red(c_T1, sgp[:], ALU.max)
                    k.ts(me[:], sgp[:], c_T1, op0=ALU.is_equal)
                    k.stt(m3[:], me[:], BIGN, sgp[:], op0=ALU.mult, op1=ALU.add)
                    k.red(c_T2, m3[:], ALU.max)
                    k.ts(m3[:], m3[:], c_T2, op0=ALU.is_equal)
                    k.tt(me[:], me[:], m3[:], ALU.add)
                    k.tt(cmb[:], me[:], orig[:], ALU.mult)
                    k.trans(ps_tr[0:16, 0:128], cmb[:], C.ident[:])
                    k.cp(ctmp[:], ps_tr[0:16, 0:128])
                    p.dma("qs", comb_dram[:, (tt * 4 + st) * 128:(tt * 4 + st + 1) * 128],
                          ctmp[:])
        p.barrier()

        with ExitStack() as stk:
            ws1 = [stk.enter_context(nc.sbuf_tensor(f"{uid}_ws1_{i}", [128, D], BF16))
                   for i in range(DC)]
            ws3 = [stk.enter_context(nc.sbuf_tensor(f"{uid}_ws3_{i}", [128, D], BF16))
                   for i in range(DC)]
            ps_c1 = stk.enter_context(nc.psum_tensor(f"{uid}_zc1", [128, 1], F32))
            bs1c = load_vec_cols(p, k, C, W["L0_ffn_bs1"], D, uid + "b1", stk, ps_c1)
            bs3c = load_vec_cols(p, k, C, W["L0_ffn_bs3"], D, uid + "b3", stk, ps_c1)
            xn_sb2 = [[stk.enter_context(nc.sbuf_tensor(f"{uid}_z_xn{j}_{i}", [128, TT], BF16))
                       for i in range(DC)] for j in range(2)]
            sg1r = [stk.enter_context(nc.sbuf_tensor(f"{uid}_zsg{j}", [128, TT], BF16)) for j in range(2)]
            ztr = [stk.enter_context(nc.sbuf_tensor(f"{uid}_zt{j}", [128, TT], BF16)) for j in range(2)]
            ps_1r = [stk.enter_context(nc.psum_tensor(f"{uid}_zp1{j}", [128, TT], F32)) for j in range(2)]
            ps_3r = [stk.enter_context(nc.psum_tensor(f"{uid}_zp3{j}", [128, TT], F32)) for j in range(2)]
            ws1c = get_cast(p, k, C, W, "ws1c")
            ws3c = get_cast(p, k, C, W, "ws3c")
            for dc in range(DC):
                p.dma("qs", ws1[dc][:], ws1c[dc * 128:(dc + 1) * 128, :], group="ws13")
                p.dma("qs", ws3[dc][:], ws3c[dc * 128:(dc + 1) * 128, :], group="ws13")
            for tt in range(NTT):
                xn_sb = xn_sb2[tt % 2]
                for dc in range(DC):
                    p.dma("qs", xn_sb[dc][:], xn_buf[dc * 128:(dc + 1) * 128, tt * TT:(tt + 1) * TT])
                for ic in range(DC):
                    ij = ic % 2
                    ps_1, ps_3, sg1, zt = ps_1r[ij], ps_3r[ij], sg1r[ij], ztr[ij]
                    for dc in range(DC):
                        k.mm(ps_1[:], ws1[dc][:, ic * 128:(ic + 1) * 128], xn_sb[dc][:],
                             start=(dc == 0), stop=(dc == DC - 1))
                    k.act(sg1[:], ps_1[:], AF.Silu, bias=bs1c[:, ic:ic + 1])
                    for dc in range(DC):
                        k.mm(ps_3[:], ws3[dc][:, ic * 128:(ic + 1) * 128], xn_sb[dc][:],
                             start=(dc == 0), stop=(dc == DC - 1))
                    k.stt(zt[:], ps_3[:], bs3c[:, ic:ic + 1], sg1[:], op0=ALU.add, op1=ALU.mult)
                    p.dma("qa", zh_buf[ic * 128:(ic + 1) * 128, tt * TT:(tt + 1) * TT], zt[:])
        p.barrier()

        with ExitStack() as stk:
            xn_all = [stk.enter_context(nc.sbuf_tensor(f"{uid}_e_xn{i}", [128, S], BF16))
                      for i in range(DC)]
            for dc in range(DC):
                p.dma("qs", xn_all[dc][:], xn_buf[dc * 128:(dc + 1) * 128, :], group="xnall")
            ps_c1 = stk.enter_context(nc.psum_tensor(f"{uid}_ec1", [128, 1], F32))
            be1c = stk.enter_context(nc.sbuf_tensor(f"{uid}_be1c", [128, 4 * E], F32))
            be3c = stk.enter_context(nc.sbuf_tensor(f"{uid}_be3c", [128, 4 * E], F32))
            ber = stk.enter_context(nc.sbuf_tensor(f"{uid}_ber", [1, INTER], F32))
            for e in range(E):
                p.dma("qs", ber[:], W["L0_ffn_be1"][e:e + 1, :])
                for c in range(IC):
                    k.trans(ps_c1[:], ber[0:1, c * 128:(c + 1) * 128], C.ident[0:1, 0:1])
                    k.cp(be1c[:, e * IC + c:e * IC + c + 1], ps_c1[:])
                p.dma("qs", ber[:], W["L0_ffn_be3"][e:e + 1, :])
                for c in range(IC):
                    k.trans(ps_c1[:], ber[0:1, c * 128:(c + 1) * 128], C.ident[0:1, 0:1])
                    k.cp(be3c[:, e * IC + c:e * IC + c + 1], ps_c1[:])
            we1 = [[stk.enter_context(nc.sbuf_tensor(f"{uid}_we1_{j}_{i}", [128, INTER], BF16))
                    for i in range(DC)] for j in range(2)]
            we3 = [[stk.enter_context(nc.sbuf_tensor(f"{uid}_we3_{j}_{i}", [128, INTER], BF16))
                    for i in range(DC)] for j in range(2)]
            c_sb = [stk.enter_context(nc.sbuf_tensor(f"{uid}_ecsb{j}", [128, TT], F32)) for j in range(2)]
            crow = [stk.enter_context(nc.sbuf_tensor(f"{uid}_crow{j}", [1, S], BF16)) for j in range(2)]
            sg1r = [stk.enter_context(nc.sbuf_tensor(f"{uid}_esg{j}", [128, TT], BF16)) for j in range(2)]
            hhr = [stk.enter_context(nc.sbuf_tensor(f"{uid}_ehh{j}", [128, TT], F32)) for j in range(2)]
            hbtr = [stk.enter_context(nc.sbuf_tensor(f"{uid}_ehb{j}", [128, TT], BF16)) for j in range(2)]
            ps_1r = [stk.enter_context(nc.psum_tensor(f"{uid}_ep1{j}", [128, TT], F32)) for j in range(2)]
            ps_3r = [stk.enter_context(nc.psum_tensor(f"{uid}_ep3{j}", [128, TT], F32)) for j in range(2)]
            ps_c = [stk.enter_context(nc.psum_tensor(f"{uid}_epc{j}", [128, TT], F32)) for j in range(2)]
            We1c = get_cast(p, k, C, W, "We1c")
            We3c = get_cast(p, k, C, W, "We3c")
            for dc in range(DC):
                p.dma("qs", we1[0][dc][:], We1c[dc * 128:(dc + 1) * 128, :], group="we13a")
                p.dma("qs", we3[0][dc][:], We3c[dc * 128:(dc + 1) * 128, :], group="we13a")
            p.dma("qs", crow[0][:], comb_dram[0:1, :])
            for e in range(E):
                ej = e % 2
                if e + 1 < E:
                    p.dma("qs", crow[1 - ej][:], comb_dram[e + 1:e + 2, :])
                if e + 1 < E:
                    for dc in range(DC):
                        p.dma("qs", we1[1 - ej][dc][:],
                              We1c[(e + 1) * 1024 + dc * 128:(e + 1) * 1024 + (dc + 1) * 128, :],
                              group="we13" + ("a" if (1 - ej) == 0 else "b"))
                        p.dma("qs", we3[1 - ej][dc][:],
                              We3c[(e + 1) * 1024 + dc * 128:(e + 1) * 1024 + (dc + 1) * 128, :],
                              group="we13" + ("a" if (1 - ej) == 0 else "b"))
                for tt in range(NTT):
                    tj = tt % 2
                    k.mm(ps_c[tj][:], C.ones_row_b[:], crow[ej][0:1, tt * TT:(tt + 1) * TT])
                    k.cp(c_sb[tj][:], ps_c[tj][:])
                    for ic in range(IC):
                        ij = ic % 2
                        ps_1, ps_3, sg1, hh, hbt = ps_1r[ij], ps_3r[ij], sg1r[ij], hhr[ij], hbtr[ij]
                        for dc in range(DC):
                            k.mm(ps_1[:], we1[ej][dc][:, ic * 128:(ic + 1) * 128],
                                 xn_all[dc][:, tt * TT:(tt + 1) * TT],
                                 start=(dc == 0), stop=(dc == DC - 1))
                        k.act(sg1[:], ps_1[:], AF.Silu, bias=be1c[:, e * IC + ic:e * IC + ic + 1])
                        for dc in range(DC):
                            k.mm(ps_3[:], we3[ej][dc][:, ic * 128:(ic + 1) * 128],
                                 xn_all[dc][:, tt * TT:(tt + 1) * TT],
                                 start=(dc == 0), stop=(dc == DC - 1))
                        k.stt(hh[:], ps_3[:], be3c[:, e * IC + ic:e * IC + ic + 1], sg1[:],
                              op0=ALU.add, op1=ALU.mult)
                        k.tt(hbt[:], hh[:], c_sb[tj][:], ALU.mult, eng="gpsimd")
                        p.dma("qa", h_buf[e * INTER + ic * 128:e * INTER + (ic + 1) * 128,
                                          tt * TT:(tt + 1) * TT], hbt[:])
        p.barrier()

        with ExitStack() as stk:
            we2 = [stk.enter_context(nc.sbuf_tensor(f"{uid}_we2_{i}", [128, D], BF16))
                   for i in range(E * IC)]
            ws2 = [stk.enter_context(nc.sbuf_tensor(f"{uid}_ws2_{i}", [128, D], BF16))
                   for i in range(DC)]
            be2f = stk.enter_context(nc.sbuf_tensor(f"{uid}_be2f", [16, D], F32))
            be2 = stk.enter_context(nc.sbuf_tensor(f"{uid}_be2", [16, D], BF16))
            ps_y = [stk.enter_context(nc.psum_tensor(f"{uid}_psy{i}", [128, 2 * TT], F32))
                    for i in range(4)]
            bs2c = load_vec_cols(p, k, C, W["L0_ffn_bs2"], D, uid + "b2", stk, ps_y[0][:, 0:1])
            We2c = get_cast(p, k, C, W, "We2c")
            ws2c = get_cast(p, k, C, W, "ws2c")
            for e in range(E):
                for ic in range(IC):
                    p.dma("qs", we2[e * IC + ic][:],
                          We2c[e * 512 + ic * 128:e * 512 + (ic + 1) * 128, :], group="we2")
            for dc in range(DC):
                p.dma("qs", ws2[dc][:], ws2c[dc * 128:(dc + 1) * 128, :], group="ws2")
            p.dma("qs", be2f[:], W["L0_ffn_be2"])
            k.cp(be2[:], be2f[:])
            combT = stk.enter_context(nc.sbuf_tensor(f"{uid}_combT", [16, S], BF16))
            p.dma("qs", combT[:], comb_dram)
            hring = [stk.enter_context(nc.sbuf_tensor(f"{uid}_hr{i}", [128, TT], BF16))
                     for i in range(8)]
            zring = [stk.enter_context(nc.sbuf_tensor(f"{uid}_zr{i}", [128, TT], BF16))
                     for i in range(4)]
            xr = [stk.enter_context(nc.sbuf_tensor(f"{uid}_yxr{i}", [128, TT], F32)) for i in range(2)]
            ob = [stk.enter_context(nc.sbuf_tensor(f"{uid}_yob{i}", [128, TT], F32)) for i in range(2)]

            def py(oc):
                return ps_y[oc // 2][:, (oc % 2) * TT:(oc % 2) * TT + TT]
            for tt in range(NTT):
                first = True
                for e in range(E):
                    for ic in range(IC):
                        hb = hring[(e * IC + ic) % 8]
                        p.dma("qs", hb[:], h_buf[e * INTER + ic * 128:e * INTER + (ic + 1) * 128,
                                                 tt * TT:(tt + 1) * TT])
                        for oc in range(DC):
                            k.mm(py(oc), we2[e * IC + ic][:, oc * 128:(oc + 1) * 128], hb[:],
                                 start=first, stop=False)
                        first = False
                for ic2 in range(DC):
                    zb = zring[ic2 % 4]
                    p.dma("qs", zb[:], zh_buf[ic2 * 128:(ic2 + 1) * 128, tt * TT:(tt + 1) * TT])
                    for oc in range(DC):
                        k.mm(py(oc), ws2[ic2][:, oc * 128:(oc + 1) * 128], zb[:],
                             start=False, stop=False)
                for oc in range(DC):
                    k.mm(py(oc), be2[:, oc * 128:(oc + 1) * 128],
                         combT[:, tt * TT:(tt + 1) * TT], start=False, stop=True)
                for oc in range(DC):
                    xrb = xr[oc % 2]
                    p.dma("qs", xrb[:], xi[oc * 128:(oc + 1) * 128, tt * TT:(tt + 1) * TT])
                    obb = ob[oc % 2]
                    k.stt(obb[:], py(oc), bs2c[:, oc:oc + 1], xrb[:], op0=ALU.add, op1=ALU.add)
                    p.dma("qa", xo[oc * 128:(oc + 1) * 128, tt * TT:(tt + 1) * TT], obb[:])
    p.barrier()


PREP_SPECS = {
    "wgT": ("L1_ffn_w_gate", 3072, 1024, I32, None),
    "wuT": ("L1_ffn_w_up", 3072, 1024, I32, None),
    "wdT": ("L1_ffn_w_down", 1024, 3072, I32, None),
    "qkvT1b": ("L1_attn_b_W_qkv", 3072, 1024, F32, None),
    "projT1b": ("L1_attn_b_W_proj", 1024, 1024, F32, None),
    "WqT": ("L1_attn_a_Wq", 1024, 1024, F32, None),
    "WkT": ("L1_attn_a_Wk", 1024, 1024, F32, None),
    "WvT": ("L1_attn_a_Wv", 1024, 1024, F32, None),
    "wvT": ("L0_attn_a_w_qkv", 1024, 1024, F32, (2048, 3072)),
    "woT": ("L0_attn_a_w_o", 1024, 1024, F32, None),
}

def emit_prep_cast(p, k, C, src, dst, rows, cols, uid, B):
    n = 2048
    total = rows * cols
    assert total % (128 * n) == 0, (rows, cols)
    nb = total // (128 * n)
    sf = src.rearrange("r c -> (r c)").rearrange("(a p n) -> a p n", p=128, n=n)
    df = dst.rearrange("r c -> (r c)").rearrange("(a p n) -> a p n", p=128, n=n)
    for b in range(nb):
        sl, sb = B.slab_f[b % 3], B.slab_b[b % 3]
        p.dma("qs", sl[:, 0:n], sf[b])
        k.cp(sb[:, 0:n], sl[:, 0:n])
        p.dma("qa", df[b], sb[:, 0:n])


CAST_SPECS = {
    "We1c": ("L0_ffn_We1", 16 * 1024, 512),
    "We3c": ("L0_ffn_We3", 16 * 1024, 512),
    "We2c": ("L0_ffn_We2", 16 * 512, 1024),
    "ws1c": ("L0_ffn_ws1", 1024, 1024),
    "ws2c": ("L0_ffn_ws2", 1024, 1024),
    "ws3c": ("L0_ffn_ws3", 1024, 1024),
    "wqkv0c": ("L0_attn_b_W_qkv", 1024, 3072),
    "wproj0c": ("L0_attn_b_W_proj", 1024, 1024),
}

CASTS_FOR_BLOCK = {1: ["We1c", "We3c", "We2c", "ws1c", "ws2c", "ws3c"],
                   2: ["wqkv0c", "wproj0c"],
                   3: ["We1c", "We3c", "We2c", "ws1c", "ws2c", "ws3c"]}


def get_cast(p, k, C, W, key, B=None):
    if not hasattr(p, "cast_cache"):
        p.cast_cache = {}
    if key in p.cast_cache:
        return p.cast_cache[key]
    assert B is not None, f"cast {key} must be emitted in the prologue"
    name, rows, cols = CAST_SPECS[key]
    src = W[name]
    if len(INPUT_SPECS[name][0]) == 3:
        src = src.rearrange("h d e -> (h d) e")
    dst = p.nc.dram_tensor("cast_" + key, [rows, cols], BF16, kind="Internal").ap()
    emit_prep_cast(p, k, C, src, dst, rows, cols, "cc" + key, B)
    p.cast_cache[key] = dst
    return dst


PREPS_FOR_BLOCK = {0: ["wvT", "woT"], 4: ["WqT", "WkT", "WvT"],
                   5: ["wgT", "wuT", "wdT"], 6: ["qkvT1b", "projT1b"],
                   7: ["wgT", "wuT", "wdT"]}


def get_prep(p, k, C, W, key, B=None):
    if not hasattr(p, "prep_cache"):
        p.prep_cache = {}
    if key in p.prep_cache:
        return p.prep_cache[key]
    assert B is not None, f"prep {key} must be emitted in the prologue"
    name, rows, cols, dt, rslice = PREP_SPECS[key]
    src = W[name]
    if len(INPUT_SPECS[name][0]) == 3:
        src = src.rearrange("h d e -> (h d) e")
    if rslice is not None:
        src = src[rslice[0]:rslice[1], :]
    dst = p.nc.dram_tensor("prep_" + key, [cols, rows], BF16, kind="Internal").ap()
    emit_prep_transpose(p, k, C, src, dst, rows, cols, dt, "pp" + key, B)
    p.prep_cache[key] = dst
    return dst


def blk1(p, k, C, xi, xo, W, uid):
    emit_block1(p, k, C, xi, xo, W, uid)


def blk_l1ffn(p, k, C, xi, xo, W, uid, last=False):
    preps = {kk: get_prep(p, k, C, W, kk) for kk in ("wgT", "wuT", "wdT")}
    emit_l1ffn(p, k, C, xi, xo, W, uid, preps, last=last)


def _nyi(*a, **kw):
    raise NotImplementedError


L1FFN_INPUTS = ["L1_ffn_w_gate", "L1_ffn_w_up", "L1_ffn_w_down"]
L0FFN_INPUTS = ["L0_ffn_gate_w", "L0_ffn_gate_b", "L0_ffn_We1", "L0_ffn_be1",
                "L0_ffn_We2", "L0_ffn_be2", "L0_ffn_We3", "L0_ffn_be3",
                "L0_ffn_ws1", "L0_ffn_bs1", "L0_ffn_ws2", "L0_ffn_bs2",
                "L0_ffn_ws3", "L0_ffn_bs3"]

BLOCK_DEFS = [
    ("b1", blk1, ["L0_attn_a_w_qkv", "L0_attn_a_w_o"]),
    ("b2", lambda p, k, C, xi, xo, W, uid: emit_l0ffn(p, k, C, xi, xo, W, uid), L0FFN_INPUTS),
    ("b3", lambda p, k, C, xi, xo, W, uid: emit_attention(p, k, C, xi, xo, W, uid, "l0b"),
     ["L0_attn_b_W_qkv", "L0_attn_b_u_qkv", "L0_attn_b_sigma_qkv",
      "L0_attn_b_W_proj", "L0_attn_b_u_proj", "L0_attn_b_sigma_proj"]),
    ("b4", lambda p, k, C, xi, xo, W, uid: emit_l0ffn(p, k, C, xi, xo, W, uid), L0FFN_INPUTS),
    ("b5", lambda p, k, C, xi, xo, W, uid: emit_attention(p, k, C, xi, xo, W, uid, "l1a"),
     ["L1_attn_a_Wq", "L1_attn_a_bq", "L1_attn_a_Wk", "L1_attn_a_bk",
      "L1_attn_a_Wv", "L1_attn_a_bv"]),
    ("b6", lambda p, k, C, xi, xo, W, uid: blk_l1ffn(p, k, C, xi, xo, W, uid, last=False),
     L1FFN_INPUTS),
    ("b7", lambda p, k, C, xi, xo, W, uid: emit_attention(p, k, C, xi, xo, W, uid, "l1b"),
     ["L1_attn_b_W_qkv", "L1_attn_b_W_proj", "L1_attn_b_b_proj"]),
    ("b8", lambda p, k, C, xi, xo, W, uid: blk_l1ffn(p, k, C, xi, xo, W, uid, last=True),
     L1FFN_INPUTS),
]


def kernel(**inputs):
    n = 8
    x_full = np.ascontiguousarray(np.asarray(inputs["L0_attn_a_x"], dtype=np.float32))
    groups = [list(range(len(BLOCK_DEFS)))]
    cur = None
    out = None
    for g in groups:
        nc, needed = build_program(g)
        in_maps = []
        for c in range(n):
            m = {}
            for name in needed:
                if name == "L0_attn_a_x":
                    m[name] = x_full[c]
                else:
                    m[name] = np.asarray(inputs[name])
            if g[0] != 0:
                m["x_fm_in"] = cur[c]
            in_maps.append(m)
        res = run_bass_kernel_spmd(nc, in_maps, core_ids=list(range(n)))
        if g[-1] == 7:
            out = np.stack([res.results[c]["y"] for c in range(n)], axis=0)
        else:
            cur = [res.results[c]["x_fm_out"] for c in range(n)]
    return out
